# version 01
# baseline (speedup 1.0000x reference)
import numpy as np
import concourse.bass as bass
import concourse.mybir as mybir

F32 = mybir.dt.float32
BF16 = mybir.dt.bfloat16
U8 = mybir.dt.uint8
AF = mybir.ActivationFunctionType
OP = mybir.AluOpType
AX = mybir.AxisListType

D = 2048
I_FFN = 8192
T = 2048
NTILE = 16
PASS_T = 512
NPASS = T // PASS_T
DC = D // 128
IC = I_FFN // 128
H = 16
DH = 128
UNIT = 256
EPS = 1e-5


class Sem:
    def __init__(self, h):
        self.h = h
        self.n = 0

    def inc(self, k=1):
        self.n += k
        return self.n


class Env:
    def __init__(self, nc, block, ctx):
        self.nc = nc
        self.block = block
        self.ctx = ctx
        self.sems = {}
        self.sbufs = {}
        self.x_ready = None

    def sem(self, name):
        if name not in self.sems:
            self.sems[name] = Sem(self.ctx.enter_context(self.nc.semaphore(name)))
        return self.sems[name]

    def sbuf(self, name, shape, dtype):
        if name not in self.sbufs:
            self.sbufs[name] = self.ctx.enter_context(
                self.nc.sbuf_tensor(name, shape, dtype))
        return self.sbufs[name]

    def psum(self, name, shape, dtype):
        if name not in self.sbufs:
            self.sbufs[name] = self.ctx.enter_context(
                self.nc.psum_tensor(name, shape, dtype))
        return self.sbufs[name]


def wge(eng, sem, val):
    if val > 0:
        eng.wait_ge(sem.h, val)


def emit_const_prologue(E):
    idn = E.sbuf("idn", [128, 128], BF16)
    ones_col = E.sbuf("ones_col", [128, 1], BF16)
    ones_row = E.sbuf("ones_row", [1, 512], BF16)
    epsb = E.sbuf("epsb", [128, 1], F32)
    c = E.sem("c_const")

    @E.block.gpsimd
    def _(gp):
        gp.memset(idn[:], 1.0)
        gp.drain()
        gp.affine_select(idn[:], idn[:], pattern=[[-1, 128]],
                         compare_op=OP.is_equal, fill=0.0, base=0,
                         channel_multiplier=1)
        gp.memset(ones_col[:], 1.0)
        gp.memset(epsb[:], EPS)
        gp.memset(ones_row[:], 1.0).then_inc(c.h, 1)
        c.inc()


def ffn_buffers(E):
    return dict(
        xbuf=E.sbuf("c_x", [128, 4 * D], F32),
        xout=E.sbuf("c_xout", [128, 2 * D], F32),
        dn_tmp=E.sbuf("f_dntmp", [128, 1024], F32),
        xsq=E.sbuf("c_xsq", [128, D], F32),
        st8=E.sbuf("c_st8", [128, 32], F32),
        xnst=E.sbuf("c_xnst", [128, 2 * D], BF16),
        xn=E.sbuf("f_xn", [128, DC * PASS_T], BF16),
        hsl=E.sbuf("f_h", [128, IC * PASS_T], BF16),
        wgs=E.sbuf("f_wg", [128, 2 * DC * 128], BF16),
        wus=E.sbuf("f_wu", [128, 2 * DC * 128], BF16),
        wds=E.sbuf("f_wd", [128, 2 * 1024], BF16),
        sg=E.sbuf("f_sg", [128, 2 * PASS_T], F32),
        hq=E.sbuf("f_hq", [128, 4 * PASS_T], F32),
        qsq=E.sbuf("f_qsq", [128, 4 * PASS_T], F32),
        qt2=E.sbuf("f_qt2", [128, 4 * PASS_T], F32),
        shs=E.sbuf("f_shs", [1, PASS_T], BF16),
        pall=E.psum("pall", [128, 4096], F32),
    )


def emit_ffn(E, kind, x_src, x_dst, wg, wu, wd, gs=None, us=None, sbc=None,
             entry_guard=()):
    B = ffn_buffers(E)
    xbuf, xout, xsq, st8, xnst = B["xbuf"], B["xout"], B["xsq"], B["st8"], B["xnst"]
    xn, hsl, wgs, wus, wds = B["xn"], B["hsl"], B["wgs"], B["wus"], B["wds"]
    sg, hq, qsq, qt2, shs = B["sg"], B["hq"], B["qsq"], B["qt2"], B["shs"]
    dn_tmp, pall = B["dn_tmp"], B["pall"]
    ones_col = E.sbufs["ones_col"]
    ones_row = E.sbufs["ones_row"]
    idn = E.sbufs["idn"]
    epsb = E.sbufs["epsb"]

    pg = [pall[:, 0:512], pall[:, 1024:1536]]
    pu = [pall[:, 512:1024], pall[:, 1536:2048]]
    psh = pall[0:1, 2048:2560]
    ptr = pall[:, 3072:4096].bitcast(BF16)
    pdn = [pall[:, tt * 1024:(tt + 1) * 1024] for tt in range(4)]

    s_ldx = [E.sem(f"f_ldx{i}") for i in range(4)]
    s_lwg = [E.sem(f"f_lwg{i}") for i in range(2)]
    s_lwu = [E.sem(f"f_lwu{i}") for i in range(2)]
    s_ldd = [E.sem(f"f_ldd{i}") for i in range(2)]
    s_stx = {(sl, oh): E.sem(f"f_stx{sl}{oh}")
             for sl in range(2) for oh in range(2)}
    S = {n: E.sem(n) for n in [
        "f_acta", "f_actb", "f_actc", "f_actd",
        "f_dvea", "f_dveb", "f_dvec", "f_dvesh", "f_dved",
        "f_petr", "f_pegu", "f_pesh", "f_pednc", "f_pedno", "f_gpa",
        "c_const"]}
    l0 = kind == "l0"
    ic_order = (list(range(IC)) if l0 else
                [16 * m + qd for qd in range(16) for m in range(4)])

    tok = {}

    x_ready_out = {}
    if not hasattr(E, "ln_guard"):
        E.ln_guard = {}
    if not hasattr(E, "xslot_guard"):
        E.xslot_guard = {}

    def _entry(eng):
        for sem, v in entry_guard:
            wge(eng, sem, v)

    for p in range(NPASS):
        base_petr = S["f_petr"].n
        base_pegu = S["f_pegu"].n
        base_pednc = S["f_pednc"].n
        base_actc = S["f_actc"].n
        base_actd = S["f_actd"].n
        base_dvec = S["f_dvec"].n
        base_dved = S["f_dved"].n
        base_stx = {k: s.n for k, s in s_stx.items()}

        @E.block.gpsimd
        def _(gq, p=p):
            if p == 0:
                _entry(gq)
            for tt in range(4):
                ti = p * 4 + tt
                for sem, v in E.xslot_guard.get(tt, []):
                    wge(gq, sem, v)
                if E.x_ready is not None:
                    for sem, v in E.x_ready[ti]:
                        gq.wait_ge(sem.h, v)
                gq.dma_start(out=xbuf[:, tt * D:(tt + 1) * D],
                             in_=x_src[ti * 128:(ti + 1) * 128, :])\
                    .then_inc(s_ldx[tt].h, 16)
                tok[("ldx", p, tt)] = s_ldx[tt].inc(16)

        @E.block.sync
        def _(sy, p=p, base_pegu=base_pegu, base_pednc=base_pednc):
            if p == 0:
                _entry(sy)
            for j, ic in enumerate(ic_order):
                slot = j % 2
                wge(sy, S["f_pegu"], base_pegu + j - 1)
                for (w, dst, sem) in ((wg, wgs, s_lwg[slot]),
                                      (wu, wus, s_lwu[slot])):
                    ap = w[:, ic * 128:(ic + 1) * 128].rearrange(
                        "(c p) i -> p c i", p=128)
                    sy.dma_start(out=dst[:, slot * 2048:(slot + 1) * 2048]
                                 .rearrange("p (c i) -> p c i", c=DC),
                                 in_=ap).then_inc(sem.h, 16)
                    sem.inc(16)
                tok[("ldw", p, j)] = (s_lwg[slot].n, s_lwu[slot].n)
            for oh in range(2):
                for ic in range(IC):
                    k = oh * IC + ic
                    slot = k % 2
                    wge(sy, S["f_pednc"], base_pednc + k - 1)
                    sy.dma_start(out=wds[:, slot * 1024:(slot + 1) * 1024],
                                 in_=wd[ic * 128:(ic + 1) * 128,
                                        oh * 1024:(oh + 1) * 1024])\
                        .then_inc(s_ldd[slot].h, 16)
                    tok[("ldd", p, oh, ic)] = s_ldd[slot].inc(16)

        @E.block.scalar
        def _(sc, p=p):
            if p == 0:
                _entry(sc)
            for tt in range(4):
                sc.wait_ge(s_ldx[tt].h, tok[("ldx", p, tt)])
                prev = E.ln_guard.get(("dvea", tt))
                if prev:
                    sc.wait_ge(prev[0].h, prev[1])
                sc.activation(xsq[:], xbuf[:, tt * D:(tt + 1) * D], AF.Square,
                              accum_out=st8[:, tt:tt + 1]).then_inc(S["f_acta"].h, 1)
                tok[("acta", p, tt)] = S["f_acta"].inc()
                sc.drain()

        @E.block.vector
        def _(ve, p=p):
            if p == 0:
                _entry(ve)
            for tt in range(4):
                ve.wait_ge(s_ldx[tt].h, tok[("ldx", p, tt)])
                prev = E.ln_guard.get(("actb", tt))
                if prev:
                    ve.wait_ge(prev[0].h, prev[1])
                ve.reduce_sum(st8[:, 4 + tt:5 + tt], xbuf[:, tt * D:(tt + 1) * D],
                              axis=AX.X)
            ve.drain()
            for tt in range(4):
                ve.tensor_scalar(st8[:, 8 + tt:9 + tt], st8[:, 4 + tt:5 + tt],
                                 1.0 / D, None, OP.mult)
            for tt in range(4):
                ve.wait_ge(S["f_acta"].h, tok[("acta", p, tt)])
                ve.tensor_scalar(st8[:, 20 + tt:21 + tt], st8[:, tt:tt + 1],
                                 1.0 / D, None, OP.mult)
            ve.drain()
            for tt in range(4):
                ve.tensor_tensor(st8[:, 24 + tt:25 + tt], st8[:, 8 + tt:9 + tt],
                                 st8[:, 8 + tt:9 + tt], OP.mult)
            ve.drain()
            for tt in range(4):
                ve.scalar_tensor_tensor(st8[:, 12 + tt:13 + tt],
                                        st8[:, 24 + tt:25 + tt], -1.0,
                                        st8[:, 20 + tt:21 + tt],
                                        OP.mult, OP.add).then_inc(S["f_dvea"].h, 1)
                tok[("dvea", p, tt)] = S["f_dvea"].inc()
                E.ln_guard[("dvea", tt)] = (S["f_dvea"], tok[("dvea", p, tt)])

        @E.block.scalar
        def _(sc, p=p):
            if p == 0:
                sc.wait_ge(S["c_const"].h, 1)
            for tt in range(4):
                sc.wait_ge(S["f_dvea"].h, tok[("dvea", p, tt)])
                prev = E.ln_guard.get(("dveb", tt))
                if prev:
                    sc.wait_ge(prev[0].h, prev[1])
                sc.activation(st8[:, 28 + tt:29 + tt], st8[:, 12 + tt:13 + tt],
                              AF.Ln, bias=epsb[:, 0:1])
            sc.drain()
            for tt in range(4):
                sc.activation(st8[:, 16 + tt:17 + tt], st8[:, 28 + tt:29 + tt],
                              AF.Exp, scale=-0.5).then_inc(S["f_actb"].h, 1)
                tok[("actb", p, tt)] = S["f_actb"].inc()
                E.ln_guard[("actb", tt)] = (S["f_actb"], tok[("actb", p, tt)])

        @E.block.vector
        def _(ve, p=p, base_petr=base_petr):
            for tt in range(4):
                ve.wait_ge(S["f_actb"].h, tok[("actb", p, tt)])
                wge(ve, S["f_petr"], base_petr + tt - 1)
                ve.tensor_scalar(xnst[:, (tt % 2) * D:(tt % 2 + 1) * D],
                                 xbuf[:, tt * D:(tt + 1) * D],
                                 st8[:, 8 + tt:9 + tt], st8[:, 16 + tt:17 + tt],
                                 OP.subtract, OP.mult).then_inc(S["f_dveb"].h, 1)
                tok[("dveb", p, tt)] = S["f_dveb"].inc()
                E.ln_guard[("dveb", tt)] = (S["f_dveb"], tok[("dveb", p, tt)])

        @E.block.tensor
        def _(te, p=p, base_actc=base_actc):
            if p == 0:
                _entry(te)
            te.wait_ge(S["c_const"].h, 1)
            for tt in range(4):
                te.wait_ge(S["f_dveb"].h, tok[("dveb", p, tt)])
                wge(te, S["f_actc"], base_actc + tt)
                d31 = tok.get(("dved", p - 1, 3, 1))
                if tt == 0 and d31:
                    te.wait_ge(S["f_dved"].h, d31)
                for dc in range(DC):
                    inst = te.transpose(
                        ptr[:, dc * 128:(dc + 1) * 128],
                        xnst[:, (tt % 2) * D + dc * 128:(tt % 2) * D + (dc + 1) * 128],
                        idn[:])
                    if dc == DC - 1:
                        inst.then_inc(S["f_petr"].h, 1)
                tok[("petr", p, tt)] = S["f_petr"].inc()

        @E.block.scalar
        def _(sc, p=p, base_pegu=base_pegu):
            for tt in range(4):
                sc.wait_ge(S["f_petr"].h, tok[("petr", p, tt)])
                if p > 0 and tt == 0:
                    sc.wait_ge(S["f_pegu"].h, base_pegu)
                sc.copy(
                    out=xn[:].rearrange("p (c t) -> p c t", c=DC)
                    [:, :, tt * 128:(tt + 1) * 128],
                    in_=ptr[:].rearrange("p (c t) -> p c t", c=DC))\
                    .then_inc(S["f_actc"].h, 1)
                tok[("actc", p, tt)] = S["f_actc"].inc()

        @E.block.tensor
        def _(te, p=p, base_actd=base_actd, base_dvec=base_dvec):
            te.wait_ge(S["f_actc"].h, tok[("actc", p, 3)])
            d = tok.get(("dved", p - 1, 3, 1))
            if d:
                te.wait_ge(S["f_dved"].h, d)
            for j, ic in enumerate(ic_order):
                slot = j % 2
                vg, vu = tok[("ldw", p, j)]
                te.wait_ge(s_lwg[slot].h, vg)
                te.wait_ge(s_lwu[slot].h, vu)
                wge(te, S["f_actd"], base_actd + j - 1)
                wge(te, S["f_dvec"], base_dvec + j - 1)
                for dc in range(DC):
                    te.matmul(pg[slot][:, :],
                              wgs[:, slot * 2048 + dc * 128:slot * 2048 + (dc + 1) * 128],
                              xn[:, dc * PASS_T:(dc + 1) * PASS_T],
                              start=(dc == 0), stop=(dc == DC - 1))
                for dc in range(DC):
                    inst = te.matmul(
                        pu[slot][:, :],
                        wus[:, slot * 2048 + dc * 128:slot * 2048 + (dc + 1) * 128],
                        xn[:, dc * PASS_T:(dc + 1) * PASS_T],
                        start=(dc == 0), stop=(dc == DC - 1))
                    if dc == DC - 1:
                        inst.then_inc(S["f_pegu"].h, 1)
                tok[("pegu", p, j)] = S["f_pegu"].inc()

        @E.block.scalar
        def _(sc, p=p, base_dvec=base_dvec):
            for j, ic in enumerate(ic_order):
                slot = j % 2
                sgslot = sg[:, slot * PASS_T:(slot + 1) * PASS_T]
                sc.wait_ge(S["f_pegu"].h, tok[("pegu", p, j)])
                wge(sc, S["f_dvec"], base_dvec + j - 1)
                sc.activation(sgslot, pg[slot][:, :], AF.Silu,
                              scale=(gs[:, ic:ic + 1] if l0 else 1.0))\
                    .then_inc(S["f_actd"].h, 1)
                tok[("actd", p, j)] = S["f_actd"].inc()

        @E.block.vector
        def _(ve, p=p):
            d = tok.get(("pednc", p - 1))
            if d:
                ve.wait_ge(S["f_pednc"].h, d)
            for j, ic in enumerate(ic_order):
                slot = j % 2
                sgslot = sg[:, slot * PASS_T:(slot + 1) * PASS_T]
                ve.wait_ge(S["f_actd"].h, tok[("actd", p, j)])
                ve.wait_ge(S["f_pegu"].h, tok[("pegu", p, j)])
                if l0:
                    ve.scalar_tensor_tensor(
                        hsl[:, ic * PASS_T:(ic + 1) * PASS_T], pu[slot][:, :],
                        us[:, ic:ic + 1], sgslot, OP.mult, OP.mult)\
                        .then_inc(S["f_dvec"].h, 1)
                else:
                    m = ic // 16
                    ve.tensor_tensor(hq[:, m * PASS_T:(m + 1) * PASS_T],
                                     sgslot, pu[slot][:, :], OP.mult)\
                        .then_inc(S["f_dvec"].h, 1)
                tok[("dvec", p, j)] = S["f_dvec"].inc()
                if not l0 and j % 4 == 3:
                    qd = ic_order[j] % 16
                    sq = [qsq[:, m * PASS_T:(m + 1) * PASS_T] for m in range(4)]
                    hm = [hq[:, m * PASS_T:(m + 1) * PASS_T] for m in range(4)]
                    A = qt2[:, 0:PASS_T]
                    Bb = qt2[:, PASS_T:2 * PASS_T]
                    C = qt2[:, 2 * PASS_T:3 * PASS_T]
                    Dd = qt2[:, 3 * PASS_T:4 * PASS_T]
                    ve.drain()
                    for m in range(4):
                        ve.tensor_tensor(sq[m], hm[m], hm[m], OP.mult)
                    ve.drain()
                    ve.tensor_tensor(A, sq[0], sq[1], OP.max)
                    ve.tensor_tensor(Bb, sq[2], sq[3], OP.max)
                    ve.tensor_tensor(C, sq[0], sq[1], OP.min)
                    ve.tensor_tensor(Dd, sq[2], sq[3], OP.min)
                    ve.drain()
                    ve.tensor_tensor(A, A, Bb, OP.min)
                    ve.tensor_tensor(C, C, Dd, OP.max)
                    ve.drain()
                    ve.tensor_tensor(A, A, C, OP.max)
                    ve.drain()
                    for m in range(4):
                        ve.tensor_tensor(sq[m], sq[m], A, OP.is_ge)
                    ve.drain()
                    for m in range(4):
                        icm = 16 * m + qd
                        inst = ve.tensor_tensor(
                            hsl[:, icm * PASS_T:(icm + 1) * PASS_T],
                            hm[m], sq[m], OP.mult)
                        if m == 3:
                            inst.then_inc(S["f_gpa"].h, 1)
                    tok[("gpa", p, qd)] = S["f_gpa"].inc()
                    ve.drain()

        if l0:
            @E.block.tensor
            def _(te, p=p):
                for j in range(IC):
                    ic = ic_order[j]
                    te.wait_ge(S["f_dvec"].h, tok[("dvec", p, j)])
                    inst = te.matmul(psh, ones_col[:],
                                     hsl[:, ic * PASS_T:(ic + 1) * PASS_T],
                                     start=(j == 0), stop=(j == IC - 1))
                    if j == IC - 1:
                        inst.then_inc(S["f_pesh"].h, 1)
                tok[("pesh", p)] = S["f_pesh"].inc()

            @E.block.vector
            def _(ve, p=p):
                ve.wait_ge(S["f_pesh"].h, tok[("pesh", p)])
                ve.tensor_scalar(shs[:, :], psh, -128.0, None, OP.mult)\
                    .then_inc(S["f_dvesh"].h, 1)
                tok[("dvesh", p)] = S["f_dvesh"].inc()

        @E.block.tensor
        def _(te, p=p, base_dved=base_dved):
            te.wait_ge(S["f_dvec"].h, tok[("dvec", p, IC - 1)])
            te.wait_ge(S["f_actd"].h, tok[("actd", p, IC - 1)])
            if not l0:
                te.wait_ge(S["f_gpa"].h, tok[("gpa", p, 15)])
            for oh in range(2):
                if oh == 1:
                    te.wait_ge(S["f_dved"].h, base_dved + 4)
                for ici in range(IC):
                    v = tok[("ldd", p, oh, ici)]
                    te.wait_ge(s_ldd[(oh * IC + ici) % 2].h, v)
                    slot = (oh * IC + ici) % 2
                    for tt in range(4):
                        if l0 and oh == 0 and ici == 0 and tt == 2:
                            te.wait_ge(S["f_dvesh"].h, tok[("dvesh", p)])
                        lt = hsl[:, ici * PASS_T + tt * 128:ici * PASS_T + (tt + 1) * 128]
                        for hf in range(2):
                            inst = te.matmul(
                                pdn[tt][:, hf * 512:(hf + 1) * 512], lt,
                                wds[:, slot * 1024 + hf * 512:slot * 1024 + (hf + 1) * 512],
                                start=(ici == 0),
                                stop=(ici == IC - 1 and not l0))
                            if tt == 3 and hf == 1:
                                inst.then_inc(S["f_pednc"].h, 1)
                    S["f_pednc"].inc()
                if l0:
                    te.wait_ge(S["f_dvesh"].h, tok[("dvesh", p)])
                    for tt in range(4):
                        for hf in range(2):
                            inst = te.matmul(
                                pdn[tt][:, hf * 512:(hf + 1) * 512],
                                shs[0:1, tt * 128:(tt + 1) * 128],
                                ones_row[0:1, 0:512],
                                start=False, stop=True)
                            if tt == 3 and hf == 1:
                                inst.then_inc(S["f_pedno"].h, 1)
                    tok[("pedno", p, oh)] = S["f_pedno"].inc()
                else:
                    tok[("pedno", p, oh)] = None
            tok[("pednc", p)] = S["f_pednc"].n

        @E.block.vector
        def _(ve, p=p, base_pednc=base_pednc):
            for oh in range(2):
                if l0:
                    ve.wait_ge(S["f_pedno"].h, tok[("pedno", p, oh)])
                else:
                    ve.wait_ge(S["f_pednc"].h, base_pednc + IC * (oh + 1))
                for tt in range(4):
                    sl = tt % 2
                    bs = base_stx[(sl, oh)]
                    if tt >= 2:
                        wge(ve, s_stx[(sl, oh)], bs + 16)
                    else:
                        wge(ve, s_stx[(sl, oh)], bs)
                    xa = xbuf[:, tt * D + oh * 1024:tt * D + (oh + 1) * 1024]
                    oa = xout[:, sl * D + oh * 1024:sl * D + (oh + 1) * 1024]
                    pa = pdn[tt][:, 0:1024]
                    if l0:
                        ve.tensor_tensor(dn_tmp[:], pa,
                                         sbc[:, oh * 1024:(oh + 1) * 1024], OP.mult)
                        ve.drain()
                        ve.tensor_tensor(oa, dn_tmp[:], xa, OP.add)\
                            .then_inc(S["f_dved"].h, 1)
                        ve.drain()
                    else:
                        ve.scalar_tensor_tensor(oa, pa, 0.5, xa, OP.mult, OP.add)\
                            .then_inc(S["f_dved"].h, 1)
                    tok[("dved", p, tt, oh)] = S["f_dved"].inc()
                    if oh == 1:
                        E.xslot_guard[tt] = [(S["f_dved"], tok[("dved", p, tt, 1)])]

        @E.block.gpsimd
        def _(sy, p=p):
            for oh in range(2):
                for tt in range(4):
                    ti = p * 4 + tt
                    sl = tt % 2
                    sy.wait_ge(S["f_dved"].h, tok[("dved", p, tt, oh)])
                    sy.dma_start(
                        out=x_dst[ti * 128:(ti + 1) * 128,
                                  oh * 1024:(oh + 1) * 1024],
                        in_=xout[:, sl * D + oh * 1024:sl * D + (oh + 1) * 1024])\
                        .then_inc(s_stx[(sl, oh)].h, 16)
                    tok[("stx", p, tt, oh)] = s_stx[(sl, oh)].inc(16)
                    if oh == 1:
                        x_ready_out[ti] = [
                            (s_stx[(sl, 0)], tok[("stx", p, tt, 0)]),
                            (s_stx[(sl, 1)], tok[("stx", p, tt, 1)])]

    E.x_ready = x_ready_out


def emit_final_wait(E):
    @E.block.gpsimd
    def _(gp):
        for name, s in E.sems.items():
            if name.startswith("f_stx"):
                gp.wait_ge(s.h, s.n)




"""Attention sublayer emitter for mk_macaron_d2: x_dst = x_src + Attn(LN(x)).

Working buffers alias the FFN buffers (bitcast views); entry_guard serializes
against the preceding sublayer, and E.attn_exit_guard (set here) must be
waited by the next sublayer's first sections.

Arena map (per-partition):
  phase A:  aw=hsl(64K) | qtm = qsq+hq (2x8K f32) | kvst = wds+sg (2x4K bf16)
            kfst+xnf = xout (4x4K bf16) | rope scratch = qt2 (8K f32)
            trig = xn (16K f32) | LN: xbuf/xsq/st8/xnst as usual
  masks  :  xsq(8 tiles) + qt2(8) + dn_tmp(4+dil128)  (computed after A)
  phase C:  S=hsl[0:16K]f32 | P=hsl[16K:24K]bf16 | pts=hsl[24K:32K]bf16
            K_h=hsl[32K:48K]x2 | ctxh=hsl[48K:56K]x2 | ctx_sc=hsl[56K:56.25K]
            V_h = xn x2 (bf16 [128,4096]) | Q_h = wgs x2 | mx/stats = st8
  phase Dp: Wo=hsl | ctx tiles = wgs x2 | xbuf/xout/stores as FFN

PSUM:  A: proj b0-3 | ptrx b4-5 | ptrk b6-7
       C: S slots b0-2 | ptrp b4-5 | pctx b6 | pctxT b7
       Dp: b0-3 / b4-7 alternating tiles
"""

SC128 = float(1.0 / np.sqrt(128.0))
TWO_PI = float(2 * np.pi)
NF = {0: 32, 1: 64}


def attn_chunks(layer, jt):
    j, r = jt // 2, jt % 2
    chunks = []
    if layer == 0:
        for c in range(2):
            base = c * 2048
            full = j * 256
            off = 0
            while off < full:
                w = min(512, full - off)
                chunks.append((base + off, w, None,
                               [c * 16 + off // 128 + k for k in range(w // 128)]))
                off += w
            chunks.append((base + j * 256, 256, ("l0", c, r),
                           [c * 16 + 2 * j, c * 16 + 2 * j + 1]))
    else:
        for pos in range(0, max(0, j - 2)):
            for c in range(2):
                mask = None if (pos == 0 and c == 0) else "dil128"
                chunks.append((c * 2048 + pos * 256, 128, mask,
                               [c * 16 + 2 * pos]))
        for e in range(min(2, j), -1, -1):
            pos = j - e
            for c in range(2):
                if pos == 0 and c == 0:
                    mask = ("l1g", e, r)
                else:
                    mask = ("l1", e, c, r)
                chunks.append((c * 2048 + pos * 256, 256, mask,
                               [c * 16 + 2 * pos, c * 16 + 2 * pos + 1]))
    ext = sum(w for (_, w, _, _) in chunks)
    assert ext % 128 == 0 and ext <= 4096
    return ext, chunks


def _entry(eng, guard):
    for sem, v in guard:
        wge(eng, sem, v)


def emit_attn(E, layer, x_src, x_dst, w_all, q_fm, kv_in, kv_out, ctx_buf,
              pofs, qpos_c, invf_b, entry_guard, replica_groups, upto="full"):
    nc = E.nc
    B = ffn_buffers(E)
    xbuf, xout, xsq, st8, xnst = B["xbuf"], B["xout"], B["xsq"], B["st8"], B["xnst"]
    pall = B["pall"]
    idn = E.sbufs["idn"]
    epsb = E.sbufs["epsb"]
    nf = NF[layer]

    hslb = B["hsl"][:]
    aw = hslb[:, 0:32768]
    S_slab = hslb.bitcast(F32)[:, 0:4096]
    P_slab = hslb[:, 8192:12288]
    pts = hslb[:, 12288:16384]
    K_h = [hslb[:, 16384 + s * 4096:16384 + (s + 1) * 4096] for s in range(2)]
    ctxh = [hslb[:, 24576 + s * 2048:24576 + (s + 1) * 2048] for s in range(2)]
    ctx_sc = hslb[:, 28672:28800]
    xnb = B["xn"][:]
    trig = xnb.bitcast(F32)
    cosk, sink = trig[:, 0:1024], trig[:, 1024:2048]
    cosq, sinq = trig[:, 2048:3072], trig[:, 3072:4096]
    V_h = [xnb[:, s * 4096:(s + 1) * 4096] for s in range(2)]
    wgsb = B["wgs"][:]
    Q_h = [wgsb[:, s * 2048:(s + 1) * 2048] for s in range(2)]
    ctxt = [wgsb[:, s * 2048:(s + 1) * 2048] for s in range(2)]
    qtm = [B["qsq"][:], B["hq"][:]]
    kvst = [B["wds"][:], B["sg"][:].bitcast(BF16)]
    xob = xout[:].bitcast(BF16)
    kfst = [xob[:, 0:2048], xob[:, 2048:4096]]
    xnf = [xob[:, 4096:6144], xob[:, 6144:8192]]
    ropet = B["qt2"][:]
    dn_tmpf = B["dn_tmp"][:]
    qt2f = B["qt2"][:]
    mask_store = ([xsq[:, i * 256:(i + 1) * 256] for i in range(8)]
                  + [qt2f[:, i * 256:(i + 1) * 256] for i in range(8)]
                  + [dn_tmpf[:, i * 256:(i + 1) * 256] for i in range(3)])
    dil128 = dn_tmpf[:, 768:896]

    pproj = pall[:, 0:2048]
    ptrx = pall[:, 2048:3072].bitcast(BF16)
    ptrk = pall[:, 3072:4096].bitcast(BF16)
    pS = [pall[:, i * 512:(i + 1) * 512] for i in range(3)]
    ptrp = pall[:, 2048:3072].bitcast(BF16)
    pctx = pall[:, 3072:3200]
    pctxT = pall[:, 3584:4096].bitcast(BF16)[:, 0:128]
    pD = [pall[:, 0:2048], pall[:, 2048:4096]]

    A = {n: E.sem(n) for n in [
        "a_lw", "a_petr", "a_actc", "a_pep", "a_actp", "a_dver", "a_petk", "a_sin",
        "a_dvek", "a_cc", "a_prep", "a_msk",
        "a_pes", "a_acts", "a_gpm", "a_dvemx", "a_acte", "a_pept", "a_dvept",
        "a_peav", "a_dvecx", "a_pecx", "a_dvecs", "a_peo", "a_dveo",
        "c_ldp", "c_const"]}
    LN = {n: E.sem(n) for n in ["f_acta", "f_actb", "f_dvea", "f_dveb"]}
    s_ldx = [E.sem(f"f_ldx{i}") for i in range(4)]
    s_stx = {(sl, oh): E.sem(f"f_stx{sl}{oh}")
             for sl in range(2) for oh in range(2)}
    s_stk = [E.sem(f"a_stk{i}") for i in range(2)]
    s_stv = [E.sem(f"a_stv{i}") for i in range(2)]
    s_lkh = [E.sem(f"a_lkh{i}") for i in range(2)]
    s_lvh = {(i, r): E.sem(f"a_lvh{i}{r}") for i in range(2) for r in range(2)}
    s_lqh = [E.sem(f"a_lqh{i}") for i in range(2)]
    s_sct = [E.sem(f"a_sct{i}") for i in range(2)]
    s_lct = [E.sem(f"a_lct{i}") for i in range(2)]

    tok = {}
    lntok = {}
    if not hasattr(E, "ln_guard"):
        E.ln_guard = {}
    if not hasattr(E, "xslot_guard"):
        E.xslot_guard = {}

    @E.block.vector
    def _(ve):
        _entry(ve, entry_guard)
        ve.wait_ge(A["c_ldp"].h, A["c_ldp"].n)
        ang = ropet[:, 0:1024]
        t1 = ropet[:, 1024:2048]
        if nf < 64:
            ve.memset(ang, 0.0)
            ve.drain()
        for tt in range(16):
            ve.tensor_scalar(ang[:, tt * 64:tt * 64 + nf], invf_b[:, 0:nf],
                             qpos_c[:, tt:tt + 1], None, OP.mult)
        ve.drain()
        for is_cos in (0, 1):
            if is_cos:
                ve.tensor_scalar(ang[:], ang[:], float(np.pi / 2), None, OP.add)
                ve.drain()
            ve.tensor_scalar(t1[:], ang[:], float(1.0 / TWO_PI), None, OP.mult)
            ve.drain()
            it32 = xsq[:].bitcast(mybir.dt.int32)[:, 0:1024]
            ve.tensor_copy(it32, t1[:])
            ve.drain()
            ve.tensor_copy(t1[:], it32)
            ve.drain()
            dst = (cosk if is_cos else sink)
            ve.scalar_tensor_tensor(dst, t1[:], -TWO_PI, ang[:], OP.mult, OP.add)
            ve.drain()
            ve.tensor_scalar(t1[:], dst, float(np.pi), None, OP.is_gt)
            ve.drain()
            ve.scalar_tensor_tensor(dst, t1[:], -TWO_PI, dst, OP.mult, OP.add)\
                .then_inc(A["a_prep"].h, 1)
            tok[("red", is_cos)] = A["a_prep"].inc()
            ve.drain()

    @E.block.scalar
    def _(sc):
        _entry(sc, entry_guard)
        for is_cos in (0, 1):
            sc.wait_ge(A["a_prep"].h, tok[("red", is_cos)])
            dst = (cosk if is_cos else sink)
            sc.activation(dst, dst, AF.Sin).then_inc(A["a_sin"].h, 1)
            A["a_sin"].inc()
        sc.drain()

    @E.block.vector
    def _(ve):
        ve.wait_ge(A["a_sin"].h, A["a_sin"].n)
        ve.tensor_scalar(cosq, cosk, SC128, None, OP.mult)
        ve.tensor_scalar(sinq, sink, SC128, None, OP.mult)\
            .then_inc(A["a_prep"].h, 1)
        A["a_prep"].inc()
    prep_tok = A["a_prep"].n

    def a_phase(mi, wrow, needs_rope, qscale):
        key = ("A", layer, mi)
        b_pep = A["a_pep"].n
        b_actp = A["a_actp"].n
        b_petr = A["a_petr"].n
        b_actc = A["a_actc"].n
        b_dver = A["a_dver"].n
        b_petk = A["a_petk"].n
        b_dvek = A["a_dvek"].n
        b_stk = [s.n for s in s_stk]
        b_stv = [s.n for s in s_stv]

        @E.block.gpsimd
        def _(gp):
            if mi == 0:
                _entry(gp, entry_guard)
            if b_pep > 0:
                gp.wait_ge(A["a_pep"].h, b_pep)
            gp.dma_start(out=aw.rearrange("p (c o) -> p c o", c=DC),
                         in_=w_all[wrow * 2048:(wrow + 1) * 2048, :]
                         .rearrange("(c p) o -> p c o", p=128))\
                .then_inc(A["a_lw"].h, 16)
            tok[(key, "lw")] = A["a_lw"].inc(16)

        for g in range(4):
            @E.block.gpsimd
            def _(gq, g=g):
                for tt in range(4):
                    ti = g * 4 + tt
                    for sem, v in E.xslot_guard.get(tt, []):
                        wge(gq, sem, v)
                    if E.x_ready is not None:
                        for sem, v in E.x_ready[ti]:
                            gq.wait_ge(sem.h, v)
                    gq.dma_start(out=xbuf[:, tt * D:(tt + 1) * D],
                                 in_=x_src[ti * 128:(ti + 1) * 128, :])\
                        .then_inc(s_ldx[tt].h, 16)
                    lntok[("ldx", key, g, tt)] = s_ldx[tt].inc(16)

            @E.block.scalar
            def _(sc, g=g):
                if mi == 0 and g == 0:
                    sc.wait_ge(A["a_prep"].h, tok[("red", 1)])
                for tt in range(4):
                    sc.wait_ge(s_ldx[tt].h, lntok[("ldx", key, g, tt)])
                    prev = E.ln_guard.get(("dvea", tt))
                    if prev:
                        sc.wait_ge(prev[0].h, prev[1])
                    sc.activation(xsq[:], xbuf[:, tt * D:(tt + 1) * D],
                                  AF.Square, accum_out=st8[:, tt:tt + 1])\
                        .then_inc(LN["f_acta"].h, 1)
                    lntok[("acta", key, g, tt)] = LN["f_acta"].inc()
                    sc.drain()

            @E.block.vector
            def _(ve, g=g):
                for tt in range(4):
                    ve.wait_ge(s_ldx[tt].h, lntok[("ldx", key, g, tt)])
                    prev = E.ln_guard.get(("actb", tt))
                    if prev:
                        ve.wait_ge(prev[0].h, prev[1])
                    ve.reduce_sum(st8[:, 4 + tt:5 + tt],
                                  xbuf[:, tt * D:(tt + 1) * D], axis=AX.X)
                ve.drain()
                for tt in range(4):
                    ve.tensor_scalar(st8[:, 8 + tt:9 + tt], st8[:, 4 + tt:5 + tt],
                                     1.0 / D, None, OP.mult)
                for tt in range(4):
                    ve.wait_ge(LN["f_acta"].h, lntok[("acta", key, g, tt)])
                    ve.tensor_scalar(st8[:, 20 + tt:21 + tt], st8[:, tt:tt + 1],
                                     1.0 / D, None, OP.mult)
                ve.drain()
                for tt in range(4):
                    ve.tensor_tensor(st8[:, 24 + tt:25 + tt],
                                     st8[:, 8 + tt:9 + tt],
                                     st8[:, 8 + tt:9 + tt], OP.mult)
                ve.drain()
                for tt in range(4):
                    ve.scalar_tensor_tensor(st8[:, 12 + tt:13 + tt],
                                            st8[:, 24 + tt:25 + tt], -1.0,
                                            st8[:, 20 + tt:21 + tt],
                                            OP.mult, OP.add)\
                        .then_inc(LN["f_dvea"].h, 1)
                    lntok[("dvea", key, g, tt)] = LN["f_dvea"].inc()
                    E.ln_guard[("dvea", tt)] = (LN["f_dvea"],
                                                lntok[("dvea", key, g, tt)])

            @E.block.scalar
            def _(sc, g=g):
                if mi == 0 and g == 0:
                    sc.wait_ge(A["c_const"].h, 1)
                for tt in range(4):
                    sc.wait_ge(LN["f_dvea"].h, lntok[("dvea", key, g, tt)])
                    prev = E.ln_guard.get(("dveb", tt))
                    if prev:
                        sc.wait_ge(prev[0].h, prev[1])
                    sc.activation(st8[:, 28 + tt:29 + tt],
                                  st8[:, 12 + tt:13 + tt], AF.Ln,
                                  bias=epsb[:, 0:1])
                sc.drain()
                for tt in range(4):
                    sc.activation(st8[:, 16 + tt:17 + tt],
                                  st8[:, 28 + tt:29 + tt], AF.Exp, scale=-0.5)\
                        .then_inc(LN["f_actb"].h, 1)
                    lntok[("actb", key, g, tt)] = LN["f_actb"].inc()
                    E.ln_guard[("actb", tt)] = (LN["f_actb"],
                                                lntok[("actb", key, g, tt)])

            @E.block.vector
            def _(ve, g=g, b_petr=b_petr):
                for tt in range(4):
                    i = g * 4 + tt
                    ve.wait_ge(LN["f_actb"].h, lntok[("actb", key, g, tt)])
                    wge(ve, A["a_petr"], b_petr + i - 1)
                    ve.tensor_scalar(xnst[:, (i % 2) * D:(i % 2 + 1) * D],
                                     xbuf[:, tt * D:(tt + 1) * D],
                                     st8[:, 8 + tt:9 + tt],
                                     st8[:, 16 + tt:17 + tt],
                                     OP.subtract, OP.mult)\
                        .then_inc(LN["f_dveb"].h, 1)
                    lntok[("dveb", key, g, tt)] = LN["f_dveb"].inc()
                    E.ln_guard[("dveb", tt)] = (LN["f_dveb"],
                                                lntok[("dveb", key, g, tt)])
                    E.xslot_guard[tt] = [(LN["f_dveb"],
                                          lntok[("dveb", key, g, tt)])]

            def _trk(te, i):
                te.wait_ge(A["a_dver"].h, b_dver + i + 1)
                wge(te, A["a_dvek"], b_dvek + i)
                for hc in range(DC):
                    inst = te.transpose(
                        ptrk[:, hc * 128:(hc + 1) * 128],
                        kvst[i % 2][:, hc * 128:(hc + 1) * 128], idn[:])
                    if hc == DC - 1:
                        inst.then_inc(A["a_petk"].h, 1)
                assert A["a_petk"].inc() == b_petk + i + 1

            @E.block.tensor
            def _(te, g=g):
                if mi == 0 and g == 0:
                    _entry(te, entry_guard)
                    te.wait_ge(A["c_const"].h, 1)
                for tt in range(4):
                    i = g * 4 + tt
                    te.wait_ge(LN["f_dveb"].h, lntok[("dveb", key, g, tt)])
                    wge(te, A["a_actc"], b_actc + i)
                    for dc in range(DC):
                        inst = te.transpose(
                            ptrx[:, dc * 128:(dc + 1) * 128],
                            xnst[:, (i % 2) * D + dc * 128:
                                 (i % 2) * D + (dc + 1) * 128], idn[:])
                        if dc == DC - 1:
                            inst.then_inc(A["a_petr"].h, 1)
                    assert A["a_petr"].inc() == b_petr + i + 1
                    te.wait_ge(A["a_actc"].h, b_actc + i + 1)
                    if i == 0:
                        te.wait_ge(A["a_lw"].h, tok[(key, "lw")])
                    wge(te, A["a_actp"], b_actp + i)
                    xf = xnf[i % 2]
                    for dc in range(DC):
                        for nn in range(4):
                            inst = te.matmul(
                                pproj[:, nn * 512:(nn + 1) * 512],
                                xf[:, dc * 128:(dc + 1) * 128],
                                aw[:, dc * 2048 + nn * 512:dc * 2048 + (nn + 1) * 512],
                                start=(dc == 0), stop=(dc == DC - 1))
                            if dc == DC - 1 and nn == 3:
                                inst.then_inc(A["a_pep"].h, 1)
                    assert A["a_pep"].inc() == b_pep + i + 1
                    if needs_rope and tt >= 1:
                        _trk(te, i - 1)
                if needs_rope:
                    _trk(te, g * 4 + 3)

            @E.block.scalar
            def _(sc, g=g):
                for tt in range(4):
                    i = g * 4 + tt
                    sl = i % 2
                    sc.wait_ge(A["a_petr"].h, b_petr + i + 1)
                    wge(sc, A["a_pep"], b_pep + i - 1)
                    sc.copy(out=xnf[sl], in_=ptrx[:])\
                        .then_inc(A["a_actc"].h, 1)
                    assert A["a_actc"].inc() == b_actc + i + 1
                    sc.drain()
                    sc.wait_ge(A["a_pep"].h, b_pep + i + 1)
                    if needs_rope:
                        wge(sc, A["a_dver"], b_dver + i - 1)
                        sc.copy(out=qtm[sl], in_=pproj[:])\
                            .then_inc(A["a_actp"].h, 1)
                    else:
                        wge(sc, s_stv[sl], b_stv[sl] + (i // 2) * 16)
                        sc.copy(out=kvst[sl], in_=pproj[:])\
                            .then_inc(A["a_actp"].h, 1)
                    assert A["a_actp"].inc() == b_actp + i + 1
                    sc.drain()

            if needs_rope:
                def _evk(ve, i):
                    sl = i % 2
                    ve.wait_ge(A["a_petk"].h, b_petk + i + 1)
                    wge(ve, s_stk[sl], b_stk[sl] + (i // 2) * 16)
                    ve.tensor_copy(kfst[sl], ptrk[:])\
                        .then_inc(A["a_dvek"].h, 1)
                    assert A["a_dvek"].inc() == b_dvek + i + 1
                    ve.drain()

                @E.block.vector
                def _(ve, g=g):
                    ct, st = (cosq, sinq) if qscale else (cosk, sink)
                    for tt in range(4):
                        i = g * 4 + tt
                        sl = i % 2
                        ve.wait_ge(A["a_actp"].h, b_actp + i + 1)
                        if mi == 0 and g == 0 and tt == 0:
                            ve.wait_ge(A["a_prep"].h, prep_tok)
                        wge(ve, A["a_petk"], b_petk + i - 1)
                        src, dst = qtm[sl], kvst[sl]
                        if layer == 0:
                            t1 = ropet[:, 0:512].rearrange("p (h j) -> p h j", h=H)
                            t2 = ropet[:, 512:1024].rearrange("p (h j) -> p h j", h=H)
                            sview = src.rearrange("p (h j two) -> p h two j",
                                                  h=H, two=2)
                            dview = dst.rearrange("p (h j two) -> p h two j",
                                                  h=H, two=2)
                            qev = sview[:, :, 0, 0:32]
                            qov = sview[:, :, 1, 0:32]
                            dev = dview[:, :, 0, 0:32]
                            dov = dview[:, :, 1, 0:32]
                            cv = ct[:, i * 64:i * 64 + 32].unsqueeze(1)\
                                .broadcast_to((128, H, 32))
                            sv = st[:, i * 64:i * 64 + 32].unsqueeze(1)\
                                .broadcast_to((128, H, 32))
                            ve.tensor_tensor(t1, qev, cv, OP.mult)
                            ve.tensor_tensor(t2, qov, sv, OP.mult)
                            ve.drain()
                            ve.tensor_tensor(dev, t1, t2, OP.subtract)
                            ve.drain()
                            ve.tensor_tensor(t1, qov, cv, OP.mult)
                            ve.tensor_tensor(t2, qev, sv, OP.mult)
                            ve.drain()
                            ve.tensor_tensor(dov, t1, t2, OP.add)
                            sr = src.rearrange("p (h dd) -> p h dd", h=H)[:, :, 64:128]
                            dr = dst.rearrange("p (h dd) -> p h dd", h=H)[:, :, 64:128]
                            inst = ve.tensor_scalar(
                                dr, sr, SC128 if qscale else 1.0, None, OP.mult)
                        else:
                            t1 = ropet[:, 0:1024].rearrange("p (h j) -> p h j", h=H)
                            t2 = ropet[:, 1024:2048].rearrange("p (h j) -> p h j", h=H)
                            sview = src.rearrange("p (h dd) -> p h dd", h=H)
                            dview = dst.rearrange("p (h dd) -> p h dd", h=H)
                            q1, q2 = sview[:, :, 0:64], sview[:, :, 64:128]
                            d1, d2 = dview[:, :, 0:64], dview[:, :, 64:128]
                            cv = ct[:, i * 64:(i + 1) * 64].unsqueeze(1)\
                                .broadcast_to((128, H, 64))
                            sv = st[:, i * 64:(i + 1) * 64].unsqueeze(1)\
                                .broadcast_to((128, H, 64))
                            ve.tensor_tensor(t1, q1, cv, OP.mult)
                            ve.tensor_tensor(t2, q2, sv, OP.mult)
                            ve.drain()
                            ve.tensor_tensor(d1, t1, t2, OP.subtract)
                            ve.drain()
                            ve.tensor_tensor(t1, q2, cv, OP.mult)
                            ve.tensor_tensor(t2, q1, sv, OP.mult)
                            ve.drain()
                            inst = ve.tensor_tensor(d2, t1, t2, OP.add)
                        inst.then_inc(A["a_dver"].h, 1)
                        assert A["a_dver"].inc() == b_dver + i + 1
                        ve.drain()
                        if tt >= 1:
                            _evk(ve, i - 1)
                    _evk(ve, g * 4 + 3)

            @E.block.gpsimd
            def _(gp, g=g):
                for tt in range(4):
                    i = g * 4 + tt
                    sl = i % 2
                    if needs_rope:
                        gp.wait_ge(A["a_dvek"].h, b_dvek + i + 1)
                        dstap = (kv_in if mi == 0 else q_fm)
                        gp.dma_start(
                            out=dstap.rearrange("(c p) t -> p c t", p=128)
                            [:, 0:DC, i * 128:(i + 1) * 128],
                            in_=kfst[sl].rearrange("p (c t) -> p c t", c=DC))\
                            .then_inc(s_stk[sl].h, 16)
                        assert s_stk[sl].inc(16) == b_stk[sl] + (i // 2 + 1) * 16
                        tok[(key, "st", i)] = s_stk[sl].n
                    else:
                        gp.wait_ge(A["a_actp"].h, b_actp + i + 1)
                        gp.dma_start(
                            out=kv_in[2048 + i * 128:2048 + (i + 1) * 128, :],
                            in_=kvst[sl]).then_inc(s_stv[sl].h, 16)
                        assert s_stv[sl].inc(16) == b_stv[sl] + (i // 2 + 1) * 16
                        tok[(key, "st", i)] = s_stv[sl].n
        return key

    kkey = a_phase(0, 1, True, False)
    vkey = a_phase(1, 2, False, False)

    @E.block.gpsimd
    def _(gp):
        for sl in range(2):
            gp.wait_ge(s_stk[sl].h, s_stk[sl].n)
            gp.wait_ge(s_stv[sl].h, s_stv[sl].n)
        gp.collective_compute(
            "AllGather", OP.bypass, replica_groups=replica_groups,
            ins=[kv_in.opt()], outs=[kv_out.opt()]).then_inc(A["a_cc"].h, 1)
        tok[("cc",)] = A["a_cc"].inc()

    qkey = a_phase(2, 0, True, True)
    if upto == "A":
        return

    mask_keys = ([("l0", c, r) for c in range(2) for r in range(2)]
                 if layer == 0 else
                 [("l1", e, c, r) for e in range(3) for c in range(2)
                  for r in range(2)]
                 + [("l1g", e, r) for e in range(3) for r in range(2)])
    masks = {k: mask_store[i] for i, k in enumerate(mask_keys)}
    if layer == 1:
        masks["dil128"] = dil128
    E.attn_masks = masks
    mscr = B["wgs"][:].bitcast(F32)
    iot = mscr[:, 0:256]
    b01 = mscr[:, 256:257]
    gl = mscr[:, 257:513]
    dil = mscr[:, 513:769]
    pp = mscr[:, 769:770]
    t0 = mscr[:, 1024:1280]
    ok = mscr[:, 1280:1536]
    ok2 = mscr[:, 1536:1792]

    @E.block.gpsimd
    def _(gp):
        gp.wait_ge(LN["f_acta"].h, LN["f_acta"].n)
        gp.wait_ge(A["a_dvek"].h, A["a_dvek"].n)
        if layer == 0:
            for (key, m) in masks.items():
                _, c, r = key
                gp.iota(m, pattern=[[-1, 256]], base=128 * r - 256 * c,
                        channel_multiplier=1,
                        allow_small_or_imprecise_dtypes=True)
        else:
            gp.iota(iot, pattern=[[-1, 4], [0, 64]], base=0,
                    channel_multiplier=0, allow_small_or_imprecise_dtypes=True)
            gp.iota(b01, pattern=[[0, 1]], base=-64, channel_multiplier=1,
                    allow_small_or_imprecise_dtypes=True)
            gp.iota(gl, pattern=[[-1, 256]], base=127, channel_multiplier=0,
                    allow_small_or_imprecise_dtypes=True)
            gp.memset(dil128[:, 0:64], 0.0)
            gp.memset(dil128[:, 64:128], -1e9)
        gp.drain()
        gp.memset(pp, 0.0).then_inc(A["a_gpm"].h, 1)
        A["a_gpm"].inc()
    mski_tok = A["a_gpm"].n

    @E.block.vector
    def _(ve):
        ve.wait_ge(A["a_gpm"].h, mski_tok)
        ve.wait_ge(A["c_ldp"].h, A["c_ldp"].n)
        if layer == 0:
            ve.tensor_scalar(pp, pofs[:, 0:1], 256.0, None, OP.mult)
            ve.drain()
            for (key, m) in masks.items():
                ve.tensor_scalar(m, m, pp[:, 0:1], None, OP.add)
                ve.drain()
                ve.tensor_scalar(m, m, 0.0, 1.0, OP.is_ge, OP.subtract)
                ve.drain()
                inst = ve.tensor_scalar(m, m, 1e9, None, OP.mult)
                ve.drain()
        else:
            ve.tensor_scalar(pp, pofs[:, 0:1], 4.0, None, OP.mult)
            ve.tensor_scalar(b01, b01, 0.0, None, OP.is_ge)
            ve.tensor_scalar(gl, gl, 0.0, None, OP.is_ge)
            ve.tensor_scalar(dil, iot, 0.0, None, OP.is_ge)
            ve.drain()
            for key, m in masks.items():
                if key == "dil128":
                    continue
                if key[0] == "l1":
                    _, e, c, r = key
                    glob = False
                else:
                    _, e, r = key
                    c = 0
                    glob = True
                C0 = 8 * e - 4 * c + 2 * r
                ve.tensor_scalar(t0, iot, float(C0), None, OP.add)
                ve.drain()
                ve.tensor_scalar(t0, t0, b01[:, 0:1], pp[:, 0:1], OP.add, OP.add)
                ve.drain()
                ve.tensor_scalar(ok, t0, 16.0, None, OP.is_le)
                ve.drain()
                ve.tensor_tensor(ok, ok, dil, OP.max)
                ve.drain()
                if glob:
                    ve.tensor_tensor(ok, ok, gl, OP.max)
                    ve.drain()
                ve.tensor_scalar(ok2, t0, 0.0, None, OP.is_ge)
                ve.drain()
                ve.tensor_tensor(ok, ok, ok2, OP.mult)
                ve.drain()
                ve.tensor_scalar(m, ok, 1.0, 1e9, OP.subtract, OP.mult)
                ve.drain()
        ve.tensor_scalar(pp, pp, 1.0, None, OP.mult).then_inc(A["a_msk"].h, 1)
        A["a_msk"].inc()
    msk_tok = A["a_msk"].n

    ckey = ("C", layer)
    qst_base = [s.n for s in s_stk]
    for h in range(H):
        hp = h % 2
        @E.block.sync
        def _(sy, h=h, hp=hp):
            if h == 0:
                _entry(sy, entry_guard)
                sy.wait_ge(A["a_cc"].h, tok[("cc",)])
                sy.wait_ge(A["a_dver"].h, A["a_dver"].n)
                sy.wait_ge(A["a_msk"].h, msk_tok)
            prevav = tok.get((ckey, "peav", h - 2, 15))
            if prevav:
                sy.wait_ge(A["a_peav"].h, prevav)
            sy.dma_start(
                out=K_h[hp].rearrange("p (r t) -> p r t", r=2),
                in_=kv_out.rearrange("(r v blk p) t -> r v blk p t",
                                     r=2, v=2, p=128)[:, 0, h, :, :]
                .rearrange("r p t -> p r t")).then_inc(s_lkh[hp].h, 16)
            tok[(ckey, "lk", h)] = s_lkh[hp].inc(16)
            for r in range(2):
                sy.dma_start(
                    out=V_h[hp].rearrange("p (kt c) -> p kt c", kt=16)
                    if False else
                    V_h[hp].rearrange("p (kt c) -> p kt c", c=128)
                    [:, r * 16:(r + 1) * 16, :],
                    in_=kv_out.rearrange("(r v blk p) c -> r v blk p c",
                                         r=2, v=2, p=128)[r, 1, :, :,
                                                          h * 128:(h + 1) * 128]
                    .rearrange("blk p c -> p blk c"))\
                    .then_inc(s_lvh[(hp, r)].h, 16)
                tok[(ckey, "lv", h, r)] = s_lvh[(hp, r)].inc(16)
            if h == 0:
                for sl in range(2):
                    sy.wait_ge(s_stk[sl].h, qst_base[sl])
            sy.dma_start(out=Q_h[hp],
                         in_=q_fm[h * 128:(h + 1) * 128, :])\
                .then_inc(s_lqh[hp].h, 16)
            tok[(ckey, "lq", h)] = s_lqh[hp].inc(16)

        for jt in range(16):
            ext, chunks = attn_chunks(layer, jt)
            nch = len(chunks)
            base_acts = A["a_acts"].n
            base_pes = A["a_pes"].n

            @E.block.tensor
            def _(te, h=h, jt=jt, hp=hp, chunks=chunks, base_acts=base_acts):
                te.wait_ge(s_lkh[hp].h, tok[(ckey, "lk", h)])
                te.wait_ge(s_lqh[hp].h, tok[(ckey, "lq", h)])
                if jt == 0 and h < 2:
                    pass
                off = 0
                for ci, (khcol, w, mk_, vts) in enumerate(chunks):
                    if ci >= 3:
                        wge(te, A["a_acts"], base_acts + ci - 2)
                    else:
                        wge(te, A["a_acts"], base_acts)
                    inst = te.matmul(pS[ci % 3][:, 0:w],
                                     Q_h[hp][:, jt * 128:(jt + 1) * 128],
                                     K_h[hp][:, khcol:khcol + w],
                                     start=True, stop=True)
                    inst.then_inc(A["a_pes"].h, 1)
                    tok[(ckey, "pes", h, jt, ci)] = A["a_pes"].inc()
                    off += w

            @E.block.scalar
            def _(sc, h=h, jt=jt, chunks=chunks):
                off = 0
                for ci, (khcol, w, mk_, vts) in enumerate(chunks):
                    sc.wait_ge(A["a_pes"].h, tok[(ckey, "pes", h, jt, ci)])
                    sc.copy(out=S_slab[:, off:off + w], in_=pS[ci % 3][:, 0:w])\
                        .then_inc(A["a_acts"].h, 1)
                    tok[(ckey, "acts", h, jt, ci)] = A["a_acts"].inc()
                    sc.drain()
                    off += w

            @E.block.gpsimd
            def _(gp, h=h, jt=jt, chunks=chunks):
                if h == 0 and jt == 0:
                    gp.wait_ge(A["a_msk"].h, msk_tok)
                off = 0
                n = 0
                last = None
                for ci, (khcol, w, mk_, vts) in enumerate(chunks):
                    if mk_ is not None:
                        gp.wait_ge(A["a_acts"].h, tok[(ckey, "acts", h, jt, ci)])
                        last = gp.tensor_tensor(
                            S_slab[:, off:off + w], S_slab[:, off:off + w],
                            masks[mk_][:, 0:w], OP.add)
                        gp.drain()
                        n += 1
                    off += w
                if last is not None:
                    last.then_inc(A["a_gpm"].h, 1)
                else:
                    gp.memset(pp, 0.0).then_inc(A["a_gpm"].h, 1)
                tok[(ckey, "gpm", h, jt)] = A["a_gpm"].inc()

            @E.block.vector
            def _(ve, h=h, jt=jt, chunks=chunks, nch=nch, ext=ext):
                ve.wait_ge(A["a_gpm"].h, tok[(ckey, "gpm", h, jt)])
                ve.wait_ge(A["a_acts"].h, tok[(ckey, "acts", h, jt, nch - 1)])
                off = 0
                for ci, (khcol, w, mk_, vts) in enumerate(chunks):
                    ve.reduce_max(st8[:, ci:ci + 1], S_slab[:, off:off + w],
                                  axis=AX.X)
                    off += w
                ve.drain()
                ve.reduce_max(st8[:, 20:21], st8[:, 0:nch], axis=AX.X)
                ve.drain()
                ve.tensor_scalar(st8[:, 21:22], st8[:, 20:21], -1.0, None,
                                 OP.mult).then_inc(A["a_dvemx"].h, 1)
                tok[(ckey, "mx", h, jt)] = A["a_dvemx"].inc()

            @E.block.scalar
            def _(sc, h=h, jt=jt, ext=ext):
                sc.wait_ge(A["a_dvemx"].h, tok[(ckey, "mx", h, jt)])
                prev = tok.get((ckey, "pept1", h, jt - 1)) or \
                    tok.get((ckey, "pept1", h - 1, 15))
                if prev:
                    sc.wait_ge(A["a_pept"].h, prev)
                sc.activation(P_slab[:, 0:ext], S_slab[:, 0:ext], AF.Exp,
                              bias=st8[:, 21:22], accum_out=st8[:, 22:23])\
                    .then_inc(A["a_acte"].h, 1)
                tok[(ckey, "acte", h, jt)] = A["a_acte"].inc()
                sc.drain()

            nk = ext // 128
            nb = (nk + 15) // 16
            for b in range(nb):
                t0_ = b * 16
                t1_ = min(nk, t0_ + 16)
                @E.block.tensor
                def _(te, h=h, jt=jt, b=b, t0_=t0_, t1_=t1_):
                    if b == 0:
                        te.wait_ge(A["a_acte"].h, tok[(ckey, "acte", h, jt)])
                    else:
                        te.wait_ge(A["a_dvept"].h, tok[(ckey, "dvept", h, jt, b - 1)])
                    for k in range(t0_, t1_):
                        inst = te.transpose(
                            ptrp[:, (k - t0_) * 128:(k - t0_ + 1) * 128],
                            P_slab[:, k * 128:(k + 1) * 128], idn[:])
                        if k == t1_ - 1:
                            inst.then_inc(A["a_pept"].h, 1)
                    tok[(ckey, f"pept{b}", h, jt)] = A["a_pept"].inc()
                    if b == nb - 1:
                        tok[(ckey, "pept1", h, jt)] = A["a_pept"].n

                @E.block.vector
                def _(ve, h=h, jt=jt, b=b, t0_=t0_, t1_=t1_):
                    ve.wait_ge(A["a_pept"].h, tok[(ckey, f"pept{b}", h, jt)])
                    ve.tensor_copy(pts[:, t0_ * 128:t1_ * 128],
                                   ptrp[:, 0:(t1_ - t0_) * 128])\
                        .then_inc(A["a_dvept"].h, 1)
                    tok[(ckey, "dvept", h, jt, b)] = A["a_dvept"].inc()
                    ve.drain()

            vtl = [vt for (_, w, _, vts) in chunks for vt in vts]
            @E.block.tensor
            def _(te, h=h, jt=jt, hp=hp, vtl=vtl, nb=nb):
                te.wait_ge(A["a_dvept"].h, tok[(ckey, "dvept", h, jt, nb - 1)])
                for r in range(2):
                    te.wait_ge(s_lvh[(hp, r)].h, tok[(ckey, "lv", h, r)])
                prevcx = tok.get((ckey, "dvecx", h, jt - 1)) or \
                    tok.get((ckey, "dvecx", h - 1, 15))
                if prevcx:
                    te.wait_ge(A["a_dvecx"].h, prevcx)
                for k, vt in enumerate(vtl):
                    inst = te.matmul(
                        pctx[:, 0:128],
                        pts[:, k * 128:(k + 1) * 128],
                        V_h[hp].rearrange("p (kt c) -> p kt c", c=128)[:, vt, :],
                        start=(k == 0), stop=(k == len(vtl) - 1))
                    if k == len(vtl) - 1:
                        inst.then_inc(A["a_peav"].h, 1)
                tok[(ckey, "peav", h, jt)] = A["a_peav"].inc()

            @E.block.vector
            def _(ve, h=h, jt=jt):
                ve.wait_ge(A["a_peav"].h, tok[(ckey, "peav", h, jt)])
                prevcx = tok.get((ckey, "pecx", h, jt - 1)) or \
                    tok.get((ckey, "pecx", h - 1, 15))
                if prevcx:
                    ve.wait_ge(A["a_pecx"].h, prevcx)
                ve.reciprocal(st8[:, 23:24], st8[:, 22:23])
                ve.drain()
                ve.tensor_scalar(ctx_sc, pctx[:, 0:128], st8[:, 23:24], None,
                                 OP.mult).then_inc(A["a_dvecx"].h, 1)
                tok[(ckey, "dvecx", h, jt)] = A["a_dvecx"].inc()
                ve.drain()

            @E.block.tensor
            def _(te, h=h, jt=jt):
                te.wait_ge(A["a_dvecx"].h, tok[(ckey, "dvecx", h, jt)])
                prevcs = tok.get((ckey, "dvecs", h, jt - 1)) or \
                    tok.get((ckey, "dvecs", h - 1, 15))
                if prevcs:
                    te.wait_ge(A["a_dvecs"].h, prevcs)
                te.transpose(pctxT, ctx_sc, idn[:]).then_inc(A["a_pecx"].h, 1)
                tok[(ckey, "pecx", h, jt)] = A["a_pecx"].inc()

            @E.block.vector
            def _(ve, h=h, jt=jt, hp=hp):
                ve.wait_ge(A["a_pecx"].h, tok[(ckey, "pecx", h, jt)])
                if jt == 0:
                    prevst = tok.get((ckey, "sct", h - 2))
                    if prevst:
                        ve.wait_ge(s_sct[hp].h, prevst)
                ve.tensor_copy(ctxh[hp][:, jt * 128:(jt + 1) * 128], pctxT)\
                    .then_inc(A["a_dvecs"].h, 1)
                tok[(ckey, "dvecs", h, jt)] = A["a_dvecs"].inc()
                ve.drain()

        @E.block.gpsimd
        def _(gp, h=h, hp=hp):
            gp.wait_ge(A["a_dvecs"].h, tok[(ckey, "dvecs", h, 15)])
            gp.dma_start(out=ctx_buf[h * 128:(h + 1) * 128, :], in_=ctxh[hp])\
                .then_inc(s_sct[hp].h, 16)
            tok[(ckey, "sct", h)] = s_sct[hp].inc(16)

    if upto == "C":
        return
    dkey = ("Dp", layer)
    base_peo = A["a_peo"].n
    base_dveo = A["a_dveo"].n
    base_stx = {k: s.n for k, s in s_stx.items()}

    @E.block.gpsimd
    def _(gp):
        gp.wait_ge(A["a_pept"].h, A["a_pept"].n)
        gp.wait_ge(A["a_peav"].h, A["a_peav"].n)
        gp.wait_ge(A["a_acte"].h, A["a_acte"].n)
        for sl in range(2):
            gp.wait_ge(s_sct[sl].h, s_sct[sl].n)
        gp.dma_start(out=aw.rearrange("p (c o) -> p c o", c=DC),
                     in_=w_all[3 * 2048:4 * 2048, :]
                     .rearrange("(c p) o -> p c o", p=128))\
            .then_inc(A["a_lw"].h, 16)
        tok[(dkey, "lw")] = A["a_lw"].inc(16)

    for ti in range(16):
        tt = ti % 4
        sl2 = ti % 2

        @E.block.gpsimd
        def _(gq, ti=ti, tt=tt):
            for sem, v in E.xslot_guard.get(tt, []):
                wge(gq, sem, v)
            if E.x_ready is not None:
                for sem, v in E.x_ready[ti]:
                    gq.wait_ge(sem.h, v)
            gq.dma_start(out=xbuf[:, tt * D:(tt + 1) * D],
                         in_=x_src[ti * 128:(ti + 1) * 128, :])\
                .then_inc(s_ldx[tt].h, 16)
            lntok[("ldx", dkey, ti)] = s_ldx[tt].inc(16)

        @E.block.sync
        def _(sy, ti=ti, sl2=sl2):
            if ti == 0:
                for sl in range(2):
                    sy.wait_ge(s_sct[sl].h, s_sct[sl].n)
            prev = tok.get((dkey, "peo", ti - 2))
            if prev:
                sy.wait_ge(A["a_peo"].h, prev)
            sy.dma_start(out=ctxt[sl2].rearrange("p (c t) -> p c t", c=DC),
                         in_=ctx_buf[:, ti * 128:(ti + 1) * 128]
                         .rearrange("(c p) t -> p c t", p=128))\
                .then_inc(s_lct[sl2].h, 16)
            tok[(dkey, "lct", ti)] = s_lct[sl2].inc(16)

        @E.block.tensor
        def _(te, ti=ti, sl2=sl2, base_dveo=base_dveo):
            te.wait_ge(s_lct[sl2].h, tok[(dkey, "lct", ti)])
            if ti == 0:
                te.wait_ge(A["a_lw"].h, tok[(dkey, "lw")])
            wge(te, A["a_dveo"], base_dveo + (ti - 2) * 2 + 2)
            for dc in range(DC):
                for nn in range(4):
                    inst = te.matmul(
                        pD[ti % 2][:, nn * 512:(nn + 1) * 512],
                        ctxt[sl2][:, dc * 128:(dc + 1) * 128],
                        aw[:, dc * 2048 + nn * 512:dc * 2048 + (nn + 1) * 512],
                        start=(dc == 0), stop=(dc == DC - 1))
                    if dc == DC - 1 and nn == 3:
                        inst.then_inc(A["a_peo"].h, 1)
            tok[(dkey, "peo", ti)] = A["a_peo"].inc()

        @E.block.vector
        def _(ve, ti=ti, tt=tt, sl2=sl2):
            ve.wait_ge(A["a_peo"].h, tok[(dkey, "peo", ti)])
            ve.wait_ge(s_ldx[tt].h, lntok[("ldx", dkey, ti)])
            if ti == 0:
                for sl in range(2):
                    ve.wait_ge(s_stk[sl].h, s_stk[sl].n)
                ve.wait_ge(A["a_pep"].h, A["a_pep"].n)
            for oh in range(2):
                sl = tt % 2
                bs = base_stx[(sl, oh)]
                wge(ve, s_stx[(sl, oh)], bs + (ti // 2) * 16)
                xa = xbuf[:, tt * D + oh * 1024:tt * D + (oh + 1) * 1024]
                oa = xout[:, sl * D + oh * 1024:sl * D + (oh + 1) * 1024]
                ve.tensor_tensor(oa, pD[ti % 2][:, oh * 1024:(oh + 1) * 1024],
                                 xa, OP.add).then_inc(A["a_dveo"].h, 1)
                tok[(dkey, "dveo", ti, oh)] = A["a_dveo"].inc()
            E.xslot_guard[tt] = [(A["a_dveo"], tok[(dkey, "dveo", ti, 1)])]

        @E.block.gpsimd
        def _(gp, ti=ti, tt=tt):
            for oh in range(2):
                sl = tt % 2
                gp.wait_ge(A["a_dveo"].h, tok[(dkey, "dveo", ti, oh)])
                gp.dma_start(
                    out=x_dst[ti * 128:(ti + 1) * 128, oh * 1024:(oh + 1) * 1024],
                    in_=xout[:, sl * D + oh * 1024:sl * D + (oh + 1) * 1024])\
                    .then_inc(s_stx[(sl, oh)].h, 16)
                tok[(dkey, "stx", ti, oh)] = s_stx[(sl, oh)].inc(16)

    E.x_ready = {ti: [(s_stx[(ti % 4 % 2, 0)], tok[(dkey, "stx", ti, 0)]),
                      (s_stx[(ti % 4 % 2, 1)], tok[(dkey, "stx", ti, 1)])]
                 for ti in range(16)}
    E.attn_exit_guard = [
        (A["a_peo"], A["a_peo"].n), (A["a_dveo"], A["a_dveo"].n),
        (A["a_acte"], A["a_acte"].n), (A["a_gpm"], A["a_gpm"].n),
        (A["a_peav"], A["a_peav"].n), (A["a_dvecs"], A["a_dvecs"].n),
        (s_stk[0], s_stk[0].n), (s_stk[1], s_stk[1].n),
        (s_stv[0], s_stv[0].n), (s_stv[1], s_stv[1].n),
        (s_sct[0], s_sct[0].n), (s_sct[1], s_sct[1].n),
        (s_lct[0], s_lct[0].n), (s_lct[1], s_lct[1].n)]


"""Fused megakernel for mk_macaron_d2: one raw-Bass program per core, one
run_bass_kernel_spmd launch for the whole forward (6 macaron sublayers).

Sharding: core = 2*b + p (batch b of 4, parity p); each core owns the
interleaved 256-token units of its sequence:
  local token lt -> global position 512*(lt//256) + 256*p + (lt%256).
Weights are sharded 1/8 per core and AllGather'ed on device; activations
stream through per-core DRAM between sublayers inside the single program.
"""
import os
import sys
from contextlib import ExitStack

import numpy as np
import ml_dtypes
import concourse.bass as bass
import concourse.mybir as mybir
from concourse.bass_utils import run_bass_kernel_spmd


T, D, I_FFN = 2048, 2048, 8192



def _l0_attn_wT(w_qkv, w_out):
    rows_q, rows_k, rows_v = [], [], []
    for h in range(16):
        m, j = h // 4, h % 4
        base = 1536 * m + 128 * j
        rows_q.extend(range(base, base + 128))
        rows_v.extend(range(base + 512, base + 640))
        rows_k.extend(range(base + 1024, base + 1152))
    wq = np.ascontiguousarray(w_qkv[rows_q].T)
    wk = np.ascontiguousarray(w_qkv[rows_k].T)
    wv = np.ascontiguousarray(w_qkv[rows_v].T)
    wo = np.ascontiguousarray(w_out.T)
    return np.concatenate([wq, wk, wv, wo], axis=0).astype(np.float32)


def _l1_attn_wT(wq, wk, wv, wo):
    return np.concatenate([np.ascontiguousarray(w.T) for w in (wq, wk, wv, wo)],
                          axis=0).astype(np.float32)


_PERM = (4 * np.arange(I_FFN // 4)[None, :] + np.arange(4)[:, None]).reshape(-1)


def _host_prep(inputs):
    h = {}
    h["wa"] = np.concatenate([
        _l0_attn_wT(np.asarray(inputs["L0_attn_w_qkv"]),
                    np.asarray(inputs["L0_attn_w_out"])),
        _l1_attn_wT(np.asarray(inputs["L1_attn_w_q"]),
                    np.asarray(inputs["L1_attn_w_k"]),
                    np.asarray(inputs["L1_attn_w_v"]),
                    np.asarray(inputs["L1_attn_w_o"]))], axis=0)
    g0 = np.asarray(inputs["L0_ffn_gate_w"]).astype(np.uint8)
    u0 = np.asarray(inputs["L0_ffn_up_w"]).astype(np.uint8)
    d0 = np.asarray(inputs["L0_ffn_down_w"]).astype(np.uint8)
    h["wq8"] = np.concatenate([np.ascontiguousarray(g0.T),
                               np.ascontiguousarray(u0.T)], axis=0)
    h["wq8d"] = np.ascontiguousarray(d0.T)
    wg1 = np.asarray(inputs["L1_ffn_w_gate"])[:, _PERM]
    wu1 = np.asarray(inputs["L1_ffn_w_up"])[:, _PERM]
    wd1 = np.asarray(inputs["L1_ffn_w_down"])[_PERM, :]
    h["wf1"] = np.concatenate([wg1, wu1], axis=0).astype(np.float32)
    h["wf1d"] = np.ascontiguousarray(wd1).astype(np.float32)
    h["gsc"] = np.ascontiguousarray(
        np.asarray(inputs["L0_ffn_gate_scale"]).reshape(64, 128).T).astype(np.float32)
    h["usc"] = np.ascontiguousarray(
        np.asarray(inputs["L0_ffn_up_scale"]).reshape(64, 128).T).astype(np.float32)
    h["dsc"] = np.ascontiguousarray(np.broadcast_to(
        np.asarray(inputs["L0_ffn_down_scale"]), (128, D))).astype(np.float32)
    iv0 = np.zeros(64, np.float32)
    iv0[:32] = 1.0 / 10000.0 ** (np.arange(0, 64, 2) / 64)
    iv1 = (1.0 / 10000.0 ** (np.arange(0, 128, 2) / 128)).astype(np.float32)
    h["invf0"] = np.broadcast_to(iv0, (128, 64)).copy()
    h["invf1"] = np.broadcast_to(iv1, (128, 64)).copy()
    return h


def _shard_x(x, b, p):
    out = np.empty((T, D), np.float32)
    for u in range(8):
        out[u * 256:(u + 1) * 256] = x[b, 512 * u + 256 * p:512 * u + 256 * p + 256]
    return np.ascontiguousarray(out)


def _unshard(res):
    y = np.empty((4, 4096, D), np.float32)
    for c in range(8):
        b, p = c // 2, c % 2
        o = res[c]
        for u in range(8):
            y[b, 512 * u + 256 * p:512 * u + 256 * p + 256] = \
                o[u * 256:(u + 1) * 256]
    return y



def build_program(ncores=8):
    nc = bass.Bass("TRN2", target_bir_lowering=False, debug=False)
    WORLD = [list(range(ncores))]
    RG = [[i, i + 1] for i in range(0, ncores, 2)]
    ap = {}
    ein = {
        "xin": ([T, D], F32),
        "wa_sh": ([16384 // ncores, 2048], F32),
        "wq8_sh": ([4096 // ncores, 8192], U8),
        "wq8d_sh": ([8192 // ncores, 2048], U8),
        "wf1_sh": ([4096 // ncores, 8192], F32),
        "wf1d_sh": ([8192 // ncores, 2048], F32),
        "gsc": ([128, 64], F32),
        "usc": ([128, 64], F32),
        "dsc": ([128, D], F32),
        "pofs_i": ([128, 1], F32),
        "qpos_i": ([128, 16], F32),
        "invf0_i": ([128, 64], F32),
        "invf1_i": ([128, 64], F32),
    }
    for n, (shape, dt) in ein.items():
        ap[n] = nc.dram_tensor(n, shape, dt, kind="ExternalInput").ap()
    ap["xout"] = nc.dram_tensor("xout", [T, D], F32, kind="ExternalOutput").ap()
    intr = {
        "wa_b": ([16384 // ncores, 2048], F32), "wa": ([16384, 2048], F32),
        "wq8_b": ([4096 // ncores, 8192], U8), "wq8": ([4096, 8192], U8),
        "wq8d_b": ([8192 // ncores, 2048], U8), "wq8d": ([8192, 2048], U8),
        "wf1_b": ([4096 // ncores, 8192], F32), "wf1": ([4096, 8192], F32),
        "wf1d_b": ([8192 // ncores, 2048], F32), "wf1d": ([8192, 2048], F32),
        "l0gu": ([4096, 8192], BF16), "l0d": ([8192, 2048], BF16),
        "l1gu": ([4096, 8192], BF16), "l1d": ([8192, 2048], BF16),
        "xs1": ([T, D], F32), "xs2": ([T, D], F32),
        "q_fm": ([2048, 2048], BF16), "kv_in": ([4096, 2048], BF16),
        "kv_out": ([8192, 2048], BF16), "ctx_buf": ([2048, 2048], BF16),
    }
    for n, (shape, dt) in intr.items():
        ap[n] = nc.dram_tensor(n, shape, dt).ap()

    with ExitStack() as ctx:
        with nc.Block() as block:
            E = Env(nc, block, ctx)
            emit_const_prologue(E)
            pofs = E.sbuf("p_pofs", [128, 1], F32)
            qpos_c = E.sbuf("p_qpos", [128, 16], F32)
            invf = [E.sbuf("p_invf0", [128, 64], F32),
                    E.sbuf("p_invf1", [128, 64], F32)]
            gs = E.sbuf("l0_gs", [128, 64], F32)
            us = E.sbuf("l0_us", [128, 64], F32)
            sbc = E.sbuf("l0_sbc", [128, D], F32)
            ldp = E.sem("c_ldp")
            sbcs = E.sem("c_sbc")

            @E.block.sync
            def _(sy):
                for (dst, src) in ((pofs, ap["pofs_i"]), (qpos_c, ap["qpos_i"]),
                                   (invf[0], ap["invf0_i"]),
                                   (invf[1], ap["invf1_i"]),
                                   (gs, ap["gsc"]), (us, ap["usc"]),
                                   (sbc, ap["dsc"])):
                    sy.dma_start(out=dst[:], in_=src).then_inc(ldp.h, 16)
                    ldp.inc(16)

            @E.block.vector
            def _(ve):
                ve.wait_ge(ldp.h, ldp.n)
                ve.tensor_scalar(sbc[:], sbc[:], 0.5, None, OP.mult)\
                    .then_inc(sbcs.h, 1)
                sbcs.inc()

            cvt = {n: E.sem(f"c_cv_{n}") for n in ["q8", "q8d", "f1", "f1d"]}
            gat = {n: E.sem(f"c_ag_{n}") for n in
                   ["wa", "wq8", "wq8d", "wf1", "wf1d"]}
            bnc = {n: E.sem(f"c_bnc_{n}") for n in
                   ["wq8", "wq8d", "wa", "wf1", "wf1d"]}

            @E.block.gpsimd
            def _(gp):
                for (sh, b, n) in (("wq8_sh", "wq8_b", "wq8"),
                                   ("wq8d_sh", "wq8d_b", "wq8d"),
                                   ("wa_sh", "wa_b", "wa"),
                                   ("wf1_sh", "wf1_b", "wf1"),
                                   ("wf1d_sh", "wf1d_b", "wf1d")):
                    gp.dma_start(out=ap[b], in_=ap[sh]).then_inc(bnc[n].h, 16)
                    bnc[n].inc(16)
                gp.wait_ge(bnc["wq8"].h, 16)
                gp.wait_ge(bnc["wq8d"].h, 16)
                gp.collective_compute(
                    "AllGather", OP.bypass, replica_groups=WORLD,
                    ins=[ap["wq8_b"].opt()], outs=[ap["wq8"].opt()])\
                    .then_inc(gat["wq8"].h, 1)
                gat["wq8"].inc()
                gp.collective_compute(
                    "AllGather", OP.bypass, replica_groups=WORLD,
                    ins=[ap["wq8d_b"].opt()], outs=[ap["wq8d"].opt()])\
                    .then_inc(gat["wq8d"].h, 1)
                gat["wq8d"].inc()
                gp.wait_ge(bnc["wa"].h, 16)
                gp.wait_ge(bnc["wf1"].h, 16)
                gp.wait_ge(bnc["wf1d"].h, 16)
                gp.collective_compute(
                    "AllGather", OP.bypass, replica_groups=WORLD,
                    ins=[ap["wa_b"].opt()], outs=[ap["wa"].opt()])\
                    .then_inc(gat["wa"].h, 1)
                gat["wa"].inc()
                gp.collective_compute(
                    "AllGather", OP.bypass, replica_groups=WORLD,
                    ins=[ap["wf1_b"].opt()], outs=[ap["wf1"].opt()])\
                    .then_inc(gat["wf1"].h, 1)
                gat["wf1"].inc()
                gp.collective_compute(
                    "AllGather", OP.bypass, replica_groups=WORLD,
                    ins=[ap["wf1d_b"].opt()], outs=[ap["wf1d"].opt()])\
                    .then_inc(gat["wf1d"].h, 1)
                gat["wf1d"].inc()
                gp.wait_ge(gat["wq8"].h, 1)
                for i in range(8):
                    gp.dma_start(out=ap["l0gu"][i * 512:(i + 1) * 512, :],
                                 in_=ap["wq8"][i * 512:(i + 1) * 512, :],
                                 max_dma_last_dim=2048).then_inc(cvt["q8"].h, 16)
                    cvt["q8"].inc(16)
                gp.wait_ge(gat["wq8d"].h, 1)
                for i in range(8):
                    gp.dma_start(out=ap["l0d"][i * 1024:(i + 1) * 1024, :],
                                 in_=ap["wq8d"][i * 1024:(i + 1) * 1024, :],
                                 max_dma_last_dim=2048).then_inc(cvt["q8d"].h, 16)
                    cvt["q8d"].inc(16)
                gp.wait_ge(gat["wf1"].h, 1)
                for i in range(8):
                    gp.dma_start(out=ap["l1gu"][i * 512:(i + 1) * 512, :],
                                 in_=ap["wf1"][i * 512:(i + 1) * 512, :],
                                 max_dma_last_dim=8192).then_inc(cvt["f1"].h, 16)
                    cvt["f1"].inc(16)
                gp.wait_ge(gat["wf1d"].h, 1)
                for i in range(8):
                    gp.dma_start(out=ap["l1d"][i * 1024:(i + 1) * 1024, :],
                                 in_=ap["wf1d"][i * 1024:(i + 1) * 1024, :],
                                 max_dma_last_dim=8192).then_inc(cvt["f1d"].h, 16)
                    cvt["f1d"].inc(16)

            def ffn_exit():
                names = ["f_pednc", "f_pegu", "f_petr", "f_pesh", "f_actd",
                         "f_actc", "f_acta", "f_actb", "f_dvea", "f_dveb",
                         "f_dvec", "f_dved", "f_dvesh", "f_gpa",
                         "f_stx00", "f_stx01", "f_stx10", "f_stx11"]
                return [(E.sems[n], E.sems[n].n) for n in names if n in E.sems]

            l0ffn = dict(wg=ap["l0gu"][0:2048, :], wu=ap["l0gu"][2048:4096, :],
                         wd=ap["l0d"], gs=gs, us=us, sbc=sbc)
            l1ffn = dict(wg=ap["l1gu"][0:2048, :], wu=ap["l1gu"][2048:4096, :],
                         wd=ap["l1d"])
            guard0 = ([(cvt["q8"], cvt["q8"].n), (cvt["q8d"], cvt["q8d"].n),
                       (sbcs, sbcs.n)])
            emit_ffn(E, "l0", ap["xin"], ap["xs1"], entry_guard=guard0,
                        **l0ffn)
            emit_attn(E, 0, ap["xs1"], ap["xs2"], ap["wa"][0:8192, :],
                          ap["q_fm"], ap["kv_in"], ap["kv_out"], ap["ctx_buf"],
                          pofs, qpos_c, invf[0],
                          ffn_exit() + [(gat["wa"], gat["wa"].n)], RG)
            emit_ffn(E, "l0", ap["xs2"], ap["xs1"],
                        entry_guard=E.attn_exit_guard, **l0ffn)
            guard1 = ffn_exit() + [(cvt["f1"], cvt["f1"].n),
                                   (cvt["f1d"], cvt["f1d"].n)]
            emit_ffn(E, "l1", ap["xs1"], ap["xs2"], entry_guard=guard1,
                        **l1ffn)
            emit_attn(E, 1, ap["xs2"], ap["xs1"], ap["wa"][8192:16384, :],
                          ap["q_fm"], ap["kv_in"], ap["kv_out"], ap["ctx_buf"],
                          pofs, qpos_c, invf[1], ffn_exit(), RG)
            emit_ffn(E, "l1", ap["xs1"], ap["xout"],
                        entry_guard=E.attn_exit_guard, **l1ffn)
            emit_final_wait(E)
    return nc


_NC_CACHE = [None]


def make_in_maps(inputs, ncores=8):
    h = _host_prep(inputs)
    x = np.asarray(inputs["L0_attn_hidden_states"], dtype=np.float32)
    in_maps = []
    for c in range(ncores):
        b, p = c // 2, c % 2
        pos = np.array([512 * (lt // 256) + 256 * p + (lt % 256)
                        for lt in range(T)], np.float32)
        r1, r2, r3 = 16384 // ncores, 4096 // ncores, 8192 // ncores
        in_maps.append({
            "xin": _shard_x(x, b, p),
            "wa_sh": np.ascontiguousarray(h["wa"][c * r1:(c + 1) * r1]),
            "wq8_sh": np.ascontiguousarray(h["wq8"][c * r2:(c + 1) * r2]),
            "wq8d_sh": np.ascontiguousarray(h["wq8d"][c * r3:(c + 1) * r3]),
            "wf1_sh": np.ascontiguousarray(h["wf1"][c * r2:(c + 1) * r2]),
            "wf1d_sh": np.ascontiguousarray(h["wf1d"][c * r3:(c + 1) * r3]),
            "gsc": h["gsc"], "usc": h["usc"], "dsc": h["dsc"],
            "pofs_i": np.full((128, 1), float(p), np.float32),
            "qpos_i": np.ascontiguousarray(pos.reshape(16, 128).T),
            "invf0_i": h["invf0"], "invf1_i": h["invf1"],
        })
    return in_maps


def kernel(**inputs):
    if _NC_CACHE[0] is None:
        _NC_CACHE[0] = build_program()
    nc = _NC_CACHE[0]
    in_maps = make_in_maps(inputs, 8)
    res = run_bass_kernel_spmd(nc, in_maps, core_ids=list(range(8)))
    return _unshard([res.results[c]["xout"] for c in range(8)])


# version 02
# speedup vs baseline: 1.0003x; 1.0003x over previous
import numpy as np
import concourse.bass as bass
import concourse.mybir as mybir

F32 = mybir.dt.float32
BF16 = mybir.dt.bfloat16
U8 = mybir.dt.uint8
AF = mybir.ActivationFunctionType
OP = mybir.AluOpType
AX = mybir.AxisListType

D = 2048
I_FFN = 8192
T = 2048
NTILE = 16
PASS_T = 512
NPASS = T // PASS_T
DC = D // 128
IC = I_FFN // 128
H = 16
DH = 128
UNIT = 256
EPS = 1e-5


class Sem:
    def __init__(self, h):
        self.h = h
        self.n = 0

    def inc(self, k=1):
        self.n += k
        return self.n


class Env:
    def __init__(self, nc, block, ctx):
        self.nc = nc
        self.block = block
        self.ctx = ctx
        self.sems = {}
        self.sbufs = {}
        self.x_ready = None

    def sem(self, name):
        if name not in self.sems:
            self.sems[name] = Sem(self.ctx.enter_context(self.nc.semaphore(name)))
        return self.sems[name]

    def sbuf(self, name, shape, dtype):
        if name not in self.sbufs:
            self.sbufs[name] = self.ctx.enter_context(
                self.nc.sbuf_tensor(name, shape, dtype))
        return self.sbufs[name]

    def psum(self, name, shape, dtype):
        if name not in self.sbufs:
            self.sbufs[name] = self.ctx.enter_context(
                self.nc.psum_tensor(name, shape, dtype))
        return self.sbufs[name]


def wge(eng, sem, val):
    if val > 0:
        eng.wait_ge(sem.h, val)


def emit_const_prologue(E):
    idn = E.sbuf("idn", [128, 128], BF16)
    ones_col = E.sbuf("ones_col", [128, 1], BF16)
    ones_row = E.sbuf("ones_row", [1, 512], BF16)
    epsb = E.sbuf("epsb", [128, 1], F32)
    c = E.sem("c_const")

    @E.block.gpsimd
    def _(gp):
        gp.memset(idn[:], 1.0)
        gp.drain()
        gp.affine_select(idn[:], idn[:], pattern=[[-1, 128]],
                         compare_op=OP.is_equal, fill=0.0, base=0,
                         channel_multiplier=1)
        gp.memset(ones_col[:], 1.0)
        gp.memset(epsb[:], EPS)
        gp.memset(ones_row[:], 1.0).then_inc(c.h, 1)
        c.inc()


def ffn_buffers(E):
    return dict(
        xbuf=E.sbuf("c_x", [128, 4 * D], F32),
        xout=E.sbuf("c_xout", [128, 2 * D], F32),
        dn_tmp=E.sbuf("f_dntmp", [128, 1024], F32),
        xsq=E.sbuf("c_xsq", [128, D], F32),
        st8=E.sbuf("c_st8", [128, 32], F32),
        xnst=E.sbuf("c_xnst", [128, 2 * D], BF16),
        xn=E.sbuf("f_xn", [128, DC * PASS_T], BF16),
        hsl=E.sbuf("f_h", [128, IC * PASS_T], BF16),
        wgs=E.sbuf("f_wg", [128, 2 * DC * 128], BF16),
        wus=E.sbuf("f_wu", [128, 2 * DC * 128], BF16),
        wds=E.sbuf("f_wd", [128, 2 * 1024], BF16),
        sg=E.sbuf("f_sg", [128, 2 * PASS_T], F32),
        hq=E.sbuf("f_hq", [128, 4 * PASS_T], F32),
        qsq=E.sbuf("f_qsq", [128, 4 * PASS_T], F32),
        qt2=E.sbuf("f_qt2", [128, 4 * PASS_T], F32),
        shs=E.sbuf("f_shs", [1, PASS_T], BF16),
        pall=E.psum("pall", [128, 4096], F32),
    )


def emit_ffn(E, kind, x_src, x_dst, wg, wu, wd, gs=None, us=None, sbc=None,
             entry_guard=()):
    B = ffn_buffers(E)
    xbuf, xout, xsq, st8, xnst = B["xbuf"], B["xout"], B["xsq"], B["st8"], B["xnst"]
    xn, hsl, wgs, wus, wds = B["xn"], B["hsl"], B["wgs"], B["wus"], B["wds"]
    sg, hq, qsq, qt2, shs = B["sg"], B["hq"], B["qsq"], B["qt2"], B["shs"]
    dn_tmp, pall = B["dn_tmp"], B["pall"]
    ones_col = E.sbufs["ones_col"]
    ones_row = E.sbufs["ones_row"]
    idn = E.sbufs["idn"]
    epsb = E.sbufs["epsb"]

    pg = [pall[:, 0:512], pall[:, 1024:1536]]
    pu = [pall[:, 512:1024], pall[:, 1536:2048]]
    psh = pall[0:1, 2048:2560]
    ptr = pall[:, 3072:4096].bitcast(BF16)
    pdn = [pall[:, tt * 1024:(tt + 1) * 1024] for tt in range(4)]

    s_ldx = [E.sem(f"f_ldx{i}") for i in range(4)]
    s_lwg = [E.sem(f"f_lwg{i}") for i in range(2)]
    s_lwu = [E.sem(f"f_lwu{i}") for i in range(2)]
    s_ldd = [E.sem(f"f_ldd{i}") for i in range(2)]
    s_stx = {(sl, oh): E.sem(f"f_stx{sl}{oh}")
             for sl in range(2) for oh in range(2)}
    S = {n: E.sem(n) for n in [
        "f_acta", "f_actb", "f_actc", "f_actd",
        "f_dvea", "f_dveb", "f_dvec", "f_dvesh", "f_dved",
        "f_petr", "f_pegu", "f_pesh", "f_pednc", "f_pedno", "f_gpa",
        "c_const"]}
    l0 = kind == "l0"
    ic_order = (list(range(IC)) if l0 else
                [16 * m + qd for qd in range(16) for m in range(4)])

    tok = {}

    x_ready_out = {}
    if not hasattr(E, "ln_guard"):
        E.ln_guard = {}
    if not hasattr(E, "xslot_guard"):
        E.xslot_guard = {}

    def _entry(eng):
        for sem, v in entry_guard:
            wge(eng, sem, v)

    for p in range(NPASS):
        base_petr = S["f_petr"].n
        base_pegu = S["f_pegu"].n
        base_pednc = S["f_pednc"].n
        base_actc = S["f_actc"].n
        base_actd = S["f_actd"].n
        base_dvec = S["f_dvec"].n
        base_dved = S["f_dved"].n
        base_stx = {k: s.n for k, s in s_stx.items()}

        @E.block.gpsimd
        def _(gq, p=p):
            if p == 0:
                _entry(gq)
            for tt in range(4):
                ti = p * 4 + tt
                for sem, v in E.xslot_guard.get(tt, []):
                    wge(gq, sem, v)
                if E.x_ready is not None:
                    for sem, v in E.x_ready[ti]:
                        gq.wait_ge(sem.h, v)
                gq.dma_start(out=xbuf[:, tt * D:(tt + 1) * D],
                             in_=x_src[ti * 128:(ti + 1) * 128, :])\
                    .then_inc(s_ldx[tt].h, 16)
                tok[("ldx", p, tt)] = s_ldx[tt].inc(16)

        @E.block.sync
        def _(sy, p=p, base_pegu=base_pegu, base_pednc=base_pednc):
            if p == 0:
                _entry(sy)
            for j, ic in enumerate(ic_order):
                slot = j % 2
                wge(sy, S["f_pegu"], base_pegu + j - 1)
                for (w, dst, sem) in ((wg, wgs, s_lwg[slot]),
                                      (wu, wus, s_lwu[slot])):
                    ap = w[:, ic * 128:(ic + 1) * 128].rearrange(
                        "(c p) i -> p c i", p=128)
                    sy.dma_start(out=dst[:, slot * 2048:(slot + 1) * 2048]
                                 .rearrange("p (c i) -> p c i", c=DC),
                                 in_=ap).then_inc(sem.h, 16)
                    sem.inc(16)
                tok[("ldw", p, j)] = (s_lwg[slot].n, s_lwu[slot].n)
            for oh in range(2):
                for ic in range(IC):
                    k = oh * IC + ic
                    slot = k % 2
                    wge(sy, S["f_pednc"], base_pednc + k - 1)
                    sy.dma_start(out=wds[:, slot * 1024:(slot + 1) * 1024],
                                 in_=wd[ic * 128:(ic + 1) * 128,
                                        oh * 1024:(oh + 1) * 1024])\
                        .then_inc(s_ldd[slot].h, 16)
                    tok[("ldd", p, oh, ic)] = s_ldd[slot].inc(16)

        @E.block.scalar
        def _(sc, p=p):
            if p == 0:
                _entry(sc)
            for tt in range(4):
                sc.wait_ge(s_ldx[tt].h, tok[("ldx", p, tt)])
                prev = E.ln_guard.get(("dvea", tt))
                if prev:
                    sc.wait_ge(prev[0].h, prev[1])
                sc.activation(xsq[:], xbuf[:, tt * D:(tt + 1) * D], AF.Square,
                              accum_out=st8[:, tt:tt + 1]).then_inc(S["f_acta"].h, 1)
                tok[("acta", p, tt)] = S["f_acta"].inc()
                sc.drain()

        @E.block.vector
        def _(ve, p=p):
            if p == 0:
                _entry(ve)
            for tt in range(4):
                ve.wait_ge(s_ldx[tt].h, tok[("ldx", p, tt)])
                prev = E.ln_guard.get(("actb", tt))
                if prev:
                    ve.wait_ge(prev[0].h, prev[1])
                ve.reduce_sum(st8[:, 4 + tt:5 + tt], xbuf[:, tt * D:(tt + 1) * D],
                              axis=AX.X)
            ve.drain()
            for tt in range(4):
                ve.tensor_scalar(st8[:, 8 + tt:9 + tt], st8[:, 4 + tt:5 + tt],
                                 1.0 / D, None, OP.mult)
            for tt in range(4):
                ve.wait_ge(S["f_acta"].h, tok[("acta", p, tt)])
                ve.tensor_scalar(st8[:, 20 + tt:21 + tt], st8[:, tt:tt + 1],
                                 1.0 / D, None, OP.mult)
            ve.drain()
            for tt in range(4):
                ve.tensor_tensor(st8[:, 24 + tt:25 + tt], st8[:, 8 + tt:9 + tt],
                                 st8[:, 8 + tt:9 + tt], OP.mult)
            ve.drain()
            for tt in range(4):
                ve.scalar_tensor_tensor(st8[:, 12 + tt:13 + tt],
                                        st8[:, 24 + tt:25 + tt], -1.0,
                                        st8[:, 20 + tt:21 + tt],
                                        OP.mult, OP.add).then_inc(S["f_dvea"].h, 1)
                tok[("dvea", p, tt)] = S["f_dvea"].inc()
                E.ln_guard[("dvea", tt)] = (S["f_dvea"], tok[("dvea", p, tt)])

        @E.block.scalar
        def _(sc, p=p):
            if p == 0:
                sc.wait_ge(S["c_const"].h, 1)
            for tt in range(4):
                sc.wait_ge(S["f_dvea"].h, tok[("dvea", p, tt)])
                prev = E.ln_guard.get(("dveb", tt))
                if prev:
                    sc.wait_ge(prev[0].h, prev[1])
                sc.activation(st8[:, 28 + tt:29 + tt], st8[:, 12 + tt:13 + tt],
                              AF.Ln, bias=epsb[:, 0:1])
            sc.drain()
            for tt in range(4):
                sc.activation(st8[:, 16 + tt:17 + tt], st8[:, 28 + tt:29 + tt],
                              AF.Exp, scale=-0.5).then_inc(S["f_actb"].h, 1)
                tok[("actb", p, tt)] = S["f_actb"].inc()
                E.ln_guard[("actb", tt)] = (S["f_actb"], tok[("actb", p, tt)])

        @E.block.vector
        def _(ve, p=p, base_petr=base_petr):
            for tt in range(4):
                ve.wait_ge(S["f_actb"].h, tok[("actb", p, tt)])
                wge(ve, S["f_petr"], base_petr + tt - 1)
                ve.tensor_scalar(xnst[:, (tt % 2) * D:(tt % 2 + 1) * D],
                                 xbuf[:, tt * D:(tt + 1) * D],
                                 st8[:, 8 + tt:9 + tt], st8[:, 16 + tt:17 + tt],
                                 OP.subtract, OP.mult).then_inc(S["f_dveb"].h, 1)
                tok[("dveb", p, tt)] = S["f_dveb"].inc()
                E.ln_guard[("dveb", tt)] = (S["f_dveb"], tok[("dveb", p, tt)])

        @E.block.tensor
        def _(te, p=p, base_actc=base_actc):
            if p == 0:
                _entry(te)
            te.wait_ge(S["c_const"].h, 1)
            for tt in range(4):
                te.wait_ge(S["f_dveb"].h, tok[("dveb", p, tt)])
                wge(te, S["f_actc"], base_actc + tt)
                d31 = tok.get(("dved", p - 1, 3, 1))
                if tt == 0 and d31:
                    te.wait_ge(S["f_dved"].h, d31)
                for dc in range(DC):
                    inst = te.transpose(
                        ptr[:, dc * 128:(dc + 1) * 128],
                        xnst[:, (tt % 2) * D + dc * 128:(tt % 2) * D + (dc + 1) * 128],
                        idn[:])
                    if dc == DC - 1:
                        inst.then_inc(S["f_petr"].h, 1)
                tok[("petr", p, tt)] = S["f_petr"].inc()

        @E.block.scalar
        def _(sc, p=p, base_pegu=base_pegu):
            for tt in range(4):
                sc.wait_ge(S["f_petr"].h, tok[("petr", p, tt)])
                if p > 0 and tt == 0:
                    sc.wait_ge(S["f_pegu"].h, base_pegu)
                sc.copy(
                    out=xn[:].rearrange("p (c t) -> p c t", c=DC)
                    [:, :, tt * 128:(tt + 1) * 128],
                    in_=ptr[:].rearrange("p (c t) -> p c t", c=DC))\
                    .then_inc(S["f_actc"].h, 1)
                tok[("actc", p, tt)] = S["f_actc"].inc()

        @E.block.tensor
        def _(te, p=p, base_actd=base_actd, base_dvec=base_dvec):
            te.wait_ge(S["f_actc"].h, tok[("actc", p, 3)])
            d = tok.get(("dved", p - 1, 3, 1))
            if d:
                te.wait_ge(S["f_dved"].h, d)
            for j, ic in enumerate(ic_order):
                slot = j % 2
                vg, vu = tok[("ldw", p, j)]
                te.wait_ge(s_lwg[slot].h, vg)
                te.wait_ge(s_lwu[slot].h, vu)
                wge(te, S["f_actd"], base_actd + j - 1)
                wge(te, S["f_dvec"], base_dvec + j - 1)
                for dc in range(DC):
                    te.matmul(pg[slot][:, :],
                              wgs[:, slot * 2048 + dc * 128:slot * 2048 + (dc + 1) * 128],
                              xn[:, dc * PASS_T:(dc + 1) * PASS_T],
                              start=(dc == 0), stop=(dc == DC - 1))
                for dc in range(DC):
                    inst = te.matmul(
                        pu[slot][:, :],
                        wus[:, slot * 2048 + dc * 128:slot * 2048 + (dc + 1) * 128],
                        xn[:, dc * PASS_T:(dc + 1) * PASS_T],
                        start=(dc == 0), stop=(dc == DC - 1))
                    if dc == DC - 1:
                        inst.then_inc(S["f_pegu"].h, 1)
                tok[("pegu", p, j)] = S["f_pegu"].inc()

        @E.block.scalar
        def _(sc, p=p, base_dvec=base_dvec):
            for j, ic in enumerate(ic_order):
                slot = j % 2
                sgslot = sg[:, slot * PASS_T:(slot + 1) * PASS_T]
                sc.wait_ge(S["f_pegu"].h, tok[("pegu", p, j)])
                wge(sc, S["f_dvec"], base_dvec + j - 1)
                sc.activation(sgslot, pg[slot][:, :], AF.Silu,
                              scale=(gs[:, ic:ic + 1] if l0 else 1.0))\
                    .then_inc(S["f_actd"].h, 1)
                tok[("actd", p, j)] = S["f_actd"].inc()

        @E.block.vector
        def _(ve, p=p):
            d = tok.get(("pednc", p - 1))
            if d:
                ve.wait_ge(S["f_pednc"].h, d)
            for j, ic in enumerate(ic_order):
                slot = j % 2
                sgslot = sg[:, slot * PASS_T:(slot + 1) * PASS_T]
                ve.wait_ge(S["f_actd"].h, tok[("actd", p, j)])
                ve.wait_ge(S["f_pegu"].h, tok[("pegu", p, j)])
                if l0:
                    ve.scalar_tensor_tensor(
                        hsl[:, ic * PASS_T:(ic + 1) * PASS_T], pu[slot][:, :],
                        us[:, ic:ic + 1], sgslot, OP.mult, OP.mult)\
                        .then_inc(S["f_dvec"].h, 1)
                else:
                    m = ic // 16
                    ve.tensor_tensor(hq[:, m * PASS_T:(m + 1) * PASS_T],
                                     sgslot, pu[slot][:, :], OP.mult)\
                        .then_inc(S["f_dvec"].h, 1)
                tok[("dvec", p, j)] = S["f_dvec"].inc()
                if not l0 and j % 4 == 3:
                    qd = ic_order[j] % 16
                    sq = [qsq[:, m * PASS_T:(m + 1) * PASS_T] for m in range(4)]
                    hm = [hq[:, m * PASS_T:(m + 1) * PASS_T] for m in range(4)]
                    A = qt2[:, 0:PASS_T]
                    Bb = qt2[:, PASS_T:2 * PASS_T]
                    C = qt2[:, 2 * PASS_T:3 * PASS_T]
                    Dd = qt2[:, 3 * PASS_T:4 * PASS_T]
                    ve.drain()
                    for m in range(4):
                        ve.tensor_tensor(sq[m], hm[m], hm[m], OP.mult)
                    ve.drain()
                    ve.tensor_tensor(A, sq[0], sq[1], OP.max)
                    ve.tensor_tensor(Bb, sq[2], sq[3], OP.max)
                    ve.tensor_tensor(C, sq[0], sq[1], OP.min)
                    ve.tensor_tensor(Dd, sq[2], sq[3], OP.min)
                    ve.drain()
                    ve.tensor_tensor(A, A, Bb, OP.min)
                    ve.tensor_tensor(C, C, Dd, OP.max)
                    ve.drain()
                    ve.tensor_tensor(A, A, C, OP.max)
                    ve.drain()
                    for m in range(4):
                        ve.tensor_tensor(sq[m], sq[m], A, OP.is_ge)
                    ve.drain()
                    for m in range(4):
                        icm = 16 * m + qd
                        inst = ve.tensor_tensor(
                            hsl[:, icm * PASS_T:(icm + 1) * PASS_T],
                            hm[m], sq[m], OP.mult)
                        if m == 3:
                            inst.then_inc(S["f_gpa"].h, 1)
                    tok[("gpa", p, qd)] = S["f_gpa"].inc()
                    ve.drain()

        if l0:
            @E.block.tensor
            def _(te, p=p):
                for j in range(IC):
                    ic = ic_order[j]
                    te.wait_ge(S["f_dvec"].h, tok[("dvec", p, j)])
                    inst = te.matmul(psh, ones_col[:],
                                     hsl[:, ic * PASS_T:(ic + 1) * PASS_T],
                                     start=(j == 0), stop=(j == IC - 1))
                    if j == IC - 1:
                        inst.then_inc(S["f_pesh"].h, 1)
                tok[("pesh", p)] = S["f_pesh"].inc()

            @E.block.vector
            def _(ve, p=p):
                ve.wait_ge(S["f_pesh"].h, tok[("pesh", p)])
                ve.tensor_scalar(shs[:, :], psh, -128.0, None, OP.mult)\
                    .then_inc(S["f_dvesh"].h, 1)
                tok[("dvesh", p)] = S["f_dvesh"].inc()

        @E.block.tensor
        def _(te, p=p, base_dved=base_dved):
            te.wait_ge(S["f_dvec"].h, tok[("dvec", p, IC - 1)])
            te.wait_ge(S["f_actd"].h, tok[("actd", p, IC - 1)])
            if not l0:
                te.wait_ge(S["f_gpa"].h, tok[("gpa", p, 15)])
            for oh in range(2):
                if oh == 1:
                    te.wait_ge(S["f_dved"].h, base_dved + 4)
                for ici in range(IC):
                    v = tok[("ldd", p, oh, ici)]
                    te.wait_ge(s_ldd[(oh * IC + ici) % 2].h, v)
                    slot = (oh * IC + ici) % 2
                    for tt in range(4):
                        if l0 and oh == 0 and ici == 0 and tt == 2:
                            te.wait_ge(S["f_dvesh"].h, tok[("dvesh", p)])
                        lt = hsl[:, ici * PASS_T + tt * 128:ici * PASS_T + (tt + 1) * 128]
                        for hf in range(2):
                            inst = te.matmul(
                                pdn[tt][:, hf * 512:(hf + 1) * 512], lt,
                                wds[:, slot * 1024 + hf * 512:slot * 1024 + (hf + 1) * 512],
                                start=(ici == 0),
                                stop=(ici == IC - 1 and not l0))
                            if tt == 3 and hf == 1:
                                inst.then_inc(S["f_pednc"].h, 1)
                    S["f_pednc"].inc()
                if l0:
                    te.wait_ge(S["f_dvesh"].h, tok[("dvesh", p)])
                    for tt in range(4):
                        for hf in range(2):
                            inst = te.matmul(
                                pdn[tt][:, hf * 512:(hf + 1) * 512],
                                shs[0:1, tt * 128:(tt + 1) * 128],
                                ones_row[0:1, 0:512],
                                start=False, stop=True)
                            if tt == 3 and hf == 1:
                                inst.then_inc(S["f_pedno"].h, 1)
                    tok[("pedno", p, oh)] = S["f_pedno"].inc()
                else:
                    tok[("pedno", p, oh)] = None
            tok[("pednc", p)] = S["f_pednc"].n

        @E.block.vector
        def _(ve, p=p, base_pednc=base_pednc):
            for oh in range(2):
                if l0:
                    ve.wait_ge(S["f_pedno"].h, tok[("pedno", p, oh)])
                else:
                    ve.wait_ge(S["f_pednc"].h, base_pednc + IC * (oh + 1))
                for tt in range(4):
                    sl = tt % 2
                    bs = base_stx[(sl, oh)]
                    if tt >= 2:
                        wge(ve, s_stx[(sl, oh)], bs + 16)
                    else:
                        wge(ve, s_stx[(sl, oh)], bs)
                    xa = xbuf[:, tt * D + oh * 1024:tt * D + (oh + 1) * 1024]
                    oa = xout[:, sl * D + oh * 1024:sl * D + (oh + 1) * 1024]
                    pa = pdn[tt][:, 0:1024]
                    if l0:
                        ve.tensor_tensor(dn_tmp[:], pa,
                                         sbc[:, oh * 1024:(oh + 1) * 1024], OP.mult)
                        ve.drain()
                        ve.tensor_tensor(oa, dn_tmp[:], xa, OP.add)\
                            .then_inc(S["f_dved"].h, 1)
                        ve.drain()
                    else:
                        ve.scalar_tensor_tensor(oa, pa, 0.5, xa, OP.mult, OP.add)\
                            .then_inc(S["f_dved"].h, 1)
                    tok[("dved", p, tt, oh)] = S["f_dved"].inc()
                    if oh == 1:
                        E.xslot_guard[tt] = [(S["f_dved"], tok[("dved", p, tt, 1)])]

        @E.block.gpsimd
        def _(sy, p=p):
            for oh in range(2):
                for tt in range(4):
                    ti = p * 4 + tt
                    sl = tt % 2
                    sy.wait_ge(S["f_dved"].h, tok[("dved", p, tt, oh)])
                    sy.dma_start(
                        out=x_dst[ti * 128:(ti + 1) * 128,
                                  oh * 1024:(oh + 1) * 1024],
                        in_=xout[:, sl * D + oh * 1024:sl * D + (oh + 1) * 1024])\
                        .then_inc(s_stx[(sl, oh)].h, 16)
                    tok[("stx", p, tt, oh)] = s_stx[(sl, oh)].inc(16)
                    if oh == 1:
                        x_ready_out[ti] = [
                            (s_stx[(sl, 0)], tok[("stx", p, tt, 0)]),
                            (s_stx[(sl, 1)], tok[("stx", p, tt, 1)])]

    E.x_ready = x_ready_out


def emit_final_wait(E):
    @E.block.gpsimd
    def _(gp):
        for name, s in E.sems.items():
            if name.startswith("f_stx"):
                gp.wait_ge(s.h, s.n)




"""Attention sublayer emitter for mk_macaron_d2: x_dst = x_src + Attn(LN(x)).

Working buffers alias the FFN buffers (bitcast views); entry_guard serializes
against the preceding sublayer, and E.attn_exit_guard (set here) must be
waited by the next sublayer's first sections.

Arena map (per-partition):
  phase A:  aw=hsl(64K) | qtm = qsq+hq (2x8K f32) | kvst = wds+sg (2x4K bf16)
            kfst+xnf = xout (4x4K bf16) | rope scratch = qt2 (8K f32)
            trig = xn (16K f32) | LN: xbuf/xsq/st8/xnst as usual
  masks  :  xsq(8 tiles) + qt2(8) + dn_tmp(4+dil128)  (computed after A)
  phase C:  S=hsl[0:16K]f32 | P=hsl[16K:24K]bf16 | pts=hsl[24K:32K]bf16
            K_h=hsl[32K:48K]x2 | ctxh=hsl[48K:56K]x2 | ctx_sc=hsl[56K:56.25K]
            V_h = xn x2 (bf16 [128,4096]) | Q_h = wgs x2 | mx/stats = st8
  phase Dp: Wo=hsl | ctx tiles = wgs x2 | xbuf/xout/stores as FFN

PSUM:  A: proj b0-3 | ptrx b4-5 | ptrk b6-7
       C: S slots b0-2 | ptrp b4-5 | pctx b6 | pctxT b7
       Dp: b0-3 / b4-7 alternating tiles
"""

SC128 = float(1.0 / np.sqrt(128.0))
TWO_PI = float(2 * np.pi)
NF = {0: 32, 1: 64}


def attn_chunks(layer, jt):
    j, r = jt // 2, jt % 2
    chunks = []
    if layer == 0:
        for c in range(2):
            base = c * 2048
            full = j * 256
            off = 0
            while off < full:
                w = min(512, full - off)
                chunks.append((base + off, w, None,
                               [c * 16 + off // 128 + k for k in range(w // 128)]))
                off += w
            chunks.append((base + j * 256, 256, ("l0", c, r),
                           [c * 16 + 2 * j, c * 16 + 2 * j + 1]))
    else:
        for pos in range(0, max(0, j - 2)):
            for c in range(2):
                mask = None if (pos == 0 and c == 0) else "dil128"
                chunks.append((c * 2048 + pos * 256, 128, mask,
                               [c * 16 + 2 * pos]))
        for e in range(min(2, j), -1, -1):
            pos = j - e
            for c in range(2):
                if pos == 0 and c == 0:
                    mask = ("l1g", e, r)
                else:
                    mask = ("l1", e, c, r)
                chunks.append((c * 2048 + pos * 256, 256, mask,
                               [c * 16 + 2 * pos, c * 16 + 2 * pos + 1]))
    ext = sum(w for (_, w, _, _) in chunks)
    assert ext % 128 == 0 and ext <= 4096
    return ext, chunks


def _entry(eng, guard):
    for sem, v in guard:
        wge(eng, sem, v)


def emit_attn(E, layer, x_src, x_dst, w_all, q_fm, kv_in, kv_out, ctx_buf,
              pofs, qpos_c, invf_b, entry_guard, replica_groups, upto="full"):
    nc = E.nc
    B = ffn_buffers(E)
    xbuf, xout, xsq, st8, xnst = B["xbuf"], B["xout"], B["xsq"], B["st8"], B["xnst"]
    pall = B["pall"]
    idn = E.sbufs["idn"]
    epsb = E.sbufs["epsb"]
    nf = NF[layer]

    hslb = B["hsl"][:]
    aw = hslb[:, 0:32768]
    S_slabs = [hslb.bitcast(F32)[:, 0:4096], hslb.bitcast(F32)[:, 4096:8192]]
    P_slab = hslb[:, 16384:20480]
    pts = hslb[:, 20480:24576]
    K_h = [hslb[:, 24576:28672]] * 2
    ctxh = [hslb[:, 28672:30720]] * 2
    ctx_sc = hslb[:, 30720:30848]
    xnb = B["xn"][:]
    trig = xnb.bitcast(F32)
    cosk, sink = trig[:, 0:1024], trig[:, 1024:2048]
    cosq, sinq = trig[:, 2048:3072], trig[:, 3072:4096]
    V_h = [xnb[:, s * 4096:(s + 1) * 4096] for s in range(2)]
    wgsb = B["wgs"][:]
    Q_h = [wgsb[:, s * 2048:(s + 1) * 2048] for s in range(2)]
    ctxt = [wgsb[:, s * 2048:(s + 1) * 2048] for s in range(2)]
    qtm = [B["qsq"][:], B["hq"][:]]
    kvst = [B["wds"][:], B["sg"][:].bitcast(BF16)]
    xob = xout[:].bitcast(BF16)
    kfst = [xob[:, 0:2048], xob[:, 2048:4096]]
    xnf = [xob[:, 4096:6144], xob[:, 6144:8192]]
    ropet = B["qt2"][:]
    dn_tmpf = B["dn_tmp"][:]
    qt2f = B["qt2"][:]
    mask_store = ([xsq[:, i * 256:(i + 1) * 256] for i in range(8)]
                  + [qt2f[:, i * 256:(i + 1) * 256] for i in range(8)]
                  + [dn_tmpf[:, i * 256:(i + 1) * 256] for i in range(3)])
    dil128 = dn_tmpf[:, 768:896]

    pproj = pall[:, 0:2048]
    ptrx = pall[:, 2048:3072].bitcast(BF16)
    ptrk = pall[:, 3072:4096].bitcast(BF16)
    pS = [pall[:, i * 512:(i + 1) * 512] for i in range(3)]
    ptrp = pall[:, 2048:3072].bitcast(BF16)
    pctx = pall[:, 3072:3200]
    pctxT = pall[:, 3584:4096].bitcast(BF16)[:, 0:128]
    pD = [pall[:, 0:2048], pall[:, 2048:4096]]

    A = {n: E.sem(n) for n in [
        "a_lw", "a_petr", "a_actc", "a_pep", "a_actp", "a_dver", "a_petk", "a_sin",
        "a_dvek", "a_cc", "a_prep", "a_msk",
        "a_pes", "a_acts", "a_gpm", "a_dvemx", "a_acte", "a_pept", "a_dvept",
        "a_peav", "a_dvecx", "a_pecx", "a_dvecs", "a_peo", "a_dveo",
        "c_ldp", "c_const"]}
    LN = {n: E.sem(n) for n in ["f_acta", "f_actb", "f_dvea", "f_dveb"]}
    s_ldx = [E.sem(f"f_ldx{i}") for i in range(4)]
    s_stx = {(sl, oh): E.sem(f"f_stx{sl}{oh}")
             for sl in range(2) for oh in range(2)}
    s_stk = [E.sem(f"a_stk{i}") for i in range(2)]
    s_stv = [E.sem(f"a_stv{i}") for i in range(2)]
    s_lkh = [E.sem(f"a_lkh{i}") for i in range(2)]
    s_lvh = {(i, r): E.sem(f"a_lvh{i}{r}") for i in range(2) for r in range(2)}
    s_lqh = [E.sem(f"a_lqh{i}") for i in range(2)]
    s_sct = [E.sem(f"a_sct{i}") for i in range(2)]
    s_lct = [E.sem(f"a_lct{i}") for i in range(2)]

    tok = {}
    lntok = {}
    if not hasattr(E, "ln_guard"):
        E.ln_guard = {}
    if not hasattr(E, "xslot_guard"):
        E.xslot_guard = {}

    @E.block.vector
    def _(ve):
        _entry(ve, entry_guard)
        ve.wait_ge(A["c_ldp"].h, A["c_ldp"].n)
        ang = ropet[:, 0:1024]
        t1 = ropet[:, 1024:2048]
        if nf < 64:
            ve.memset(ang, 0.0)
            ve.drain()
        for tt in range(16):
            ve.tensor_scalar(ang[:, tt * 64:tt * 64 + nf], invf_b[:, 0:nf],
                             qpos_c[:, tt:tt + 1], None, OP.mult)
        ve.drain()
        for is_cos in (0, 1):
            if is_cos:
                ve.tensor_scalar(ang[:], ang[:], float(np.pi / 2), None, OP.add)
                ve.drain()
            ve.tensor_scalar(t1[:], ang[:], float(1.0 / TWO_PI), None, OP.mult)
            ve.drain()
            it32 = xsq[:].bitcast(mybir.dt.int32)[:, 0:1024]
            ve.tensor_copy(it32, t1[:])
            ve.drain()
            ve.tensor_copy(t1[:], it32)
            ve.drain()
            dst = (cosk if is_cos else sink)
            ve.scalar_tensor_tensor(dst, t1[:], -TWO_PI, ang[:], OP.mult, OP.add)
            ve.drain()
            ve.tensor_scalar(t1[:], dst, float(np.pi), None, OP.is_gt)
            ve.drain()
            ve.scalar_tensor_tensor(dst, t1[:], -TWO_PI, dst, OP.mult, OP.add)\
                .then_inc(A["a_prep"].h, 1)
            tok[("red", is_cos)] = A["a_prep"].inc()
            ve.drain()

    @E.block.scalar
    def _(sc):
        _entry(sc, entry_guard)
        for is_cos in (0, 1):
            sc.wait_ge(A["a_prep"].h, tok[("red", is_cos)])
            dst = (cosk if is_cos else sink)
            sc.activation(dst, dst, AF.Sin).then_inc(A["a_sin"].h, 1)
            A["a_sin"].inc()
        sc.drain()

    @E.block.vector
    def _(ve):
        ve.wait_ge(A["a_sin"].h, A["a_sin"].n)
        ve.tensor_scalar(cosq, cosk, SC128, None, OP.mult)
        ve.tensor_scalar(sinq, sink, SC128, None, OP.mult)\
            .then_inc(A["a_prep"].h, 1)
        A["a_prep"].inc()
    prep_tok = A["a_prep"].n

    def a_phase(mi, wrow, needs_rope, qscale):
        key = ("A", layer, mi)
        b_pep = A["a_pep"].n
        b_actp = A["a_actp"].n
        b_petr = A["a_petr"].n
        b_actc = A["a_actc"].n
        b_dver = A["a_dver"].n
        b_petk = A["a_petk"].n
        b_dvek = A["a_dvek"].n
        b_stk = [s.n for s in s_stk]
        b_stv = [s.n for s in s_stv]

        @E.block.gpsimd
        def _(gp):
            if mi == 0:
                _entry(gp, entry_guard)
            if b_pep > 0:
                gp.wait_ge(A["a_pep"].h, b_pep)
            gp.dma_start(out=aw.rearrange("p (c o) -> p c o", c=DC),
                         in_=w_all[wrow * 2048:(wrow + 1) * 2048, :]
                         .rearrange("(c p) o -> p c o", p=128))\
                .then_inc(A["a_lw"].h, 16)
            tok[(key, "lw")] = A["a_lw"].inc(16)

        for g in range(4):
            @E.block.gpsimd
            def _(gq, g=g):
                for tt in range(4):
                    ti = g * 4 + tt
                    for sem, v in E.xslot_guard.get(tt, []):
                        wge(gq, sem, v)
                    if E.x_ready is not None:
                        for sem, v in E.x_ready[ti]:
                            gq.wait_ge(sem.h, v)
                    gq.dma_start(out=xbuf[:, tt * D:(tt + 1) * D],
                                 in_=x_src[ti * 128:(ti + 1) * 128, :])\
                        .then_inc(s_ldx[tt].h, 16)
                    lntok[("ldx", key, g, tt)] = s_ldx[tt].inc(16)

            @E.block.scalar
            def _(sc, g=g):
                if mi == 0 and g == 0:
                    sc.wait_ge(A["a_prep"].h, tok[("red", 1)])
                for tt in range(4):
                    sc.wait_ge(s_ldx[tt].h, lntok[("ldx", key, g, tt)])
                    prev = E.ln_guard.get(("dvea", tt))
                    if prev:
                        sc.wait_ge(prev[0].h, prev[1])
                    sc.activation(xsq[:], xbuf[:, tt * D:(tt + 1) * D],
                                  AF.Square, accum_out=st8[:, tt:tt + 1])\
                        .then_inc(LN["f_acta"].h, 1)
                    lntok[("acta", key, g, tt)] = LN["f_acta"].inc()
                    sc.drain()

            @E.block.vector
            def _(ve, g=g):
                for tt in range(4):
                    ve.wait_ge(s_ldx[tt].h, lntok[("ldx", key, g, tt)])
                    prev = E.ln_guard.get(("actb", tt))
                    if prev:
                        ve.wait_ge(prev[0].h, prev[1])
                    ve.reduce_sum(st8[:, 4 + tt:5 + tt],
                                  xbuf[:, tt * D:(tt + 1) * D], axis=AX.X)
                ve.drain()
                for tt in range(4):
                    ve.tensor_scalar(st8[:, 8 + tt:9 + tt], st8[:, 4 + tt:5 + tt],
                                     1.0 / D, None, OP.mult)
                for tt in range(4):
                    ve.wait_ge(LN["f_acta"].h, lntok[("acta", key, g, tt)])
                    ve.tensor_scalar(st8[:, 20 + tt:21 + tt], st8[:, tt:tt + 1],
                                     1.0 / D, None, OP.mult)
                ve.drain()
                for tt in range(4):
                    ve.tensor_tensor(st8[:, 24 + tt:25 + tt],
                                     st8[:, 8 + tt:9 + tt],
                                     st8[:, 8 + tt:9 + tt], OP.mult)
                ve.drain()
                for tt in range(4):
                    ve.scalar_tensor_tensor(st8[:, 12 + tt:13 + tt],
                                            st8[:, 24 + tt:25 + tt], -1.0,
                                            st8[:, 20 + tt:21 + tt],
                                            OP.mult, OP.add)\
                        .then_inc(LN["f_dvea"].h, 1)
                    lntok[("dvea", key, g, tt)] = LN["f_dvea"].inc()
                    E.ln_guard[("dvea", tt)] = (LN["f_dvea"],
                                                lntok[("dvea", key, g, tt)])

            @E.block.scalar
            def _(sc, g=g):
                if mi == 0 and g == 0:
                    sc.wait_ge(A["c_const"].h, 1)
                for tt in range(4):
                    sc.wait_ge(LN["f_dvea"].h, lntok[("dvea", key, g, tt)])
                    prev = E.ln_guard.get(("dveb", tt))
                    if prev:
                        sc.wait_ge(prev[0].h, prev[1])
                    sc.activation(st8[:, 28 + tt:29 + tt],
                                  st8[:, 12 + tt:13 + tt], AF.Ln,
                                  bias=epsb[:, 0:1])
                sc.drain()
                for tt in range(4):
                    sc.activation(st8[:, 16 + tt:17 + tt],
                                  st8[:, 28 + tt:29 + tt], AF.Exp, scale=-0.5)\
                        .then_inc(LN["f_actb"].h, 1)
                    lntok[("actb", key, g, tt)] = LN["f_actb"].inc()
                    E.ln_guard[("actb", tt)] = (LN["f_actb"],
                                                lntok[("actb", key, g, tt)])

            @E.block.vector
            def _(ve, g=g, b_petr=b_petr):
                for tt in range(4):
                    i = g * 4 + tt
                    ve.wait_ge(LN["f_actb"].h, lntok[("actb", key, g, tt)])
                    wge(ve, A["a_petr"], b_petr + i - 1)
                    ve.tensor_scalar(xnst[:, (i % 2) * D:(i % 2 + 1) * D],
                                     xbuf[:, tt * D:(tt + 1) * D],
                                     st8[:, 8 + tt:9 + tt],
                                     st8[:, 16 + tt:17 + tt],
                                     OP.subtract, OP.mult)\
                        .then_inc(LN["f_dveb"].h, 1)
                    lntok[("dveb", key, g, tt)] = LN["f_dveb"].inc()
                    E.ln_guard[("dveb", tt)] = (LN["f_dveb"],
                                                lntok[("dveb", key, g, tt)])
                    E.xslot_guard[tt] = [(LN["f_dveb"],
                                          lntok[("dveb", key, g, tt)])]

            def _trk(te, i):
                te.wait_ge(A["a_dver"].h, b_dver + i + 1)
                wge(te, A["a_dvek"], b_dvek + i)
                for hc in range(DC):
                    inst = te.transpose(
                        ptrk[:, hc * 128:(hc + 1) * 128],
                        kvst[i % 2][:, hc * 128:(hc + 1) * 128], idn[:])
                    if hc == DC - 1:
                        inst.then_inc(A["a_petk"].h, 1)
                assert A["a_petk"].inc() == b_petk + i + 1

            @E.block.tensor
            def _(te, g=g):
                if mi == 0 and g == 0:
                    _entry(te, entry_guard)
                    te.wait_ge(A["c_const"].h, 1)
                for tt in range(4):
                    i = g * 4 + tt
                    te.wait_ge(LN["f_dveb"].h, lntok[("dveb", key, g, tt)])
                    wge(te, A["a_actc"], b_actc + i)
                    for dc in range(DC):
                        inst = te.transpose(
                            ptrx[:, dc * 128:(dc + 1) * 128],
                            xnst[:, (i % 2) * D + dc * 128:
                                 (i % 2) * D + (dc + 1) * 128], idn[:])
                        if dc == DC - 1:
                            inst.then_inc(A["a_petr"].h, 1)
                    assert A["a_petr"].inc() == b_petr + i + 1
                    te.wait_ge(A["a_actc"].h, b_actc + i + 1)
                    if i == 0:
                        te.wait_ge(A["a_lw"].h, tok[(key, "lw")])
                    wge(te, A["a_actp"], b_actp + i)
                    xf = xnf[i % 2]
                    for dc in range(DC):
                        for nn in range(4):
                            inst = te.matmul(
                                pproj[:, nn * 512:(nn + 1) * 512],
                                xf[:, dc * 128:(dc + 1) * 128],
                                aw[:, dc * 2048 + nn * 512:dc * 2048 + (nn + 1) * 512],
                                start=(dc == 0), stop=(dc == DC - 1))
                            if dc == DC - 1 and nn == 3:
                                inst.then_inc(A["a_pep"].h, 1)
                    assert A["a_pep"].inc() == b_pep + i + 1
                    if needs_rope and tt >= 1:
                        _trk(te, i - 1)
                if needs_rope:
                    _trk(te, g * 4 + 3)

            @E.block.scalar
            def _(sc, g=g):
                for tt in range(4):
                    i = g * 4 + tt
                    sl = i % 2
                    sc.wait_ge(A["a_petr"].h, b_petr + i + 1)
                    wge(sc, A["a_pep"], b_pep + i - 1)
                    sc.copy(out=xnf[sl], in_=ptrx[:])\
                        .then_inc(A["a_actc"].h, 1)
                    assert A["a_actc"].inc() == b_actc + i + 1
                    sc.drain()
                    sc.wait_ge(A["a_pep"].h, b_pep + i + 1)
                    if needs_rope:
                        wge(sc, A["a_dver"], b_dver + i - 1)
                        sc.copy(out=qtm[sl], in_=pproj[:])\
                            .then_inc(A["a_actp"].h, 1)
                    else:
                        wge(sc, s_stv[sl], b_stv[sl] + (i // 2) * 16)
                        sc.copy(out=kvst[sl], in_=pproj[:])\
                            .then_inc(A["a_actp"].h, 1)
                    assert A["a_actp"].inc() == b_actp + i + 1
                    sc.drain()

            if needs_rope:
                def _evk(ve, i):
                    sl = i % 2
                    ve.wait_ge(A["a_petk"].h, b_petk + i + 1)
                    wge(ve, s_stk[sl], b_stk[sl] + (i // 2) * 16)
                    ve.tensor_copy(kfst[sl], ptrk[:])\
                        .then_inc(A["a_dvek"].h, 1)
                    assert A["a_dvek"].inc() == b_dvek + i + 1
                    ve.drain()

                @E.block.vector
                def _(ve, g=g):
                    ct, st = (cosq, sinq) if qscale else (cosk, sink)
                    for tt in range(4):
                        i = g * 4 + tt
                        sl = i % 2
                        ve.wait_ge(A["a_actp"].h, b_actp + i + 1)
                        if mi == 0 and g == 0 and tt == 0:
                            ve.wait_ge(A["a_prep"].h, prep_tok)
                        wge(ve, A["a_petk"], b_petk + i - 1)
                        src, dst = qtm[sl], kvst[sl]
                        if layer == 0:
                            t1 = ropet[:, 0:512].rearrange("p (h j) -> p h j", h=H)
                            t2 = ropet[:, 512:1024].rearrange("p (h j) -> p h j", h=H)
                            sview = src.rearrange("p (h j two) -> p h two j",
                                                  h=H, two=2)
                            dview = dst.rearrange("p (h j two) -> p h two j",
                                                  h=H, two=2)
                            qev = sview[:, :, 0, 0:32]
                            qov = sview[:, :, 1, 0:32]
                            dev = dview[:, :, 0, 0:32]
                            dov = dview[:, :, 1, 0:32]
                            cv = ct[:, i * 64:i * 64 + 32].unsqueeze(1)\
                                .broadcast_to((128, H, 32))
                            sv = st[:, i * 64:i * 64 + 32].unsqueeze(1)\
                                .broadcast_to((128, H, 32))
                            ve.tensor_tensor(t1, qev, cv, OP.mult)
                            ve.tensor_tensor(t2, qov, sv, OP.mult)
                            ve.drain()
                            ve.tensor_tensor(dev, t1, t2, OP.subtract)
                            ve.drain()
                            ve.tensor_tensor(t1, qov, cv, OP.mult)
                            ve.tensor_tensor(t2, qev, sv, OP.mult)
                            ve.drain()
                            ve.tensor_tensor(dov, t1, t2, OP.add)
                            sr = src.rearrange("p (h dd) -> p h dd", h=H)[:, :, 64:128]
                            dr = dst.rearrange("p (h dd) -> p h dd", h=H)[:, :, 64:128]
                            inst = ve.tensor_scalar(
                                dr, sr, SC128 if qscale else 1.0, None, OP.mult)
                        else:
                            t1 = ropet[:, 0:1024].rearrange("p (h j) -> p h j", h=H)
                            t2 = ropet[:, 1024:2048].rearrange("p (h j) -> p h j", h=H)
                            sview = src.rearrange("p (h dd) -> p h dd", h=H)
                            dview = dst.rearrange("p (h dd) -> p h dd", h=H)
                            q1, q2 = sview[:, :, 0:64], sview[:, :, 64:128]
                            d1, d2 = dview[:, :, 0:64], dview[:, :, 64:128]
                            cv = ct[:, i * 64:(i + 1) * 64].unsqueeze(1)\
                                .broadcast_to((128, H, 64))
                            sv = st[:, i * 64:(i + 1) * 64].unsqueeze(1)\
                                .broadcast_to((128, H, 64))
                            ve.tensor_tensor(t1, q1, cv, OP.mult)
                            ve.tensor_tensor(t2, q2, sv, OP.mult)
                            ve.drain()
                            ve.tensor_tensor(d1, t1, t2, OP.subtract)
                            ve.drain()
                            ve.tensor_tensor(t1, q2, cv, OP.mult)
                            ve.tensor_tensor(t2, q1, sv, OP.mult)
                            ve.drain()
                            inst = ve.tensor_tensor(d2, t1, t2, OP.add)
                        inst.then_inc(A["a_dver"].h, 1)
                        assert A["a_dver"].inc() == b_dver + i + 1
                        ve.drain()
                        if tt >= 1:
                            _evk(ve, i - 1)
                    _evk(ve, g * 4 + 3)

            @E.block.gpsimd
            def _(gp, g=g):
                for tt in range(4):
                    i = g * 4 + tt
                    sl = i % 2
                    if needs_rope:
                        gp.wait_ge(A["a_dvek"].h, b_dvek + i + 1)
                        dstap = (kv_in if mi == 0 else q_fm)
                        gp.dma_start(
                            out=dstap.rearrange("(c p) t -> p c t", p=128)
                            [:, 0:DC, i * 128:(i + 1) * 128],
                            in_=kfst[sl].rearrange("p (c t) -> p c t", c=DC))\
                            .then_inc(s_stk[sl].h, 16)
                        assert s_stk[sl].inc(16) == b_stk[sl] + (i // 2 + 1) * 16
                        tok[(key, "st", i)] = s_stk[sl].n
                    else:
                        gp.wait_ge(A["a_actp"].h, b_actp + i + 1)
                        gp.dma_start(
                            out=kv_in[2048 + i * 128:2048 + (i + 1) * 128, :],
                            in_=kvst[sl]).then_inc(s_stv[sl].h, 16)
                        assert s_stv[sl].inc(16) == b_stv[sl] + (i // 2 + 1) * 16
                        tok[(key, "st", i)] = s_stv[sl].n
        return key

    kkey = a_phase(0, 1, True, False)
    vkey = a_phase(1, 2, False, False)

    @E.block.gpsimd
    def _(gp):
        for sl in range(2):
            gp.wait_ge(s_stk[sl].h, s_stk[sl].n)
            gp.wait_ge(s_stv[sl].h, s_stv[sl].n)
        gp.collective_compute(
            "AllGather", OP.bypass, replica_groups=replica_groups,
            ins=[kv_in.opt()], outs=[kv_out.opt()]).then_inc(A["a_cc"].h, 1)
        tok[("cc",)] = A["a_cc"].inc()

    qkey = a_phase(2, 0, True, True)
    if upto == "A":
        return

    mask_keys = ([("l0", c, r) for c in range(2) for r in range(2)]
                 if layer == 0 else
                 [("l1", e, c, r) for e in range(3) for c in range(2)
                  for r in range(2)]
                 + [("l1g", e, r) for e in range(3) for r in range(2)])
    masks = {k: mask_store[i] for i, k in enumerate(mask_keys)}
    if layer == 1:
        masks["dil128"] = dil128
    E.attn_masks = masks
    mscr = B["wgs"][:].bitcast(F32)
    iot = mscr[:, 0:256]
    b01 = mscr[:, 256:257]
    gl = mscr[:, 257:513]
    dil = mscr[:, 513:769]
    pp = mscr[:, 769:770]
    t0 = mscr[:, 1024:1280]
    ok = mscr[:, 1280:1536]
    ok2 = mscr[:, 1536:1792]

    @E.block.gpsimd
    def _(gp):
        gp.wait_ge(LN["f_acta"].h, LN["f_acta"].n)
        gp.wait_ge(A["a_dvek"].h, A["a_dvek"].n)
        if layer == 0:
            for (key, m) in masks.items():
                _, c, r = key
                gp.iota(m, pattern=[[-1, 256]], base=128 * r - 256 * c,
                        channel_multiplier=1,
                        allow_small_or_imprecise_dtypes=True)
        else:
            gp.iota(iot, pattern=[[-1, 4], [0, 64]], base=0,
                    channel_multiplier=0, allow_small_or_imprecise_dtypes=True)
            gp.iota(b01, pattern=[[0, 1]], base=-64, channel_multiplier=1,
                    allow_small_or_imprecise_dtypes=True)
            gp.iota(gl, pattern=[[-1, 256]], base=127, channel_multiplier=0,
                    allow_small_or_imprecise_dtypes=True)
            gp.memset(dil128[:, 0:64], 0.0)
            gp.memset(dil128[:, 64:128], -1e9)
        gp.drain()
        gp.memset(pp, 0.0).then_inc(A["a_gpm"].h, 1)
        A["a_gpm"].inc()
    mski_tok = A["a_gpm"].n

    @E.block.vector
    def _(ve):
        ve.wait_ge(A["a_gpm"].h, mski_tok)
        ve.wait_ge(A["c_ldp"].h, A["c_ldp"].n)
        if layer == 0:
            ve.tensor_scalar(pp, pofs[:, 0:1], 256.0, None, OP.mult)
            ve.drain()
            for (key, m) in masks.items():
                ve.tensor_scalar(m, m, pp[:, 0:1], None, OP.add)
                ve.drain()
                ve.tensor_scalar(m, m, 0.0, 1.0, OP.is_ge, OP.subtract)
                ve.drain()
                inst = ve.tensor_scalar(m, m, 1e9, None, OP.mult)
                ve.drain()
        else:
            ve.tensor_scalar(pp, pofs[:, 0:1], 4.0, None, OP.mult)
            ve.tensor_scalar(b01, b01, 0.0, None, OP.is_ge)
            ve.tensor_scalar(gl, gl, 0.0, None, OP.is_ge)
            ve.tensor_scalar(dil, iot, 0.0, None, OP.is_ge)
            ve.drain()
            for key, m in masks.items():
                if key == "dil128":
                    continue
                if key[0] == "l1":
                    _, e, c, r = key
                    glob = False
                else:
                    _, e, r = key
                    c = 0
                    glob = True
                C0 = 8 * e - 4 * c + 2 * r
                ve.tensor_scalar(t0, iot, float(C0), None, OP.add)
                ve.drain()
                ve.tensor_scalar(t0, t0, b01[:, 0:1], pp[:, 0:1], OP.add, OP.add)
                ve.drain()
                ve.tensor_scalar(ok, t0, 16.0, None, OP.is_le)
                ve.drain()
                ve.tensor_tensor(ok, ok, dil, OP.max)
                ve.drain()
                if glob:
                    ve.tensor_tensor(ok, ok, gl, OP.max)
                    ve.drain()
                ve.tensor_scalar(ok2, t0, 0.0, None, OP.is_ge)
                ve.drain()
                ve.tensor_tensor(ok, ok, ok2, OP.mult)
                ve.drain()
                ve.tensor_scalar(m, ok, 1.0, 1e9, OP.subtract, OP.mult)
                ve.drain()
        ve.tensor_scalar(pp, pp, 1.0, None, OP.mult).then_inc(A["a_msk"].h, 1)
        A["a_msk"].inc()
    msk_tok = A["a_msk"].n

    ckey = ("C", layer)
    qst_base = [s.n for s in s_stk]
    for h in range(H):
        hp = h % 2
        @E.block.sync
        def _(sy, h=h, hp=hp):
            if h == 0:
                _entry(sy, entry_guard)
                sy.wait_ge(A["a_cc"].h, tok[("cc",)])
                sy.wait_ge(A["a_dver"].h, A["a_dver"].n)
                sy.wait_ge(A["a_msk"].h, msk_tok)
            prevav = tok.get((ckey, "peav", h - 1, 15))
            if prevav:
                sy.wait_ge(A["a_peav"].h, prevav)
            prevsct = tok.get((ckey, "sct", h - 1))
            if prevsct:
                sy.wait_ge(s_sct[(h - 1) % 2].h, prevsct)
            sy.dma_start(
                out=K_h[hp].rearrange("p (r t) -> p r t", r=2),
                in_=kv_out.rearrange("(r v blk p) t -> r v blk p t",
                                     r=2, v=2, p=128)[:, 0, h, :, :]
                .rearrange("r p t -> p r t")).then_inc(s_lkh[hp].h, 16)
            tok[(ckey, "lk", h)] = s_lkh[hp].inc(16)
            for r in range(2):
                sy.dma_start(
                    out=V_h[hp].rearrange("p (kt c) -> p kt c", kt=16)
                    if False else
                    V_h[hp].rearrange("p (kt c) -> p kt c", c=128)
                    [:, r * 16:(r + 1) * 16, :],
                    in_=kv_out.rearrange("(r v blk p) c -> r v blk p c",
                                         r=2, v=2, p=128)[r, 1, :, :,
                                                          h * 128:(h + 1) * 128]
                    .rearrange("blk p c -> p blk c"))\
                    .then_inc(s_lvh[(hp, r)].h, 16)
                tok[(ckey, "lv", h, r)] = s_lvh[(hp, r)].inc(16)
            if h == 0:
                for sl in range(2):
                    sy.wait_ge(s_stk[sl].h, qst_base[sl])
            sy.dma_start(out=Q_h[hp],
                         in_=q_fm[h * 128:(h + 1) * 128, :])\
                .then_inc(s_lqh[hp].h, 16)
            tok[(ckey, "lq", h)] = s_lqh[hp].inc(16)

        for jt in range(16):
            ext, chunks = attn_chunks(layer, jt)
            S_slab = S_slabs[jt % 2]
            nch = len(chunks)
            base_acts = A["a_acts"].n
            base_pes = A["a_pes"].n

            @E.block.tensor
            def _(te, h=h, jt=jt, hp=hp, chunks=chunks, base_acts=base_acts):
                te.wait_ge(s_lkh[hp].h, tok[(ckey, "lk", h)])
                te.wait_ge(s_lqh[hp].h, tok[(ckey, "lq", h)])
                if jt == 0 and h < 2:
                    pass
                off = 0
                for ci, (khcol, w, mk_, vts) in enumerate(chunks):
                    if ci >= 3:
                        wge(te, A["a_acts"], base_acts + ci - 2)
                    else:
                        wge(te, A["a_acts"], base_acts)
                    inst = te.matmul(pS[ci % 3][:, 0:w],
                                     Q_h[hp][:, jt * 128:(jt + 1) * 128],
                                     K_h[hp][:, khcol:khcol + w],
                                     start=True, stop=True)
                    inst.then_inc(A["a_pes"].h, 1)
                    tok[(ckey, "pes", h, jt, ci)] = A["a_pes"].inc()
                    off += w

            @E.block.scalar
            def _(sc, h=h, jt=jt, chunks=chunks):
                off = 0
                for ci, (khcol, w, mk_, vts) in enumerate(chunks):
                    sc.wait_ge(A["a_pes"].h, tok[(ckey, "pes", h, jt, ci)])
                    sc.copy(out=S_slab[:, off:off + w], in_=pS[ci % 3][:, 0:w])\
                        .then_inc(A["a_acts"].h, 1)
                    tok[(ckey, "acts", h, jt, ci)] = A["a_acts"].inc()
                    sc.drain()
                    off += w

            @E.block.gpsimd
            def _(gp, h=h, jt=jt, chunks=chunks):
                if h == 0 and jt == 0:
                    gp.wait_ge(A["a_msk"].h, msk_tok)
                off = 0
                n = 0
                last = None
                for ci, (khcol, w, mk_, vts) in enumerate(chunks):
                    if mk_ is not None:
                        gp.wait_ge(A["a_acts"].h, tok[(ckey, "acts", h, jt, ci)])
                        last = gp.tensor_tensor(
                            S_slab[:, off:off + w], S_slab[:, off:off + w],
                            masks[mk_][:, 0:w], OP.add)
                        gp.drain()
                        n += 1
                    off += w
                if last is not None:
                    last.then_inc(A["a_gpm"].h, 1)
                else:
                    gp.memset(pp, 0.0).then_inc(A["a_gpm"].h, 1)
                tok[(ckey, "gpm", h, jt)] = A["a_gpm"].inc()

            @E.block.vector
            def _(ve, h=h, jt=jt, chunks=chunks, nch=nch, ext=ext):
                ve.wait_ge(A["a_gpm"].h, tok[(ckey, "gpm", h, jt)])
                ve.wait_ge(A["a_acts"].h, tok[(ckey, "acts", h, jt, nch - 1)])
                off = 0
                for ci, (khcol, w, mk_, vts) in enumerate(chunks):
                    ve.reduce_max(st8[:, ci:ci + 1], S_slab[:, off:off + w],
                                  axis=AX.X)
                    off += w
                ve.drain()
                ve.reduce_max(st8[:, 20:21], st8[:, 0:nch], axis=AX.X)
                ve.drain()
                ve.tensor_scalar(st8[:, 21:22], st8[:, 20:21], -1.0, None,
                                 OP.mult).then_inc(A["a_dvemx"].h, 1)
                tok[(ckey, "mx", h, jt)] = A["a_dvemx"].inc()

            @E.block.scalar
            def _(sc, h=h, jt=jt, ext=ext):
                sc.wait_ge(A["a_dvemx"].h, tok[(ckey, "mx", h, jt)])
                prev = tok.get((ckey, "pept1", h, jt - 1)) or \
                    tok.get((ckey, "pept1", h - 1, 15))
                if prev:
                    sc.wait_ge(A["a_pept"].h, prev)
                sc.activation(P_slab[:, 0:ext], S_slab[:, 0:ext], AF.Exp,
                              bias=st8[:, 21:22], accum_out=st8[:, 22:23])\
                    .then_inc(A["a_acte"].h, 1)
                tok[(ckey, "acte", h, jt)] = A["a_acte"].inc()
                sc.drain()

            nk = ext // 128
            nb = (nk + 15) // 16
            for b in range(nb):
                t0_ = b * 16
                t1_ = min(nk, t0_ + 16)
                @E.block.tensor
                def _(te, h=h, jt=jt, b=b, t0_=t0_, t1_=t1_):
                    if b == 0:
                        te.wait_ge(A["a_acte"].h, tok[(ckey, "acte", h, jt)])
                    else:
                        te.wait_ge(A["a_dvept"].h, tok[(ckey, "dvept", h, jt, b - 1)])
                    for k in range(t0_, t1_):
                        inst = te.transpose(
                            ptrp[:, (k - t0_) * 128:(k - t0_ + 1) * 128],
                            P_slab[:, k * 128:(k + 1) * 128], idn[:])
                        if k == t1_ - 1:
                            inst.then_inc(A["a_pept"].h, 1)
                    tok[(ckey, f"pept{b}", h, jt)] = A["a_pept"].inc()
                    if b == nb - 1:
                        tok[(ckey, "pept1", h, jt)] = A["a_pept"].n

                @E.block.vector
                def _(ve, h=h, jt=jt, b=b, t0_=t0_, t1_=t1_):
                    ve.wait_ge(A["a_pept"].h, tok[(ckey, f"pept{b}", h, jt)])
                    ve.tensor_copy(pts[:, t0_ * 128:t1_ * 128],
                                   ptrp[:, 0:(t1_ - t0_) * 128])\
                        .then_inc(A["a_dvept"].h, 1)
                    tok[(ckey, "dvept", h, jt, b)] = A["a_dvept"].inc()
                    ve.drain()

            vtl = [vt for (_, w, _, vts) in chunks for vt in vts]
            @E.block.tensor
            def _(te, h=h, jt=jt, hp=hp, vtl=vtl, nb=nb):
                te.wait_ge(A["a_dvept"].h, tok[(ckey, "dvept", h, jt, nb - 1)])
                for r in range(2):
                    te.wait_ge(s_lvh[(hp, r)].h, tok[(ckey, "lv", h, r)])
                prevcx = tok.get((ckey, "dvecx", h, jt - 1)) or \
                    tok.get((ckey, "dvecx", h - 1, 15))
                if prevcx:
                    te.wait_ge(A["a_dvecx"].h, prevcx)
                for k, vt in enumerate(vtl):
                    inst = te.matmul(
                        pctx[:, 0:128],
                        pts[:, k * 128:(k + 1) * 128],
                        V_h[hp].rearrange("p (kt c) -> p kt c", c=128)[:, vt, :],
                        start=(k == 0), stop=(k == len(vtl) - 1))
                    if k == len(vtl) - 1:
                        inst.then_inc(A["a_peav"].h, 1)
                tok[(ckey, "peav", h, jt)] = A["a_peav"].inc()

            @E.block.vector
            def _(ve, h=h, jt=jt):
                ve.wait_ge(A["a_peav"].h, tok[(ckey, "peav", h, jt)])
                prevcx = tok.get((ckey, "pecx", h, jt - 1)) or \
                    tok.get((ckey, "pecx", h - 1, 15))
                if prevcx:
                    ve.wait_ge(A["a_pecx"].h, prevcx)
                ve.reciprocal(st8[:, 23:24], st8[:, 22:23])
                ve.drain()
                ve.tensor_scalar(ctx_sc, pctx[:, 0:128], st8[:, 23:24], None,
                                 OP.mult).then_inc(A["a_dvecx"].h, 1)
                tok[(ckey, "dvecx", h, jt)] = A["a_dvecx"].inc()
                ve.drain()

            @E.block.tensor
            def _(te, h=h, jt=jt):
                te.wait_ge(A["a_dvecx"].h, tok[(ckey, "dvecx", h, jt)])
                prevcs = tok.get((ckey, "dvecs", h, jt - 1)) or \
                    tok.get((ckey, "dvecs", h - 1, 15))
                if prevcs:
                    te.wait_ge(A["a_dvecs"].h, prevcs)
                te.transpose(pctxT, ctx_sc, idn[:]).then_inc(A["a_pecx"].h, 1)
                tok[(ckey, "pecx", h, jt)] = A["a_pecx"].inc()

            @E.block.vector
            def _(ve, h=h, jt=jt, hp=hp):
                ve.wait_ge(A["a_pecx"].h, tok[(ckey, "pecx", h, jt)])
                if jt == 0:
                    prevst = tok.get((ckey, "sct", h - 1))
                    if prevst:
                        ve.wait_ge(s_sct[(h - 1) % 2].h, prevst)
                ve.tensor_copy(ctxh[hp][:, jt * 128:(jt + 1) * 128], pctxT)\
                    .then_inc(A["a_dvecs"].h, 1)
                tok[(ckey, "dvecs", h, jt)] = A["a_dvecs"].inc()
                ve.drain()

        @E.block.gpsimd
        def _(gp, h=h, hp=hp):
            gp.wait_ge(A["a_dvecs"].h, tok[(ckey, "dvecs", h, 15)])
            gp.dma_start(out=ctx_buf[h * 128:(h + 1) * 128, :], in_=ctxh[hp])\
                .then_inc(s_sct[hp].h, 16)
            tok[(ckey, "sct", h)] = s_sct[hp].inc(16)

    if upto == "C":
        return
    dkey = ("Dp", layer)
    base_peo = A["a_peo"].n
    base_dveo = A["a_dveo"].n
    base_stx = {k: s.n for k, s in s_stx.items()}

    @E.block.gpsimd
    def _(gp):
        gp.wait_ge(A["a_pept"].h, A["a_pept"].n)
        gp.wait_ge(A["a_peav"].h, A["a_peav"].n)
        gp.wait_ge(A["a_acte"].h, A["a_acte"].n)
        for sl in range(2):
            gp.wait_ge(s_sct[sl].h, s_sct[sl].n)
        gp.dma_start(out=aw.rearrange("p (c o) -> p c o", c=DC),
                     in_=w_all[3 * 2048:4 * 2048, :]
                     .rearrange("(c p) o -> p c o", p=128))\
            .then_inc(A["a_lw"].h, 16)
        tok[(dkey, "lw")] = A["a_lw"].inc(16)

    for ti in range(16):
        tt = ti % 4
        sl2 = ti % 2

        @E.block.gpsimd
        def _(gq, ti=ti, tt=tt):
            for sem, v in E.xslot_guard.get(tt, []):
                wge(gq, sem, v)
            if E.x_ready is not None:
                for sem, v in E.x_ready[ti]:
                    gq.wait_ge(sem.h, v)
            gq.dma_start(out=xbuf[:, tt * D:(tt + 1) * D],
                         in_=x_src[ti * 128:(ti + 1) * 128, :])\
                .then_inc(s_ldx[tt].h, 16)
            lntok[("ldx", dkey, ti)] = s_ldx[tt].inc(16)

        @E.block.sync
        def _(sy, ti=ti, sl2=sl2):
            if ti == 0:
                for sl in range(2):
                    sy.wait_ge(s_sct[sl].h, s_sct[sl].n)
            prev = tok.get((dkey, "peo", ti - 2))
            if prev:
                sy.wait_ge(A["a_peo"].h, prev)
            sy.dma_start(out=ctxt[sl2].rearrange("p (c t) -> p c t", c=DC),
                         in_=ctx_buf[:, ti * 128:(ti + 1) * 128]
                         .rearrange("(c p) t -> p c t", p=128))\
                .then_inc(s_lct[sl2].h, 16)
            tok[(dkey, "lct", ti)] = s_lct[sl2].inc(16)

        @E.block.tensor
        def _(te, ti=ti, sl2=sl2, base_dveo=base_dveo):
            te.wait_ge(s_lct[sl2].h, tok[(dkey, "lct", ti)])
            if ti == 0:
                te.wait_ge(A["a_lw"].h, tok[(dkey, "lw")])
            wge(te, A["a_dveo"], base_dveo + (ti - 2) * 2 + 2)
            for dc in range(DC):
                for nn in range(4):
                    inst = te.matmul(
                        pD[ti % 2][:, nn * 512:(nn + 1) * 512],
                        ctxt[sl2][:, dc * 128:(dc + 1) * 128],
                        aw[:, dc * 2048 + nn * 512:dc * 2048 + (nn + 1) * 512],
                        start=(dc == 0), stop=(dc == DC - 1))
                    if dc == DC - 1 and nn == 3:
                        inst.then_inc(A["a_peo"].h, 1)
            tok[(dkey, "peo", ti)] = A["a_peo"].inc()

        @E.block.vector
        def _(ve, ti=ti, tt=tt, sl2=sl2):
            ve.wait_ge(A["a_peo"].h, tok[(dkey, "peo", ti)])
            ve.wait_ge(s_ldx[tt].h, lntok[("ldx", dkey, ti)])
            if ti == 0:
                for sl in range(2):
                    ve.wait_ge(s_stk[sl].h, s_stk[sl].n)
                ve.wait_ge(A["a_pep"].h, A["a_pep"].n)
            for oh in range(2):
                sl = tt % 2
                bs = base_stx[(sl, oh)]
                wge(ve, s_stx[(sl, oh)], bs + (ti // 2) * 16)
                xa = xbuf[:, tt * D + oh * 1024:tt * D + (oh + 1) * 1024]
                oa = xout[:, sl * D + oh * 1024:sl * D + (oh + 1) * 1024]
                ve.tensor_tensor(oa, pD[ti % 2][:, oh * 1024:(oh + 1) * 1024],
                                 xa, OP.add).then_inc(A["a_dveo"].h, 1)
                tok[(dkey, "dveo", ti, oh)] = A["a_dveo"].inc()
            E.xslot_guard[tt] = [(A["a_dveo"], tok[(dkey, "dveo", ti, 1)])]

        @E.block.gpsimd
        def _(gp, ti=ti, tt=tt):
            for oh in range(2):
                sl = tt % 2
                gp.wait_ge(A["a_dveo"].h, tok[(dkey, "dveo", ti, oh)])
                gp.dma_start(
                    out=x_dst[ti * 128:(ti + 1) * 128, oh * 1024:(oh + 1) * 1024],
                    in_=xout[:, sl * D + oh * 1024:sl * D + (oh + 1) * 1024])\
                    .then_inc(s_stx[(sl, oh)].h, 16)
                tok[(dkey, "stx", ti, oh)] = s_stx[(sl, oh)].inc(16)

    E.x_ready = {ti: [(s_stx[(ti % 4 % 2, 0)], tok[(dkey, "stx", ti, 0)]),
                      (s_stx[(ti % 4 % 2, 1)], tok[(dkey, "stx", ti, 1)])]
                 for ti in range(16)}
    E.attn_exit_guard = [
        (A["a_peo"], A["a_peo"].n), (A["a_dveo"], A["a_dveo"].n),
        (A["a_acte"], A["a_acte"].n), (A["a_gpm"], A["a_gpm"].n),
        (A["a_peav"], A["a_peav"].n), (A["a_dvecs"], A["a_dvecs"].n),
        (s_stk[0], s_stk[0].n), (s_stk[1], s_stk[1].n),
        (s_stv[0], s_stv[0].n), (s_stv[1], s_stv[1].n),
        (s_sct[0], s_sct[0].n), (s_sct[1], s_sct[1].n),
        (s_lct[0], s_lct[0].n), (s_lct[1], s_lct[1].n)]


"""Fused megakernel for mk_macaron_d2: one raw-Bass program per core, one
run_bass_kernel_spmd launch for the whole forward (6 macaron sublayers).

Sharding: core = 2*b + p (batch b of 4, parity p); each core owns the
interleaved 256-token units of its sequence:
  local token lt -> global position 512*(lt//256) + 256*p + (lt%256).
Weights are sharded 1/8 per core and AllGather'ed on device; activations
stream through per-core DRAM between sublayers inside the single program.
"""
import os
import sys
from contextlib import ExitStack

import numpy as np
import ml_dtypes
import concourse.bass as bass
import concourse.mybir as mybir
from concourse.bass_utils import run_bass_kernel_spmd


T, D, I_FFN = 2048, 2048, 8192



def _l0_attn_wT(w_qkv, w_out):
    rows_q, rows_k, rows_v = [], [], []
    for h in range(16):
        m, j = h // 4, h % 4
        base = 1536 * m + 128 * j
        rows_q.extend(range(base, base + 128))
        rows_v.extend(range(base + 512, base + 640))
        rows_k.extend(range(base + 1024, base + 1152))
    wq = np.ascontiguousarray(w_qkv[rows_q].T)
    wk = np.ascontiguousarray(w_qkv[rows_k].T)
    wv = np.ascontiguousarray(w_qkv[rows_v].T)
    wo = np.ascontiguousarray(w_out.T)
    return np.concatenate([wq, wk, wv, wo], axis=0).astype(np.float32)


def _l1_attn_wT(wq, wk, wv, wo):
    return np.concatenate([np.ascontiguousarray(w.T) for w in (wq, wk, wv, wo)],
                          axis=0).astype(np.float32)


_PERM = (4 * np.arange(I_FFN // 4)[None, :] + np.arange(4)[:, None]).reshape(-1)


def _host_prep(inputs):
    h = {}
    h["wa"] = np.concatenate([
        _l0_attn_wT(np.asarray(inputs["L0_attn_w_qkv"]),
                    np.asarray(inputs["L0_attn_w_out"])),
        _l1_attn_wT(np.asarray(inputs["L1_attn_w_q"]),
                    np.asarray(inputs["L1_attn_w_k"]),
                    np.asarray(inputs["L1_attn_w_v"]),
                    np.asarray(inputs["L1_attn_w_o"]))], axis=0)
    g0 = np.asarray(inputs["L0_ffn_gate_w"]).astype(np.uint8)
    u0 = np.asarray(inputs["L0_ffn_up_w"]).astype(np.uint8)
    d0 = np.asarray(inputs["L0_ffn_down_w"]).astype(np.uint8)
    h["wq8"] = np.concatenate([np.ascontiguousarray(g0.T),
                               np.ascontiguousarray(u0.T)], axis=0)
    h["wq8d"] = np.ascontiguousarray(d0.T)
    wg1 = np.asarray(inputs["L1_ffn_w_gate"])[:, _PERM]
    wu1 = np.asarray(inputs["L1_ffn_w_up"])[:, _PERM]
    wd1 = np.asarray(inputs["L1_ffn_w_down"])[_PERM, :]
    h["wf1"] = np.concatenate([wg1, wu1], axis=0).astype(np.float32)
    h["wf1d"] = np.ascontiguousarray(wd1).astype(np.float32)
    h["gsc"] = np.ascontiguousarray(
        np.asarray(inputs["L0_ffn_gate_scale"]).reshape(64, 128).T).astype(np.float32)
    h["usc"] = np.ascontiguousarray(
        np.asarray(inputs["L0_ffn_up_scale"]).reshape(64, 128).T).astype(np.float32)
    h["dsc"] = np.ascontiguousarray(np.broadcast_to(
        np.asarray(inputs["L0_ffn_down_scale"]), (128, D))).astype(np.float32)
    iv0 = np.zeros(64, np.float32)
    iv0[:32] = 1.0 / 10000.0 ** (np.arange(0, 64, 2) / 64)
    iv1 = (1.0 / 10000.0 ** (np.arange(0, 128, 2) / 128)).astype(np.float32)
    h["invf0"] = np.broadcast_to(iv0, (128, 64)).copy()
    h["invf1"] = np.broadcast_to(iv1, (128, 64)).copy()
    return h


def _shard_x(x, b, p):
    out = np.empty((T, D), np.float32)
    for u in range(8):
        out[u * 256:(u + 1) * 256] = x[b, 512 * u + 256 * p:512 * u + 256 * p + 256]
    return np.ascontiguousarray(out)


def _unshard(res):
    y = np.empty((4, 4096, D), np.float32)
    for c in range(8):
        b, p = c // 2, c % 2
        o = res[c]
        for u in range(8):
            y[b, 512 * u + 256 * p:512 * u + 256 * p + 256] = \
                o[u * 256:(u + 1) * 256]
    return y



def build_program(ncores=8):
    nc = bass.Bass("TRN2", target_bir_lowering=False, debug=False)
    WORLD = [list(range(ncores))]
    RG = [[i, i + 1] for i in range(0, ncores, 2)]
    ap = {}
    ein = {
        "xin": ([T, D], F32),
        "wa_sh": ([16384 // ncores, 2048], F32),
        "wq8_sh": ([4096 // ncores, 8192], U8),
        "wq8d_sh": ([8192 // ncores, 2048], U8),
        "wf1_sh": ([4096 // ncores, 8192], F32),
        "wf1d_sh": ([8192 // ncores, 2048], F32),
        "gsc": ([128, 64], F32),
        "usc": ([128, 64], F32),
        "dsc": ([128, D], F32),
        "pofs_i": ([128, 1], F32),
        "qpos_i": ([128, 16], F32),
        "invf0_i": ([128, 64], F32),
        "invf1_i": ([128, 64], F32),
    }
    for n, (shape, dt) in ein.items():
        ap[n] = nc.dram_tensor(n, shape, dt, kind="ExternalInput").ap()
    ap["xout"] = nc.dram_tensor("xout", [T, D], F32, kind="ExternalOutput").ap()
    intr = {
        "wa_b": ([16384 // ncores, 2048], F32), "wa": ([16384, 2048], F32),
        "wq8_b": ([4096 // ncores, 8192], U8), "wq8": ([4096, 8192], U8),
        "wq8d_b": ([8192 // ncores, 2048], U8), "wq8d": ([8192, 2048], U8),
        "wf1_b": ([4096 // ncores, 8192], F32), "wf1": ([4096, 8192], F32),
        "wf1d_b": ([8192 // ncores, 2048], F32), "wf1d": ([8192, 2048], F32),
        "l0gu": ([4096, 8192], BF16), "l0d": ([8192, 2048], BF16),
        "l1gu": ([4096, 8192], BF16), "l1d": ([8192, 2048], BF16),
        "xs1": ([T, D], F32), "xs2": ([T, D], F32),
        "q_fm": ([2048, 2048], BF16), "kv_in": ([4096, 2048], BF16),
        "kv_out": ([8192, 2048], BF16), "ctx_buf": ([2048, 2048], BF16),
    }
    for n, (shape, dt) in intr.items():
        ap[n] = nc.dram_tensor(n, shape, dt).ap()

    with ExitStack() as ctx:
        with nc.Block() as block:
            E = Env(nc, block, ctx)
            emit_const_prologue(E)
            pofs = E.sbuf("p_pofs", [128, 1], F32)
            qpos_c = E.sbuf("p_qpos", [128, 16], F32)
            invf = [E.sbuf("p_invf0", [128, 64], F32),
                    E.sbuf("p_invf1", [128, 64], F32)]
            gs = E.sbuf("l0_gs", [128, 64], F32)
            us = E.sbuf("l0_us", [128, 64], F32)
            sbc = E.sbuf("l0_sbc", [128, D], F32)
            ldp = E.sem("c_ldp")
            sbcs = E.sem("c_sbc")

            @E.block.sync
            def _(sy):
                for (dst, src) in ((pofs, ap["pofs_i"]), (qpos_c, ap["qpos_i"]),
                                   (invf[0], ap["invf0_i"]),
                                   (invf[1], ap["invf1_i"]),
                                   (gs, ap["gsc"]), (us, ap["usc"]),
                                   (sbc, ap["dsc"])):
                    sy.dma_start(out=dst[:], in_=src).then_inc(ldp.h, 16)
                    ldp.inc(16)

            @E.block.vector
            def _(ve):
                ve.wait_ge(ldp.h, ldp.n)
                ve.tensor_scalar(sbc[:], sbc[:], 0.5, None, OP.mult)\
                    .then_inc(sbcs.h, 1)
                sbcs.inc()

            cvt = {n: E.sem(f"c_cv_{n}") for n in ["q8", "q8d", "f1", "f1d"]}
            gat = {n: E.sem(f"c_ag_{n}") for n in
                   ["wa", "wq8", "wq8d", "wf1", "wf1d"]}
            bnc = {n: E.sem(f"c_bnc_{n}") for n in
                   ["wq8", "wq8d", "wa", "wf1", "wf1d"]}

            @E.block.gpsimd
            def _(gp):
                for (sh, b, n) in (("wq8_sh", "wq8_b", "wq8"),
                                   ("wq8d_sh", "wq8d_b", "wq8d"),
                                   ("wa_sh", "wa_b", "wa"),
                                   ("wf1_sh", "wf1_b", "wf1"),
                                   ("wf1d_sh", "wf1d_b", "wf1d")):
                    gp.dma_start(out=ap[b], in_=ap[sh]).then_inc(bnc[n].h, 16)
                    bnc[n].inc(16)
                gp.wait_ge(bnc["wq8"].h, 16)
                gp.wait_ge(bnc["wq8d"].h, 16)
                gp.collective_compute(
                    "AllGather", OP.bypass, replica_groups=WORLD,
                    ins=[ap["wq8_b"].opt()], outs=[ap["wq8"].opt()])\
                    .then_inc(gat["wq8"].h, 1)
                gat["wq8"].inc()
                gp.collective_compute(
                    "AllGather", OP.bypass, replica_groups=WORLD,
                    ins=[ap["wq8d_b"].opt()], outs=[ap["wq8d"].opt()])\
                    .then_inc(gat["wq8d"].h, 1)
                gat["wq8d"].inc()
                gp.wait_ge(bnc["wa"].h, 16)
                gp.wait_ge(bnc["wf1"].h, 16)
                gp.wait_ge(bnc["wf1d"].h, 16)
                gp.collective_compute(
                    "AllGather", OP.bypass, replica_groups=WORLD,
                    ins=[ap["wa_b"].opt()], outs=[ap["wa"].opt()])\
                    .then_inc(gat["wa"].h, 1)
                gat["wa"].inc()
                gp.collective_compute(
                    "AllGather", OP.bypass, replica_groups=WORLD,
                    ins=[ap["wf1_b"].opt()], outs=[ap["wf1"].opt()])\
                    .then_inc(gat["wf1"].h, 1)
                gat["wf1"].inc()
                gp.collective_compute(
                    "AllGather", OP.bypass, replica_groups=WORLD,
                    ins=[ap["wf1d_b"].opt()], outs=[ap["wf1d"].opt()])\
                    .then_inc(gat["wf1d"].h, 1)
                gat["wf1d"].inc()
                gp.wait_ge(gat["wq8"].h, 1)
                for i in range(8):
                    gp.dma_start(out=ap["l0gu"][i * 512:(i + 1) * 512, :],
                                 in_=ap["wq8"][i * 512:(i + 1) * 512, :],
                                 max_dma_last_dim=2048).then_inc(cvt["q8"].h, 16)
                    cvt["q8"].inc(16)
                gp.wait_ge(gat["wq8d"].h, 1)
                for i in range(8):
                    gp.dma_start(out=ap["l0d"][i * 1024:(i + 1) * 1024, :],
                                 in_=ap["wq8d"][i * 1024:(i + 1) * 1024, :],
                                 max_dma_last_dim=2048).then_inc(cvt["q8d"].h, 16)
                    cvt["q8d"].inc(16)
                gp.wait_ge(gat["wf1"].h, 1)
                for i in range(8):
                    gp.dma_start(out=ap["l1gu"][i * 512:(i + 1) * 512, :],
                                 in_=ap["wf1"][i * 512:(i + 1) * 512, :],
                                 max_dma_last_dim=8192).then_inc(cvt["f1"].h, 16)
                    cvt["f1"].inc(16)
                gp.wait_ge(gat["wf1d"].h, 1)
                for i in range(8):
                    gp.dma_start(out=ap["l1d"][i * 1024:(i + 1) * 1024, :],
                                 in_=ap["wf1d"][i * 1024:(i + 1) * 1024, :],
                                 max_dma_last_dim=8192).then_inc(cvt["f1d"].h, 16)
                    cvt["f1d"].inc(16)

            def ffn_exit():
                names = ["f_pednc", "f_pegu", "f_petr", "f_pesh", "f_actd",
                         "f_actc", "f_acta", "f_actb", "f_dvea", "f_dveb",
                         "f_dvec", "f_dved", "f_dvesh", "f_gpa",
                         "f_stx00", "f_stx01", "f_stx10", "f_stx11"]
                return [(E.sems[n], E.sems[n].n) for n in names if n in E.sems]

            l0ffn = dict(wg=ap["l0gu"][0:2048, :], wu=ap["l0gu"][2048:4096, :],
                         wd=ap["l0d"], gs=gs, us=us, sbc=sbc)
            l1ffn = dict(wg=ap["l1gu"][0:2048, :], wu=ap["l1gu"][2048:4096, :],
                         wd=ap["l1d"])
            guard0 = ([(cvt["q8"], cvt["q8"].n), (cvt["q8d"], cvt["q8d"].n),
                       (sbcs, sbcs.n)])
            emit_ffn(E, "l0", ap["xin"], ap["xs1"], entry_guard=guard0,
                        **l0ffn)
            emit_attn(E, 0, ap["xs1"], ap["xs2"], ap["wa"][0:8192, :],
                          ap["q_fm"], ap["kv_in"], ap["kv_out"], ap["ctx_buf"],
                          pofs, qpos_c, invf[0],
                          ffn_exit() + [(gat["wa"], gat["wa"].n)], RG)
            emit_ffn(E, "l0", ap["xs2"], ap["xs1"],
                        entry_guard=E.attn_exit_guard, **l0ffn)
            guard1 = ffn_exit() + [(cvt["f1"], cvt["f1"].n),
                                   (cvt["f1d"], cvt["f1d"].n)]
            emit_ffn(E, "l1", ap["xs1"], ap["xs2"], entry_guard=guard1,
                        **l1ffn)
            emit_attn(E, 1, ap["xs2"], ap["xs1"], ap["wa"][8192:16384, :],
                          ap["q_fm"], ap["kv_in"], ap["kv_out"], ap["ctx_buf"],
                          pofs, qpos_c, invf[1], ffn_exit(), RG)
            emit_ffn(E, "l1", ap["xs1"], ap["xout"],
                        entry_guard=E.attn_exit_guard, **l1ffn)
            emit_final_wait(E)
    return nc


_NC_CACHE = [None]


def make_in_maps(inputs, ncores=8):
    h = _host_prep(inputs)
    x = np.asarray(inputs["L0_attn_hidden_states"], dtype=np.float32)
    in_maps = []
    for c in range(ncores):
        b, p = c // 2, c % 2
        pos = np.array([512 * (lt // 256) + 256 * p + (lt % 256)
                        for lt in range(T)], np.float32)
        r1, r2, r3 = 16384 // ncores, 4096 // ncores, 8192 // ncores
        in_maps.append({
            "xin": _shard_x(x, b, p),
            "wa_sh": np.ascontiguousarray(h["wa"][c * r1:(c + 1) * r1]),
            "wq8_sh": np.ascontiguousarray(h["wq8"][c * r2:(c + 1) * r2]),
            "wq8d_sh": np.ascontiguousarray(h["wq8d"][c * r3:(c + 1) * r3]),
            "wf1_sh": np.ascontiguousarray(h["wf1"][c * r2:(c + 1) * r2]),
            "wf1d_sh": np.ascontiguousarray(h["wf1d"][c * r3:(c + 1) * r3]),
            "gsc": h["gsc"], "usc": h["usc"], "dsc": h["dsc"],
            "pofs_i": np.full((128, 1), float(p), np.float32),
            "qpos_i": np.ascontiguousarray(pos.reshape(16, 128).T),
            "invf0_i": h["invf0"], "invf1_i": h["invf1"],
        })
    return in_maps


def kernel(**inputs):
    if _NC_CACHE[0] is None:
        _NC_CACHE[0] = build_program()
    nc = _NC_CACHE[0]
    in_maps = make_in_maps(inputs, 8)
    res = run_bass_kernel_spmd(nc, in_maps, core_ids=list(range(8)))
    return _unshard([res.results[c]["xout"] for c in range(8)])


# version 04
# speedup vs baseline: 1.1073x; 1.1070x over previous
import numpy as np
import concourse.bass as bass
import concourse.mybir as mybir

F32 = mybir.dt.float32
BF16 = mybir.dt.bfloat16
U8 = mybir.dt.uint8
AF = mybir.ActivationFunctionType
OP = mybir.AluOpType
AX = mybir.AxisListType

D = 2048
I_FFN = 8192
T = 2048
NTILE = 16
PASS_T = 512
NPASS = T // PASS_T
DC = D // 128
IC = I_FFN // 128
H = 16
DH = 128
UNIT = 256
EPS = 1e-5


class Sem:
    def __init__(self, h):
        self.h = h
        self.n = 0

    def inc(self, k=1):
        self.n += k
        return self.n


class Env:
    def __init__(self, nc, block, ctx):
        self.nc = nc
        self.block = block
        self.ctx = ctx
        self.sems = {}
        self.sbufs = {}
        self.x_ready = None

    def sem(self, name):
        if name not in self.sems:
            self.sems[name] = Sem(self.ctx.enter_context(self.nc.semaphore(name)))
        return self.sems[name]

    def sbuf(self, name, shape, dtype):
        if name not in self.sbufs:
            self.sbufs[name] = self.ctx.enter_context(
                self.nc.sbuf_tensor(name, shape, dtype))
        return self.sbufs[name]

    def psum(self, name, shape, dtype):
        if name not in self.sbufs:
            self.sbufs[name] = self.ctx.enter_context(
                self.nc.psum_tensor(name, shape, dtype))
        return self.sbufs[name]


def wge(eng, sem, val):
    if val > 0:
        eng.wait_ge(sem.h, val)


def emit_const_prologue(E):
    idn = E.sbuf("idn", [128, 128], BF16)
    ones_col = E.sbuf("ones_col", [128, 1], BF16)
    ones_row = E.sbuf("ones_row", [1, 512], BF16)
    epsb = E.sbuf("epsb", [128, 1], F32)
    c = E.sem("c_const")

    @E.block.gpsimd
    def _(gp):
        gp.memset(idn[:], 1.0)
        gp.drain()
        gp.affine_select(idn[:], idn[:], pattern=[[-1, 128]],
                         compare_op=OP.is_equal, fill=0.0, base=0,
                         channel_multiplier=1)
        gp.memset(ones_col[:], 1.0)
        gp.memset(epsb[:], EPS)
        gp.memset(ones_row[:], 1.0).then_inc(c.h, 1)
        c.inc()


def ffn_buffers(E):
    return dict(
        xbuf=E.sbuf("c_x", [128, 4 * D], F32),
        xout=E.sbuf("c_xout", [128, 2 * D], F32),
        dn_tmp=E.sbuf("f_dntmp", [128, 1024], F32),
        st8=E.sbuf("c_st8", [128, 128], F32),
        xnst=E.sbuf("c_xnst", [128, 2 * D], BF16),
        xn=E.sbuf("f_xn", [128, DC * PASS_T], BF16),
        hsl=E.sbuf("f_h", [128, IC * PASS_T], BF16),
        wgs=E.sbuf("f_wg", [128, 3 * DC * 128], BF16),
        wus=E.sbuf("f_wu", [128, 3 * DC * 128], BF16),
        wds=E.sbuf("f_wd", [128, 3 * 1024], BF16),
        sg=E.sbuf("f_sg", [128, 2 * PASS_T], F32),
        hq=E.sbuf("f_hq", [128, 4 * PASS_T], F32),
        qsq=E.sbuf("f_qsq", [128, 4 * PASS_T], F32),
        qt2=E.sbuf("f_qt2", [128, 3 * PASS_T], F32),
        pall=E.psum("pall", [128, 4096], F32),
    )


def emit_ffn(E, kind, x_src, x_dst, wg, wu, wd, gs=None, us=None, sbc=None,
             entry_guard=()):
    B = ffn_buffers(E)
    xbuf, xout, st8, xnst = B["xbuf"], B["xout"], B["st8"], B["xnst"]
    xn, hsl, wgs, wus, wds = B["xn"], B["hsl"], B["wgs"], B["wus"], B["wds"]
    sg, hq, qsq, qt2 = B["sg"], B["hq"], B["qsq"], B["qt2"]
    shs = qt2[0:1, 0:256].bitcast(BF16)
    dn_tmp, pall = B["dn_tmp"], B["pall"]
    ones_col = E.sbufs["ones_col"]
    ones_row = E.sbufs["ones_row"]
    idn = E.sbufs["idn"]
    epsb = E.sbufs["epsb"]

    pg = [pall[:, 0:512], pall[:, 1024:1536]]
    pu = [pall[:, 512:1024], pall[:, 1536:2048]]
    psh = pall[0:1, 2048:2560]
    ptr = pall[:, 3072:4096].bitcast(BF16)
    pdn = [pall[:, tt * 1024:(tt + 1) * 1024] for tt in range(4)]

    s_ldx = [E.sem(f"f_ldx{i}") for i in range(4)]
    s_lwg = [E.sem(f"f_lwg{i}") for i in range(3)]
    s_lwu = [E.sem(f"f_lwu{i}") for i in range(3)]
    s_ldd = [E.sem(f"f_ldd{i}") for i in range(3)]
    s_stx = {(sl, oh): E.sem(f"f_stx{sl}{oh}")
             for sl in range(2) for oh in range(2)}
    S = {n: E.sem(n) for n in [
        "f_acta", "f_actb", "f_actc", "f_actd",
        "f_dvea", "f_dveb", "f_dvec", "f_dvesh", "f_dved",
        "f_petr", "f_pegu", "f_pesh", "f_pednc", "f_pedno", "f_gpa",
        "c_const"]}
    l0 = kind == "l0"
    ic_order = (list(range(IC)) if l0 else
                [16 * m + qd for qd in range(16) for m in range(4)])

    tok = {}

    x_ready_out = {}
    if not hasattr(E, "ln_guard"):
        E.ln_guard = {}
    if not hasattr(E, "xslot_guard"):
        E.xslot_guard = {}

    def _entry(eng):
        for sem, v in entry_guard:
            wge(eng, sem, v)

    for p in range(NPASS):
        base_petr = S["f_petr"].n
        base_pegu = S["f_pegu"].n
        base_pednc = S["f_pednc"].n
        base_actc = S["f_actc"].n
        base_actd = S["f_actd"].n
        base_dvec = S["f_dvec"].n
        base_dved = S["f_dved"].n
        base_stx = {k: s.n for k, s in s_stx.items()}

        @E.block.gpsimd
        def _(gq, p=p):
            if p == 0:
                _entry(gq)
            for tt in range(4):
                ti = p * 4 + tt
                for sem, v in E.xslot_guard.get(tt, []):
                    wge(gq, sem, v)
                if E.x_ready is not None:
                    for sem, v in E.x_ready[ti]:
                        gq.wait_ge(sem.h, v)
                gq.dma_start(out=xbuf[:, tt * D:(tt + 1) * D],
                             in_=x_src[ti * 128:(ti + 1) * 128, :])\
                    .then_inc(s_ldx[tt].h, 16)
                tok[("ldx", p, tt)] = s_ldx[tt].inc(16)

        @E.block.sync
        def _(sy, p=p, base_pegu=base_pegu, base_pednc=base_pednc):
            if p == 0:
                _entry(sy)
            for j, ic in enumerate(ic_order):
                slot = j % 3
                wge(sy, S["f_pegu"], base_pegu + j - 2)
                for (w, dst, sem) in ((wg, wgs, s_lwg[slot]),
                                      (wu, wus, s_lwu[slot])):
                    ap = w[:, ic * 128:(ic + 1) * 128].rearrange(
                        "(c p) i -> p c i", p=128)
                    sy.dma_start(out=dst[:, slot * 2048:(slot + 1) * 2048]
                                 .rearrange("p (c i) -> p c i", c=DC),
                                 in_=ap).then_inc(sem.h, 16)
                    sem.inc(16)
                tok[("ldw", p, j)] = (s_lwg[slot].n, s_lwu[slot].n)
            for oh in range(2):
                for ic in range(IC):
                    k = oh * IC + ic
                    slot = k % 3
                    wge(sy, S["f_pednc"], base_pednc + k - 2)
                    sy.dma_start(out=wds[:, slot * 1024:(slot + 1) * 1024],
                                 in_=wd[ic * 128:(ic + 1) * 128,
                                        oh * 1024:(oh + 1) * 1024])\
                        .then_inc(s_ldd[slot].h, 16)
                    tok[("ldd", p, oh, ic)] = s_ldd[slot].inc(16)

        @E.block.vector
        def _(ve, p=p):
            if p == 0:
                _entry(ve)
            for tt in range(4):
                ve.wait_ge(s_ldx[tt].h, tok[("ldx", p, tt)])
                for ch in range(4):
                    ve.bn_stats(st8[:, tt * 24 + ch * 6:tt * 24 + ch * 6 + 6],
                                xbuf[:, tt * D + ch * 512:tt * D + (ch + 1) * 512])
            ve.drain()
            for tt in range(4):
                prev = E.ln_guard.get(("actb", tt))
                if prev:
                    ve.wait_ge(prev[0].h, prev[1])
                ve.bn_aggr(st8[:, 96 + 2 * tt:98 + 2 * tt],
                           st8[:, tt * 24:tt * 24 + 24]).then_inc(S["f_dvea"].h, 1)
                tok[("dvea", p, tt)] = S["f_dvea"].inc()
                E.ln_guard[("dvea", tt)] = (S["f_dvea"], tok[("dvea", p, tt)])
            ve.drain()

        @E.block.scalar
        def _(sc, p=p):
            if p == 0:
                _entry(sc)
                sc.wait_ge(S["c_const"].h, 1)
            for tt in range(4):
                sc.wait_ge(S["f_dvea"].h, tok[("dvea", p, tt)])
                prev = E.ln_guard.get(("dveb", tt))
                if prev:
                    sc.wait_ge(prev[0].h, prev[1])
                sc.activation(st8[:, 108 + tt:109 + tt],
                              st8[:, 97 + 2 * tt:98 + 2 * tt],
                              AF.Ln, bias=epsb[:, 0:1])
            sc.drain()
            for tt in range(4):
                sc.activation(st8[:, 104 + tt:105 + tt],
                              st8[:, 108 + tt:109 + tt],
                              AF.Exp, scale=-0.5).then_inc(S["f_actb"].h, 1)
                tok[("actb", p, tt)] = S["f_actb"].inc()
                E.ln_guard[("actb", tt)] = (S["f_actb"], tok[("actb", p, tt)])

        @E.block.vector
        def _(ve, p=p, base_petr=base_petr):
            for tt in range(4):
                ve.wait_ge(S["f_actb"].h, tok[("actb", p, tt)])
                wge(ve, S["f_petr"], base_petr + tt - 1)
                ve.tensor_scalar(xnst[:, (tt % 2) * D:(tt % 2 + 1) * D],
                                 xbuf[:, tt * D:(tt + 1) * D],
                                 st8[:, 96 + 2 * tt:97 + 2 * tt],
                                 st8[:, 104 + tt:105 + tt],
                                 OP.subtract, OP.mult).then_inc(S["f_dveb"].h, 1)
                tok[("dveb", p, tt)] = S["f_dveb"].inc()
                E.ln_guard[("dveb", tt)] = (S["f_dveb"], tok[("dveb", p, tt)])

        @E.block.tensor
        def _(te, p=p, base_actc=base_actc):
            if p == 0:
                _entry(te)
            te.wait_ge(S["c_const"].h, 1)
            for tt in range(4):
                te.wait_ge(S["f_dveb"].h, tok[("dveb", p, tt)])
                wge(te, S["f_actc"], base_actc + tt)
                d31 = tok.get(("dved", p - 1, 3, 1))
                if tt == 0 and d31:
                    te.wait_ge(S["f_dved"].h, d31)
                for dc in range(DC):
                    inst = te.transpose(
                        ptr[:, dc * 128:(dc + 1) * 128],
                        xnst[:, (tt % 2) * D + dc * 128:(tt % 2) * D + (dc + 1) * 128],
                        idn[:])
                    if dc == DC - 1:
                        inst.then_inc(S["f_petr"].h, 1)
                tok[("petr", p, tt)] = S["f_petr"].inc()

        @E.block.scalar
        def _(sc, p=p, base_pegu=base_pegu):
            for tt in range(4):
                sc.wait_ge(S["f_petr"].h, tok[("petr", p, tt)])
                if p > 0 and tt == 0:
                    sc.wait_ge(S["f_pegu"].h, base_pegu)
                sc.copy(
                    out=xn[:].rearrange("p (c t) -> p c t", c=DC)
                    [:, :, tt * 128:(tt + 1) * 128],
                    in_=ptr[:].rearrange("p (c t) -> p c t", c=DC))\
                    .then_inc(S["f_actc"].h, 1)
                tok[("actc", p, tt)] = S["f_actc"].inc()

        @E.block.tensor
        def _(te, p=p, base_actd=base_actd, base_dvec=base_dvec):
            te.wait_ge(S["f_actc"].h, tok[("actc", p, 3)])
            d = tok.get(("dved", p - 1, 3, 1))
            if d:
                te.wait_ge(S["f_dved"].h, d)
            for j, ic in enumerate(ic_order):
                slot = j % 2
                wslot = j % 3
                vg, vu = tok[("ldw", p, j)]
                te.wait_ge(s_lwg[wslot].h, vg)
                te.wait_ge(s_lwu[wslot].h, vu)
                wge(te, S["f_actd"], base_actd + j - 1)
                wge(te, S["f_dvec"], base_dvec + j - 1)
                for dc in range(DC):
                    te.matmul(pg[slot][:, :],
                              wgs[:, wslot * 2048 + dc * 128:wslot * 2048 + (dc + 1) * 128],
                              xn[:, dc * PASS_T:(dc + 1) * PASS_T],
                              start=(dc == 0), stop=(dc == DC - 1))
                for dc in range(DC):
                    inst = te.matmul(
                        pu[slot][:, :],
                        wus[:, wslot * 2048 + dc * 128:wslot * 2048 + (dc + 1) * 128],
                        xn[:, dc * PASS_T:(dc + 1) * PASS_T],
                        start=(dc == 0), stop=(dc == DC - 1))
                    if dc == DC - 1:
                        inst.then_inc(S["f_pegu"].h, 1)
                tok[("pegu", p, j)] = S["f_pegu"].inc()

        @E.block.scalar
        def _(sc, p=p, base_dvec=base_dvec):
            for j, ic in enumerate(ic_order):
                slot = j % 2
                sgslot = sg[:, slot * PASS_T:(slot + 1) * PASS_T]
                sc.wait_ge(S["f_pegu"].h, tok[("pegu", p, j)])
                wge(sc, S["f_dvec"], base_dvec + j - 1)
                sc.activation(sgslot, pg[slot][:, :], AF.Silu,
                              scale=(gs[:, ic:ic + 1] if l0 else 1.0))\
                    .then_inc(S["f_actd"].h, 1)
                tok[("actd", p, j)] = S["f_actd"].inc()

        @E.block.vector
        def _(ve, p=p):
            d = tok.get(("pednc", p - 1))
            if d:
                ve.wait_ge(S["f_pednc"].h, d)
            for j, ic in enumerate(ic_order):
                slot = j % 2
                sgslot = sg[:, slot * PASS_T:(slot + 1) * PASS_T]
                ve.wait_ge(S["f_actd"].h, tok[("actd", p, j)])
                ve.wait_ge(S["f_pegu"].h, tok[("pegu", p, j)])
                if l0:
                    ve.scalar_tensor_tensor(
                        hsl[:, ic * PASS_T:(ic + 1) * PASS_T], pu[slot][:, :],
                        us[:, ic:ic + 1], sgslot, OP.mult, OP.mult)\
                        .then_inc(S["f_dvec"].h, 1)
                else:
                    m = ic // 16
                    ve.tensor_tensor(hq[:, m * PASS_T:(m + 1) * PASS_T],
                                     sgslot, pu[slot][:, :], OP.mult)\
                        .then_inc(S["f_dvec"].h, 1)
                tok[("dvec", p, j)] = S["f_dvec"].inc()
                if not l0 and j % 4 == 3:
                    qd = ic_order[j] % 16
                    sq = [qsq[:, m * PASS_T:(m + 1) * PASS_T] for m in range(4)]
                    hm = [hq[:, m * PASS_T:(m + 1) * PASS_T] for m in range(4)]
                    A = qt2[:, 0:PASS_T]
                    Bb = qt2[:, PASS_T:2 * PASS_T]
                    C = qt2[:, 2 * PASS_T:3 * PASS_T]
                    Dd = dn_tmp[:, 0:PASS_T]
                    ve.drain()
                    for m in range(4):
                        ve.tensor_tensor(sq[m], hm[m], hm[m], OP.mult)
                    ve.drain()
                    ve.tensor_tensor(A, sq[0], sq[1], OP.max)
                    ve.tensor_tensor(Bb, sq[2], sq[3], OP.max)
                    ve.tensor_tensor(C, sq[0], sq[1], OP.min)
                    ve.tensor_tensor(Dd, sq[2], sq[3], OP.min)
                    ve.drain()
                    ve.tensor_tensor(A, A, Bb, OP.min)
                    ve.tensor_tensor(C, C, Dd, OP.max)
                    ve.drain()
                    ve.tensor_tensor(A, A, C, OP.max)
                    ve.drain()
                    for m in range(4):
                        ve.tensor_tensor(sq[m], sq[m], A, OP.is_ge)
                    ve.drain()
                    for m in range(4):
                        icm = 16 * m + qd
                        inst = ve.tensor_tensor(
                            hsl[:, icm * PASS_T:(icm + 1) * PASS_T],
                            hm[m], sq[m], OP.mult)
                        if m == 3:
                            inst.then_inc(S["f_gpa"].h, 1)
                    tok[("gpa", p, qd)] = S["f_gpa"].inc()
                    ve.drain()

        if l0:
            @E.block.tensor
            def _(te, p=p):
                for j in range(IC):
                    ic = ic_order[j]
                    te.wait_ge(S["f_dvec"].h, tok[("dvec", p, j)])
                    inst = te.matmul(psh, ones_col[:],
                                     hsl[:, ic * PASS_T:(ic + 1) * PASS_T],
                                     start=(j == 0), stop=(j == IC - 1))
                    if j == IC - 1:
                        inst.then_inc(S["f_pesh"].h, 1)
                tok[("pesh", p)] = S["f_pesh"].inc()

            @E.block.vector
            def _(ve, p=p):
                ve.wait_ge(S["f_pesh"].h, tok[("pesh", p)])
                ve.tensor_scalar(shs[:, :], psh, -128.0, None, OP.mult)\
                    .then_inc(S["f_dvesh"].h, 1)
                tok[("dvesh", p)] = S["f_dvesh"].inc()

        @E.block.tensor
        def _(te, p=p, base_dved=base_dved):
            te.wait_ge(S["f_dvec"].h, tok[("dvec", p, IC - 1)])
            te.wait_ge(S["f_actd"].h, tok[("actd", p, IC - 1)])
            if not l0:
                te.wait_ge(S["f_gpa"].h, tok[("gpa", p, 15)])
            for oh in range(2):
                if oh == 1:
                    te.wait_ge(S["f_dved"].h, base_dved + 4)
                for ici in range(IC):
                    v = tok[("ldd", p, oh, ici)]
                    te.wait_ge(s_ldd[(oh * IC + ici) % 3].h, v)
                    slot = (oh * IC + ici) % 3
                    for tt in range(4):
                        if l0 and oh == 0 and ici == 0 and tt == 2:
                            te.wait_ge(S["f_dvesh"].h, tok[("dvesh", p)])
                        lt = hsl[:, ici * PASS_T + tt * 128:ici * PASS_T + (tt + 1) * 128]
                        for hf in range(2):
                            inst = te.matmul(
                                pdn[tt][:, hf * 512:(hf + 1) * 512], lt,
                                wds[:, slot * 1024 + hf * 512:slot * 1024 + (hf + 1) * 512],
                                start=(ici == 0),
                                stop=(ici == IC - 1 and not l0))
                            if tt == 3 and hf == 1:
                                inst.then_inc(S["f_pednc"].h, 1)
                    S["f_pednc"].inc()
                if l0:
                    te.wait_ge(S["f_dvesh"].h, tok[("dvesh", p)])
                    for tt in range(4):
                        for hf in range(2):
                            inst = te.matmul(
                                pdn[tt][:, hf * 512:(hf + 1) * 512],
                                shs[0:1, tt * 128:(tt + 1) * 128],
                                ones_row[0:1, 0:512],
                                start=False, stop=True)
                            if tt == 3 and hf == 1:
                                inst.then_inc(S["f_pedno"].h, 1)
                    tok[("pedno", p, oh)] = S["f_pedno"].inc()
                else:
                    tok[("pedno", p, oh)] = None
            tok[("pednc", p)] = S["f_pednc"].n

        @E.block.vector
        def _(ve, p=p, base_pednc=base_pednc):
            for oh in range(2):
                if l0:
                    ve.wait_ge(S["f_pedno"].h, tok[("pedno", p, oh)])
                else:
                    ve.wait_ge(S["f_pednc"].h, base_pednc + IC * (oh + 1))
                for tt in range(4):
                    sl = tt % 2
                    bs = base_stx[(sl, oh)]
                    if tt >= 2:
                        wge(ve, s_stx[(sl, oh)], bs + 16)
                    else:
                        wge(ve, s_stx[(sl, oh)], bs)
                    xa = xbuf[:, tt * D + oh * 1024:tt * D + (oh + 1) * 1024]
                    oa = xout[:, sl * D + oh * 1024:sl * D + (oh + 1) * 1024]
                    pa = pdn[tt][:, 0:1024]
                    if l0:
                        ve.tensor_tensor(dn_tmp[:], pa,
                                         sbc[:, oh * 1024:(oh + 1) * 1024], OP.mult)
                        ve.drain()
                        ve.tensor_tensor(oa, dn_tmp[:], xa, OP.add)\
                            .then_inc(S["f_dved"].h, 1)
                        ve.drain()
                    else:
                        ve.scalar_tensor_tensor(oa, pa, 0.5, xa, OP.mult, OP.add)\
                            .then_inc(S["f_dved"].h, 1)
                    tok[("dved", p, tt, oh)] = S["f_dved"].inc()
                    if oh == 1:
                        E.xslot_guard[tt] = [(S["f_dved"], tok[("dved", p, tt, 1)])]

        @E.block.gpsimd
        def _(sy, p=p):
            for oh in range(2):
                for tt in range(4):
                    ti = p * 4 + tt
                    sl = tt % 2
                    sy.wait_ge(S["f_dved"].h, tok[("dved", p, tt, oh)])
                    sy.dma_start(
                        out=x_dst[ti * 128:(ti + 1) * 128,
                                  oh * 1024:(oh + 1) * 1024],
                        in_=xout[:, sl * D + oh * 1024:sl * D + (oh + 1) * 1024])\
                        .then_inc(s_stx[(sl, oh)].h, 16)
                    tok[("stx", p, tt, oh)] = s_stx[(sl, oh)].inc(16)
                    if oh == 1:
                        x_ready_out[ti] = [
                            (s_stx[(sl, 0)], tok[("stx", p, tt, 0)]),
                            (s_stx[(sl, 1)], tok[("stx", p, tt, 1)])]

    E.x_ready = x_ready_out


def emit_final_wait(E):
    @E.block.gpsimd
    def _(gp):
        for name, s in E.sems.items():
            if name.startswith("f_stx"):
                gp.wait_ge(s.h, s.n)




"""Attention sublayer emitter for mk_macaron_d2: x_dst = x_src + Attn(LN(x)).

Working buffers alias the FFN buffers (bitcast views); entry_guard serializes
against the preceding sublayer, and E.attn_exit_guard (set here) must be
waited by the next sublayer's first sections.

Arena map (per-partition):
  phase A:  aw=hsl(64K) | qtm = qsq+hq (2x8K f32) | kvst = wds+sg (2x4K bf16)
            kfst+xnf = xout (4x4K bf16) | rope scratch = qt2 (8K f32)
            trig = xn (16K f32) | LN: xbuf/xsq/st8/xnst as usual
  masks  :  xsq(8 tiles) + qt2(8) + dn_tmp(4+dil128)  (computed after A)
  phase C:  S=hsl[0:16K]f32 | P=hsl[16K:24K]bf16 | pts=hsl[24K:32K]bf16
            K_h=hsl[32K:48K]x2 | ctxh=hsl[48K:56K]x2 | ctx_sc=hsl[56K:56.25K]
            V_h = xn x2 (bf16 [128,4096]) | Q_h = wgs x2 | mx/stats = st8
  phase Dp: Wo=hsl | ctx tiles = wgs x2 | xbuf/xout/stores as FFN

PSUM:  A: proj b0-3 | ptrx b4-5 | ptrk b6-7
       C: S slots b0-2 | ptrp b4-5 | pctx b6 | pctxT b7
       Dp: b0-3 / b4-7 alternating tiles
"""

SC128 = float(1.0 / np.sqrt(128.0))
TWO_PI = float(2 * np.pi)
NF = {0: 32, 1: 64}


def attn_chunks(layer, jt):
    j, r = jt // 2, jt % 2
    chunks = []
    if layer == 0:
        for c in range(2):
            base = c * 2048
            full = j * 256
            off = 0
            while off < full:
                w = min(512, full - off)
                chunks.append((base + off, w, None,
                               [c * 16 + off // 128 + k for k in range(w // 128)]))
                off += w
            chunks.append((base + j * 256, 256, ("l0", c, r),
                           [c * 16 + 2 * j, c * 16 + 2 * j + 1]))
    else:
        for pos in range(0, max(0, j - 2)):
            for c in range(2):
                mask = None if (pos == 0 and c == 0) else "dil128"
                chunks.append((c * 2048 + pos * 256, 128, mask,
                               [c * 16 + 2 * pos]))
        for e in range(min(2, j), -1, -1):
            pos = j - e
            for c in range(2):
                if pos == 0 and c == 0:
                    mask = ("l1g", e, r)
                else:
                    mask = ("l1", e, c, r)
                chunks.append((c * 2048 + pos * 256, 256, mask,
                               [c * 16 + 2 * pos, c * 16 + 2 * pos + 1]))
    ext = sum(w for (_, w, _, _) in chunks)
    assert ext % 128 == 0 and ext <= 4096
    return ext, chunks


def _entry(eng, guard):
    for sem, v in guard:
        wge(eng, sem, v)


def emit_attn(E, layer, x_src, x_dst, w_all, q_fm, kv_in, kv_out, ctx_buf,
              pofs, qpos_c, invf_b, entry_guard, replica_groups, upto="full"):
    nc = E.nc
    B = ffn_buffers(E)
    xbuf, xout, st8, xnst = B["xbuf"], B["xout"], B["st8"], B["xnst"]
    pall = B["pall"]
    idn = E.sbufs["idn"]
    epsb = E.sbufs["epsb"]
    nf = NF[layer]

    hslb = B["hsl"][:]
    aw = hslb[:, 0:32768]
    S_slabs = [hslb.bitcast(F32)[:, 0:4096], hslb.bitcast(F32)[:, 4096:8192]]
    P_slab = hslb[:, 16384:20480]
    pts = hslb[:, 20480:24576]
    K_h = [hslb[:, 24576:28672]] * 2
    ctxh = [hslb[:, 28672:30720]] * 2
    ctx_sc = hslb[:, 30720:30848]
    xnb = B["xn"][:]
    trig = xnb.bitcast(F32)
    cosk, sink = trig[:, 0:1024], trig[:, 1024:2048]
    cosq, sinq = trig[:, 2048:3072], trig[:, 3072:4096]
    V_h = [xnb[:, s * 4096:(s + 1) * 4096] for s in range(2)]
    wgsb = B["wgs"][:]
    Q_h = [wgsb[:, s * 2048:(s + 1) * 2048] for s in range(2)]
    ctxt = [wgsb[:, s * 2048:(s + 1) * 2048] for s in range(2)]
    qtm = [B["qsq"][:], B["hq"][:]]
    kvst = [B["wds"][:, 0:2048], B["sg"][:].bitcast(BF16)]
    xob = xout[:].bitcast(BF16)
    kfst = [xob[:, 0:2048], xob[:, 2048:4096]]
    xnf = [xob[:, 4096:6144], xob[:, 6144:8192]]
    ropet = B["qt2"][:]
    dn_tmpf = B["dn_tmp"][:]
    qt2f = B["qt2"][:]
    hqf = B["hq"][:]
    sgf = B["sg"][:]
    mask_store = ([hqf[:, i * 256:(i + 1) * 256] for i in range(8)]
                  + [qt2f[:, i * 256:(i + 1) * 256] for i in range(6)]
                  + [dn_tmpf[:, i * 256:(i + 1) * 256] for i in range(3)]
                  + [sgf[:, 0:256]])
    dil128 = sgf[:, 256:384]

    pproj = pall[:, 0:2048]
    ptrx = pall[:, 2048:3072].bitcast(BF16)
    ptrk = pall[:, 3072:4096].bitcast(BF16)
    pS = [pall[:, i * 512:(i + 1) * 512] for i in range(3)]
    ptrp = pall[:, 2048:3072].bitcast(BF16)
    pctx = pall[:, 3072:3200]
    pctxT = pall[:, 3584:4096].bitcast(BF16)[:, 0:128]
    pD = [pall[:, 0:2048], pall[:, 2048:4096]]

    A = {n: E.sem(n) for n in [
        "a_lw", "a_petr", "a_actc", "a_pep", "a_actp", "a_dver", "a_petk", "a_sin",
        "a_dvek", "a_cc", "a_prep", "a_msk",
        "a_pes", "a_acts", "a_gpm", "a_dvemx", "a_acte", "a_pept", "a_dvept",
        "a_peav", "a_dvecx", "a_pecx", "a_dvecs", "a_peo", "a_dveo",
        "c_ldp", "c_const"]}
    LN = {n: E.sem(n) for n in ["f_acta", "f_actb", "f_dvea", "f_dveb"]}
    s_ldx = [E.sem(f"f_ldx{i}") for i in range(4)]
    s_stx = {(sl, oh): E.sem(f"f_stx{sl}{oh}")
             for sl in range(2) for oh in range(2)}
    s_stk = [E.sem(f"a_stk{i}") for i in range(2)]
    s_stv = [E.sem(f"a_stv{i}") for i in range(2)]
    s_lkh = [E.sem(f"a_lkh{i}") for i in range(2)]
    s_lvh = {(i, r): E.sem(f"a_lvh{i}{r}") for i in range(2) for r in range(2)}
    s_lqh = [E.sem(f"a_lqh{i}") for i in range(2)]
    s_sct = [E.sem(f"a_sct{i}") for i in range(2)]
    s_lct = [E.sem(f"a_lct{i}") for i in range(2)]

    tok = {}
    lntok = {}
    if not hasattr(E, "ln_guard"):
        E.ln_guard = {}
    if not hasattr(E, "xslot_guard"):
        E.xslot_guard = {}

    @E.block.vector
    def _(ve):
        _entry(ve, entry_guard)
        ve.wait_ge(A["c_ldp"].h, A["c_ldp"].n)
        ang = ropet[:, 0:1024]
        t1 = dn_tmpf[:, 0:1024]
        if nf < 64:
            ve.memset(ang, 0.0)
            ve.drain()
        for tt in range(16):
            ve.tensor_scalar(ang[:, tt * 64:tt * 64 + nf], invf_b[:, 0:nf],
                             qpos_c[:, tt:tt + 1], None, OP.mult)
        ve.drain()
        for is_cos in (0, 1):
            if is_cos:
                ve.tensor_scalar(ang[:], ang[:], float(np.pi / 2), None, OP.add)
                ve.drain()
            ve.tensor_scalar(t1[:], ang[:], float(1.0 / TWO_PI), None, OP.mult)
            ve.drain()
            it32 = B["qsq"][:].bitcast(mybir.dt.int32)[:, 0:1024]
            ve.tensor_copy(it32, t1[:])
            ve.drain()
            ve.tensor_copy(t1[:], it32)
            ve.drain()
            dst = (cosk if is_cos else sink)
            ve.scalar_tensor_tensor(dst, t1[:], -TWO_PI, ang[:], OP.mult, OP.add)
            ve.drain()
            ve.tensor_scalar(t1[:], dst, float(np.pi), None, OP.is_gt)
            ve.drain()
            ve.scalar_tensor_tensor(dst, t1[:], -TWO_PI, dst, OP.mult, OP.add)\
                .then_inc(A["a_prep"].h, 1)
            tok[("red", is_cos)] = A["a_prep"].inc()
            ve.drain()

    @E.block.scalar
    def _(sc):
        _entry(sc, entry_guard)
        for is_cos in (0, 1):
            sc.wait_ge(A["a_prep"].h, tok[("red", is_cos)])
            dst = (cosk if is_cos else sink)
            sc.activation(dst, dst, AF.Sin).then_inc(A["a_sin"].h, 1)
            A["a_sin"].inc()
        sc.drain()

    @E.block.vector
    def _(ve):
        ve.wait_ge(A["a_sin"].h, A["a_sin"].n)
        ve.tensor_scalar(cosq, cosk, SC128, None, OP.mult)
        ve.tensor_scalar(sinq, sink, SC128, None, OP.mult)\
            .then_inc(A["a_prep"].h, 1)
        A["a_prep"].inc()
    prep_tok = A["a_prep"].n

    def a_phase(mi, wrow, needs_rope, qscale):
        key = ("A", layer, mi)
        b_pep = A["a_pep"].n
        b_actp = A["a_actp"].n
        b_petr = A["a_petr"].n
        b_actc = A["a_actc"].n
        b_dver = A["a_dver"].n
        b_petk = A["a_petk"].n
        b_dvek = A["a_dvek"].n
        b_stk = [s.n for s in s_stk]
        b_stv = [s.n for s in s_stv]

        @E.block.gpsimd
        def _(gp):
            if mi == 0:
                _entry(gp, entry_guard)
            if b_pep > 0:
                gp.wait_ge(A["a_pep"].h, b_pep)
            gp.dma_start(out=aw.rearrange("p (c o) -> p c o", c=DC),
                         in_=w_all[wrow * 2048:(wrow + 1) * 2048, :]
                         .rearrange("(c p) o -> p c o", p=128))\
                .then_inc(A["a_lw"].h, 16)
            tok[(key, "lw")] = A["a_lw"].inc(16)

        for g in range(4):
            @E.block.gpsimd
            def _(gq, g=g):
                for tt in range(4):
                    ti = g * 4 + tt
                    for sem, v in E.xslot_guard.get(tt, []):
                        wge(gq, sem, v)
                    if E.x_ready is not None:
                        for sem, v in E.x_ready[ti]:
                            gq.wait_ge(sem.h, v)
                    gq.dma_start(out=xbuf[:, tt * D:(tt + 1) * D],
                                 in_=x_src[ti * 128:(ti + 1) * 128, :])\
                        .then_inc(s_ldx[tt].h, 16)
                    lntok[("ldx", key, g, tt)] = s_ldx[tt].inc(16)

            @E.block.vector
            def _(ve, g=g):
                if mi == 0 and g == 0:
                    ve.wait_ge(A["a_prep"].h, tok[("red", 1)])
                for tt in range(4):
                    ve.wait_ge(s_ldx[tt].h, lntok[("ldx", key, g, tt)])
                    for ch in range(4):
                        ve.bn_stats(st8[:, tt * 24 + ch * 6:tt * 24 + ch * 6 + 6],
                                    xbuf[:, tt * D + ch * 512:tt * D + (ch + 1) * 512])
                ve.drain()
                for tt in range(4):
                    prev = E.ln_guard.get(("actb", tt))
                    if prev:
                        ve.wait_ge(prev[0].h, prev[1])
                    ve.bn_aggr(st8[:, 96 + 2 * tt:98 + 2 * tt],
                               st8[:, tt * 24:tt * 24 + 24])\
                        .then_inc(LN["f_dvea"].h, 1)
                    lntok[("dvea", key, g, tt)] = LN["f_dvea"].inc()
                    E.ln_guard[("dvea", tt)] = (LN["f_dvea"],
                                                lntok[("dvea", key, g, tt)])
                ve.drain()

            @E.block.scalar
            def _(sc, g=g):
                if mi == 0 and g == 0:
                    sc.wait_ge(A["c_const"].h, 1)
                for tt in range(4):
                    sc.wait_ge(LN["f_dvea"].h, lntok[("dvea", key, g, tt)])
                    prev = E.ln_guard.get(("dveb", tt))
                    if prev:
                        sc.wait_ge(prev[0].h, prev[1])
                    sc.activation(st8[:, 108 + tt:109 + tt],
                                  st8[:, 97 + 2 * tt:98 + 2 * tt],
                                  AF.Ln, bias=epsb[:, 0:1])
                sc.drain()
                for tt in range(4):
                    sc.activation(st8[:, 104 + tt:105 + tt],
                                  st8[:, 108 + tt:109 + tt],
                                  AF.Exp, scale=-0.5).then_inc(LN["f_actb"].h, 1)
                    lntok[("actb", key, g, tt)] = LN["f_actb"].inc()
                    E.ln_guard[("actb", tt)] = (LN["f_actb"],
                                                lntok[("actb", key, g, tt)])

            @E.block.vector
            def _(ve, g=g, b_petr=b_petr):
                for tt in range(4):
                    i = g * 4 + tt
                    ve.wait_ge(LN["f_actb"].h, lntok[("actb", key, g, tt)])
                    wge(ve, A["a_petr"], b_petr + i - 1)
                    ve.tensor_scalar(xnst[:, (i % 2) * D:(i % 2 + 1) * D],
                                     xbuf[:, tt * D:(tt + 1) * D],
                                     st8[:, 96 + 2 * tt:97 + 2 * tt],
                                     st8[:, 104 + tt:105 + tt],
                                     OP.subtract, OP.mult)\
                        .then_inc(LN["f_dveb"].h, 1)
                    lntok[("dveb", key, g, tt)] = LN["f_dveb"].inc()
                    E.ln_guard[("dveb", tt)] = (LN["f_dveb"],
                                                lntok[("dveb", key, g, tt)])
                    E.xslot_guard[tt] = [(LN["f_dveb"],
                                          lntok[("dveb", key, g, tt)])]

            def _trk(te, i):
                te.wait_ge(A["a_dver"].h, b_dver + i + 1)
                wge(te, A["a_dvek"], b_dvek + i)
                for hc in range(DC):
                    inst = te.transpose(
                        ptrk[:, hc * 128:(hc + 1) * 128],
                        kvst[i % 2][:, hc * 128:(hc + 1) * 128], idn[:])
                    if hc == DC - 1:
                        inst.then_inc(A["a_petk"].h, 1)
                assert A["a_petk"].inc() == b_petk + i + 1

            @E.block.tensor
            def _(te, g=g):
                if mi == 0 and g == 0:
                    _entry(te, entry_guard)
                    te.wait_ge(A["c_const"].h, 1)
                for tt in range(4):
                    i = g * 4 + tt
                    te.wait_ge(LN["f_dveb"].h, lntok[("dveb", key, g, tt)])
                    wge(te, A["a_actc"], b_actc + i)
                    for dc in range(DC):
                        inst = te.transpose(
                            ptrx[:, dc * 128:(dc + 1) * 128],
                            xnst[:, (i % 2) * D + dc * 128:
                                 (i % 2) * D + (dc + 1) * 128], idn[:])
                        if dc == DC - 1:
                            inst.then_inc(A["a_petr"].h, 1)
                    assert A["a_petr"].inc() == b_petr + i + 1
                    te.wait_ge(A["a_actc"].h, b_actc + i + 1)
                    if i == 0:
                        te.wait_ge(A["a_lw"].h, tok[(key, "lw")])
                    wge(te, A["a_actp"], b_actp + i)
                    xf = xnf[i % 2]
                    for dc in range(DC):
                        for nn in range(4):
                            inst = te.matmul(
                                pproj[:, nn * 512:(nn + 1) * 512],
                                xf[:, dc * 128:(dc + 1) * 128],
                                aw[:, dc * 2048 + nn * 512:dc * 2048 + (nn + 1) * 512],
                                start=(dc == 0), stop=(dc == DC - 1))
                            if dc == DC - 1 and nn == 3:
                                inst.then_inc(A["a_pep"].h, 1)
                    assert A["a_pep"].inc() == b_pep + i + 1
                    if needs_rope and tt >= 1:
                        _trk(te, i - 1)
                if needs_rope:
                    _trk(te, g * 4 + 3)

            @E.block.scalar
            def _(sc, g=g):
                for tt in range(4):
                    i = g * 4 + tt
                    sl = i % 2
                    sc.wait_ge(A["a_petr"].h, b_petr + i + 1)
                    wge(sc, A["a_pep"], b_pep + i - 1)
                    sc.copy(out=xnf[sl], in_=ptrx[:])\
                        .then_inc(A["a_actc"].h, 1)
                    assert A["a_actc"].inc() == b_actc + i + 1
                    sc.drain()
                    sc.wait_ge(A["a_pep"].h, b_pep + i + 1)
                    if needs_rope:
                        wge(sc, A["a_dver"], b_dver + i - 1)
                        sc.copy(out=qtm[sl], in_=pproj[:])\
                            .then_inc(A["a_actp"].h, 1)
                    else:
                        wge(sc, s_stv[sl], b_stv[sl] + (i // 2) * 16)
                        sc.copy(out=kvst[sl], in_=pproj[:])\
                            .then_inc(A["a_actp"].h, 1)
                    assert A["a_actp"].inc() == b_actp + i + 1
                    sc.drain()

            if needs_rope:
                def _evk(ve, i):
                    sl = i % 2
                    ve.wait_ge(A["a_petk"].h, b_petk + i + 1)
                    wge(ve, s_stk[sl], b_stk[sl] + (i // 2) * 16)
                    ve.tensor_copy(kfst[sl], ptrk[:])\
                        .then_inc(A["a_dvek"].h, 1)
                    assert A["a_dvek"].inc() == b_dvek + i + 1
                    ve.drain()

                @E.block.vector
                def _(ve, g=g):
                    ct, st = (cosq, sinq) if qscale else (cosk, sink)
                    for tt in range(4):
                        i = g * 4 + tt
                        sl = i % 2
                        ve.wait_ge(A["a_actp"].h, b_actp + i + 1)
                        if mi == 0 and g == 0 and tt == 0:
                            ve.wait_ge(A["a_prep"].h, prep_tok)
                        wge(ve, A["a_petk"], b_petk + i - 1)
                        src, dst = qtm[sl], kvst[sl]
                        if layer == 0:
                            t1 = ropet[:, 0:512].rearrange("p (h j) -> p h j", h=H)
                            t2 = ropet[:, 512:1024].rearrange("p (h j) -> p h j", h=H)
                            sview = src.rearrange("p (h j two) -> p h two j",
                                                  h=H, two=2)
                            dview = dst.rearrange("p (h j two) -> p h two j",
                                                  h=H, two=2)
                            qev = sview[:, :, 0, 0:32]
                            qov = sview[:, :, 1, 0:32]
                            dev = dview[:, :, 0, 0:32]
                            dov = dview[:, :, 1, 0:32]
                            cv = ct[:, i * 64:i * 64 + 32].unsqueeze(1)\
                                .broadcast_to((128, H, 32))
                            sv = st[:, i * 64:i * 64 + 32].unsqueeze(1)\
                                .broadcast_to((128, H, 32))
                            ve.tensor_tensor(t1, qev, cv, OP.mult)
                            ve.tensor_tensor(t2, qov, sv, OP.mult)
                            ve.drain()
                            ve.tensor_tensor(dev, t1, t2, OP.subtract)
                            ve.drain()
                            ve.tensor_tensor(t1, qov, cv, OP.mult)
                            ve.tensor_tensor(t2, qev, sv, OP.mult)
                            ve.drain()
                            ve.tensor_tensor(dov, t1, t2, OP.add)
                            sr = src.rearrange("p (h dd) -> p h dd", h=H)[:, :, 64:128]
                            dr = dst.rearrange("p (h dd) -> p h dd", h=H)[:, :, 64:128]
                            inst = ve.tensor_scalar(
                                dr, sr, SC128 if qscale else 1.0, None, OP.mult)
                        else:
                            t1 = ropet[:, 0:1024].rearrange("p (h j) -> p h j", h=H)
                            t2 = dn_tmpf[:, 0:1024].rearrange("p (h j) -> p h j", h=H)
                            sview = src.rearrange("p (h dd) -> p h dd", h=H)
                            dview = dst.rearrange("p (h dd) -> p h dd", h=H)
                            q1, q2 = sview[:, :, 0:64], sview[:, :, 64:128]
                            d1, d2 = dview[:, :, 0:64], dview[:, :, 64:128]
                            cv = ct[:, i * 64:(i + 1) * 64].unsqueeze(1)\
                                .broadcast_to((128, H, 64))
                            sv = st[:, i * 64:(i + 1) * 64].unsqueeze(1)\
                                .broadcast_to((128, H, 64))
                            ve.tensor_tensor(t1, q1, cv, OP.mult)
                            ve.tensor_tensor(t2, q2, sv, OP.mult)
                            ve.drain()
                            ve.tensor_tensor(d1, t1, t2, OP.subtract)
                            ve.drain()
                            ve.tensor_tensor(t1, q2, cv, OP.mult)
                            ve.tensor_tensor(t2, q1, sv, OP.mult)
                            ve.drain()
                            inst = ve.tensor_tensor(d2, t1, t2, OP.add)
                        inst.then_inc(A["a_dver"].h, 1)
                        assert A["a_dver"].inc() == b_dver + i + 1
                        ve.drain()
                        if tt >= 1:
                            _evk(ve, i - 1)
                    _evk(ve, g * 4 + 3)

            @E.block.gpsimd
            def _(gp, g=g):
                for tt in range(4):
                    i = g * 4 + tt
                    sl = i % 2
                    if needs_rope:
                        gp.wait_ge(A["a_dvek"].h, b_dvek + i + 1)
                        dstap = (kv_in if mi == 0 else q_fm)
                        gp.dma_start(
                            out=dstap.rearrange("(c p) t -> p c t", p=128)
                            [:, 0:DC, i * 128:(i + 1) * 128],
                            in_=kfst[sl].rearrange("p (c t) -> p c t", c=DC))\
                            .then_inc(s_stk[sl].h, 16)
                        assert s_stk[sl].inc(16) == b_stk[sl] + (i // 2 + 1) * 16
                        tok[(key, "st", i)] = s_stk[sl].n
                    else:
                        gp.wait_ge(A["a_actp"].h, b_actp + i + 1)
                        gp.dma_start(
                            out=kv_in[2048 + i * 128:2048 + (i + 1) * 128, :],
                            in_=kvst[sl]).then_inc(s_stv[sl].h, 16)
                        assert s_stv[sl].inc(16) == b_stv[sl] + (i // 2 + 1) * 16
                        tok[(key, "st", i)] = s_stv[sl].n
        return key

    kkey = a_phase(0, 1, True, False)
    vkey = a_phase(1, 2, False, False)

    @E.block.gpsimd
    def _(gp):
        for sl in range(2):
            gp.wait_ge(s_stk[sl].h, s_stk[sl].n)
            gp.wait_ge(s_stv[sl].h, s_stv[sl].n)
        gp.collective_compute(
            "AllGather", OP.bypass, replica_groups=replica_groups,
            ins=[kv_in.opt()], outs=[kv_out.opt()]).then_inc(A["a_cc"].h, 1)
        tok[("cc",)] = A["a_cc"].inc()

    qkey = a_phase(2, 0, True, True)
    if upto == "A":
        return

    mask_keys = ([("l0", c, r) for c in range(2) for r in range(2)]
                 if layer == 0 else
                 [("l1", e, c, r) for e in range(3) for c in range(2)
                  for r in range(2)]
                 + [("l1g", e, r) for e in range(3) for r in range(2)])
    masks = {k: mask_store[i] for i, k in enumerate(mask_keys)}
    if layer == 1:
        masks["dil128"] = dil128
    E.attn_masks = masks
    mscr = B["wgs"][:].bitcast(F32)
    iot = mscr[:, 0:256]
    b01 = mscr[:, 256:257]
    gl = mscr[:, 257:513]
    dil = mscr[:, 513:769]
    pp = mscr[:, 769:770]
    t0 = mscr[:, 1024:1280]
    ok = mscr[:, 1280:1536]
    ok2 = mscr[:, 1536:1792]

    @E.block.gpsimd
    def _(gp):
        gp.wait_ge(LN["f_acta"].h, LN["f_acta"].n)
        gp.wait_ge(A["a_dvek"].h, A["a_dvek"].n)
        if layer == 0:
            for (key, m) in masks.items():
                _, c, r = key
                gp.iota(m, pattern=[[-1, 256]], base=128 * r - 256 * c,
                        channel_multiplier=1,
                        allow_small_or_imprecise_dtypes=True)
        else:
            gp.iota(iot, pattern=[[-1, 4], [0, 64]], base=0,
                    channel_multiplier=0, allow_small_or_imprecise_dtypes=True)
            gp.iota(b01, pattern=[[0, 1]], base=-64, channel_multiplier=1,
                    allow_small_or_imprecise_dtypes=True)
            gp.iota(gl, pattern=[[-1, 256]], base=127, channel_multiplier=0,
                    allow_small_or_imprecise_dtypes=True)
            gp.memset(dil128[:, 0:64], 0.0)
            gp.memset(dil128[:, 64:128], -1e9)
        gp.drain()
        gp.memset(pp, 0.0).then_inc(A["a_gpm"].h, 1)
        A["a_gpm"].inc()
    mski_tok = A["a_gpm"].n

    @E.block.vector
    def _(ve):
        ve.wait_ge(A["a_gpm"].h, mski_tok)
        ve.wait_ge(A["c_ldp"].h, A["c_ldp"].n)
        if layer == 0:
            ve.tensor_scalar(pp, pofs[:, 0:1], 256.0, None, OP.mult)
            ve.drain()
            for (key, m) in masks.items():
                ve.tensor_scalar(m, m, pp[:, 0:1], None, OP.add)
                ve.drain()
                ve.tensor_scalar(m, m, 0.0, 1.0, OP.is_ge, OP.subtract)
                ve.drain()
                inst = ve.tensor_scalar(m, m, 1e9, None, OP.mult)
                ve.drain()
        else:
            ve.tensor_scalar(pp, pofs[:, 0:1], 4.0, None, OP.mult)
            ve.tensor_scalar(b01, b01, 0.0, None, OP.is_ge)
            ve.tensor_scalar(gl, gl, 0.0, None, OP.is_ge)
            ve.tensor_scalar(dil, iot, 0.0, None, OP.is_ge)
            ve.drain()
            for key, m in masks.items():
                if key == "dil128":
                    continue
                if key[0] == "l1":
                    _, e, c, r = key
                    glob = False
                else:
                    _, e, r = key
                    c = 0
                    glob = True
                C0 = 8 * e - 4 * c + 2 * r
                ve.tensor_scalar(t0, iot, float(C0), None, OP.add)
                ve.drain()
                ve.tensor_scalar(t0, t0, b01[:, 0:1], pp[:, 0:1], OP.add, OP.add)
                ve.drain()
                ve.tensor_scalar(ok, t0, 16.0, None, OP.is_le)
                ve.drain()
                ve.tensor_tensor(ok, ok, dil, OP.max)
                ve.drain()
                if glob:
                    ve.tensor_tensor(ok, ok, gl, OP.max)
                    ve.drain()
                ve.tensor_scalar(ok2, t0, 0.0, None, OP.is_ge)
                ve.drain()
                ve.tensor_tensor(ok, ok, ok2, OP.mult)
                ve.drain()
                ve.tensor_scalar(m, ok, 1.0, 1e9, OP.subtract, OP.mult)
                ve.drain()
        ve.tensor_scalar(pp, pp, 1.0, None, OP.mult).then_inc(A["a_msk"].h, 1)
        A["a_msk"].inc()
    msk_tok = A["a_msk"].n

    ckey = ("C", layer)
    qst_base = [s.n for s in s_stk]
    for h in range(H):
        hp = h % 2
        @E.block.sync
        def _(sy, h=h, hp=hp):
            if h == 0:
                _entry(sy, entry_guard)
                sy.wait_ge(A["a_cc"].h, tok[("cc",)])
                sy.wait_ge(A["a_dver"].h, A["a_dver"].n)
                sy.wait_ge(A["a_msk"].h, msk_tok)
            prevav = tok.get((ckey, "peav", h - 1, 15))
            if prevav:
                sy.wait_ge(A["a_peav"].h, prevav)
            prevsct = tok.get((ckey, "sct", h - 1))
            if prevsct:
                sy.wait_ge(s_sct[(h - 1) % 2].h, prevsct)
            sy.dma_start(
                out=K_h[hp].rearrange("p (r t) -> p r t", r=2),
                in_=kv_out.rearrange("(r v blk p) t -> r v blk p t",
                                     r=2, v=2, p=128)[:, 0, h, :, :]
                .rearrange("r p t -> p r t")).then_inc(s_lkh[hp].h, 16)
            tok[(ckey, "lk", h)] = s_lkh[hp].inc(16)
            for r in range(2):
                sy.dma_start(
                    out=V_h[hp].rearrange("p (kt c) -> p kt c", kt=16)
                    if False else
                    V_h[hp].rearrange("p (kt c) -> p kt c", c=128)
                    [:, r * 16:(r + 1) * 16, :],
                    in_=kv_out.rearrange("(r v blk p) c -> r v blk p c",
                                         r=2, v=2, p=128)[r, 1, :, :,
                                                          h * 128:(h + 1) * 128]
                    .rearrange("blk p c -> p blk c"))\
                    .then_inc(s_lvh[(hp, r)].h, 16)
                tok[(ckey, "lv", h, r)] = s_lvh[(hp, r)].inc(16)
            if h == 0:
                for sl in range(2):
                    sy.wait_ge(s_stk[sl].h, qst_base[sl])
            sy.dma_start(out=Q_h[hp],
                         in_=q_fm[h * 128:(h + 1) * 128, :])\
                .then_inc(s_lqh[hp].h, 16)
            tok[(ckey, "lq", h)] = s_lqh[hp].inc(16)

        for jt in range(16):
            ext, chunks = attn_chunks(layer, jt)
            S_slab = S_slabs[jt % 2]
            nch = len(chunks)
            base_acts = A["a_acts"].n
            base_pes = A["a_pes"].n

            @E.block.tensor
            def _(te, h=h, jt=jt, hp=hp, chunks=chunks, base_acts=base_acts):
                te.wait_ge(s_lkh[hp].h, tok[(ckey, "lk", h)])
                te.wait_ge(s_lqh[hp].h, tok[(ckey, "lq", h)])
                if jt == 0 and h < 2:
                    pass
                off = 0
                for ci, (khcol, w, mk_, vts) in enumerate(chunks):
                    if ci >= 3:
                        wge(te, A["a_acts"], base_acts + ci - 2)
                    else:
                        wge(te, A["a_acts"], base_acts)
                    inst = te.matmul(pS[ci % 3][:, 0:w],
                                     Q_h[hp][:, jt * 128:(jt + 1) * 128],
                                     K_h[hp][:, khcol:khcol + w],
                                     start=True, stop=True)
                    inst.then_inc(A["a_pes"].h, 1)
                    tok[(ckey, "pes", h, jt, ci)] = A["a_pes"].inc()
                    off += w

            @E.block.vector
            def _(ve, h=h, jt=jt, chunks=chunks, nch=nch, ext=ext):
                if h == 0 and jt == 0:
                    ve.wait_ge(A["a_msk"].h, msk_tok)
                off = 0
                for ci, (khcol, w, mk_, vts) in enumerate(chunks):
                    ve.wait_ge(A["a_pes"].h, tok[(ckey, "pes", h, jt, ci)])
                    if mk_ is not None:
                        ve.tensor_tensor(S_slab[:, off:off + w],
                                         pS[ci % 3][:, 0:w],
                                         masks[mk_][:, 0:w], OP.add)
                    else:
                        ve.tensor_copy(S_slab[:, off:off + w], pS[ci % 3][:, 0:w])
                    ve.drain()
                    inst = ve.reduce_max(st8[:, 112 + ci:113 + ci],
                                         S_slab[:, off:off + w], axis=AX.X)
                    inst.then_inc(A["a_acts"].h, 1)
                    tok[(ckey, "acts", h, jt, ci)] = A["a_acts"].inc()
                    off += w
                ve.drain()
                ve.reduce_max(st8[:, 20:21], st8[:, 112:112 + nch], axis=AX.X)
                ve.drain()
                ve.tensor_scalar(st8[:, 21:22], st8[:, 20:21], -1.0, None,
                                 OP.mult).then_inc(A["a_dvemx"].h, 1)
                tok[(ckey, "mx", h, jt)] = A["a_dvemx"].inc()

            nhalf = 2 if ext > 2048 else 1
            tok[(ckey, "nhalf", h, jt)] = nhalf
            @E.block.scalar
            def _(sc, h=h, jt=jt, ext=ext, nhalf=nhalf):
                sc.wait_ge(A["a_dvemx"].h, tok[(ckey, "mx", h, jt)])
                prev = tok.get((ckey, "pept1", h, jt - 1)) or \
                    tok.get((ckey, "pept1", h - 1, 15))
                if prev:
                    sc.wait_ge(A["a_pept"].h, prev)
                for hf in range(nhalf):
                    a, b = hf * 2048, min(ext, (hf + 1) * 2048)
                    sc.activation(P_slab[:, a:b], S_slab[:, a:b], AF.Exp,
                                  bias=st8[:, 21:22],
                                  accum_out=st8[:, 22 + hf:23 + hf])\
                        .then_inc(A["a_acte"].h, 1)
                    tok[(ckey, "acte", h, jt, hf)] = A["a_acte"].inc()
                    sc.drain()
                tok[(ckey, "acte", h, jt)] = A["a_acte"].n

            nk = ext // 128
            nb = (nk + 15) // 16
            for b in range(nb):
                t0_ = b * 16
                t1_ = min(nk, t0_ + 16)
                @E.block.tensor
                def _(te, h=h, jt=jt, b=b, t0_=t0_, t1_=t1_):
                    if b == 0:
                        te.wait_ge(A["a_acte"].h, tok[(ckey, "acte", h, jt, 0)])
                    else:
                        te.wait_ge(A["a_acte"].h, tok[(ckey, "acte", h, jt)])
                        te.wait_ge(A["a_dvept"].h, tok[(ckey, "dvept", h, jt, b - 1)])
                    for k in range(t0_, t1_):
                        inst = te.transpose(
                            ptrp[:, (k - t0_) * 128:(k - t0_ + 1) * 128],
                            P_slab[:, k * 128:(k + 1) * 128], idn[:])
                        if k == t1_ - 1:
                            inst.then_inc(A["a_pept"].h, 1)
                    tok[(ckey, f"pept{b}", h, jt)] = A["a_pept"].inc()
                    if b == nb - 1:
                        tok[(ckey, "pept1", h, jt)] = A["a_pept"].n

                @E.block.vector
                def _(ve, h=h, jt=jt, b=b, t0_=t0_, t1_=t1_):
                    ve.wait_ge(A["a_pept"].h, tok[(ckey, f"pept{b}", h, jt)])
                    ve.tensor_copy(pts[:, t0_ * 128:t1_ * 128],
                                   ptrp[:, 0:(t1_ - t0_) * 128])\
                        .then_inc(A["a_dvept"].h, 1)
                    tok[(ckey, "dvept", h, jt, b)] = A["a_dvept"].inc()
                    ve.drain()

            vtl = [vt for (_, w, _, vts) in chunks for vt in vts]
            @E.block.tensor
            def _(te, h=h, jt=jt, hp=hp, vtl=vtl, nb=nb):
                te.wait_ge(A["a_dvept"].h, tok[(ckey, "dvept", h, jt, nb - 1)])
                for r in range(2):
                    te.wait_ge(s_lvh[(hp, r)].h, tok[(ckey, "lv", h, r)])
                prevcx = tok.get((ckey, "dvecx", h, jt - 1)) or \
                    tok.get((ckey, "dvecx", h - 1, 15))
                if prevcx:
                    te.wait_ge(A["a_dvecx"].h, prevcx)
                for k, vt in enumerate(vtl):
                    inst = te.matmul(
                        pctx[:, 0:128],
                        pts[:, k * 128:(k + 1) * 128],
                        V_h[hp].rearrange("p (kt c) -> p kt c", c=128)[:, vt, :],
                        start=(k == 0), stop=(k == len(vtl) - 1))
                    if k == len(vtl) - 1:
                        inst.then_inc(A["a_peav"].h, 1)
                tok[(ckey, "peav", h, jt)] = A["a_peav"].inc()

            @E.block.vector
            def _(ve, h=h, jt=jt):
                ve.wait_ge(A["a_peav"].h, tok[(ckey, "peav", h, jt)])
                prevcx = tok.get((ckey, "pecx", h, jt - 1)) or \
                    tok.get((ckey, "pecx", h - 1, 15))
                if prevcx:
                    ve.wait_ge(A["a_pecx"].h, prevcx)
                if tok[(ckey, "nhalf", h, jt)] == 2:
                    ve.tensor_tensor(st8[:, 24:25], st8[:, 22:23],
                                     st8[:, 23:24], OP.add)
                else:
                    ve.tensor_scalar(st8[:, 24:25], st8[:, 22:23], 1.0, None,
                                     OP.mult)
                ve.drain()
                ve.reciprocal(st8[:, 25:26], st8[:, 24:25])
                ve.drain()
                ve.tensor_scalar(ctx_sc, pctx[:, 0:128], st8[:, 25:26], None,
                                 OP.mult).then_inc(A["a_dvecx"].h, 1)
                tok[(ckey, "dvecx", h, jt)] = A["a_dvecx"].inc()
                ve.drain()

            @E.block.tensor
            def _(te, h=h, jt=jt):
                te.wait_ge(A["a_dvecx"].h, tok[(ckey, "dvecx", h, jt)])
                prevcs = tok.get((ckey, "dvecs", h, jt - 1)) or \
                    tok.get((ckey, "dvecs", h - 1, 15))
                if prevcs:
                    te.wait_ge(A["a_dvecs"].h, prevcs)
                te.transpose(pctxT, ctx_sc, idn[:]).then_inc(A["a_pecx"].h, 1)
                tok[(ckey, "pecx", h, jt)] = A["a_pecx"].inc()

            @E.block.vector
            def _(ve, h=h, jt=jt, hp=hp):
                ve.wait_ge(A["a_pecx"].h, tok[(ckey, "pecx", h, jt)])
                if jt == 0:
                    prevst = tok.get((ckey, "sct", h - 1))
                    if prevst:
                        ve.wait_ge(s_sct[(h - 1) % 2].h, prevst)
                ve.tensor_copy(ctxh[hp][:, jt * 128:(jt + 1) * 128], pctxT)\
                    .then_inc(A["a_dvecs"].h, 1)
                tok[(ckey, "dvecs", h, jt)] = A["a_dvecs"].inc()
                ve.drain()

        @E.block.gpsimd
        def _(gp, h=h, hp=hp):
            gp.wait_ge(A["a_dvecs"].h, tok[(ckey, "dvecs", h, 15)])
            gp.dma_start(out=ctx_buf[h * 128:(h + 1) * 128, :], in_=ctxh[hp])\
                .then_inc(s_sct[hp].h, 16)
            tok[(ckey, "sct", h)] = s_sct[hp].inc(16)

    if upto == "C":
        return
    dkey = ("Dp", layer)
    base_peo = A["a_peo"].n
    base_dveo = A["a_dveo"].n
    base_stx = {k: s.n for k, s in s_stx.items()}

    @E.block.gpsimd
    def _(gp):
        gp.wait_ge(A["a_pept"].h, A["a_pept"].n)
        gp.wait_ge(A["a_peav"].h, A["a_peav"].n)
        gp.wait_ge(A["a_acte"].h, A["a_acte"].n)
        for sl in range(2):
            gp.wait_ge(s_sct[sl].h, s_sct[sl].n)
        gp.dma_start(out=aw.rearrange("p (c o) -> p c o", c=DC),
                     in_=w_all[3 * 2048:4 * 2048, :]
                     .rearrange("(c p) o -> p c o", p=128))\
            .then_inc(A["a_lw"].h, 16)
        tok[(dkey, "lw")] = A["a_lw"].inc(16)

    for ti in range(16):
        tt = ti % 4
        sl2 = ti % 2

        @E.block.gpsimd
        def _(gq, ti=ti, tt=tt):
            for sem, v in E.xslot_guard.get(tt, []):
                wge(gq, sem, v)
            if E.x_ready is not None:
                for sem, v in E.x_ready[ti]:
                    gq.wait_ge(sem.h, v)
            gq.dma_start(out=xbuf[:, tt * D:(tt + 1) * D],
                         in_=x_src[ti * 128:(ti + 1) * 128, :])\
                .then_inc(s_ldx[tt].h, 16)
            lntok[("ldx", dkey, ti)] = s_ldx[tt].inc(16)

        @E.block.sync
        def _(sy, ti=ti, sl2=sl2):
            if ti == 0:
                for sl in range(2):
                    sy.wait_ge(s_sct[sl].h, s_sct[sl].n)
            prev = tok.get((dkey, "peo", ti - 2))
            if prev:
                sy.wait_ge(A["a_peo"].h, prev)
            sy.dma_start(out=ctxt[sl2].rearrange("p (c t) -> p c t", c=DC),
                         in_=ctx_buf[:, ti * 128:(ti + 1) * 128]
                         .rearrange("(c p) t -> p c t", p=128))\
                .then_inc(s_lct[sl2].h, 16)
            tok[(dkey, "lct", ti)] = s_lct[sl2].inc(16)

        @E.block.tensor
        def _(te, ti=ti, sl2=sl2, base_dveo=base_dveo):
            te.wait_ge(s_lct[sl2].h, tok[(dkey, "lct", ti)])
            if ti == 0:
                te.wait_ge(A["a_lw"].h, tok[(dkey, "lw")])
            wge(te, A["a_dveo"], base_dveo + (ti - 2) * 2 + 2)
            for dc in range(DC):
                for nn in range(4):
                    inst = te.matmul(
                        pD[ti % 2][:, nn * 512:(nn + 1) * 512],
                        ctxt[sl2][:, dc * 128:(dc + 1) * 128],
                        aw[:, dc * 2048 + nn * 512:dc * 2048 + (nn + 1) * 512],
                        start=(dc == 0), stop=(dc == DC - 1))
                    if dc == DC - 1 and nn == 3:
                        inst.then_inc(A["a_peo"].h, 1)
            tok[(dkey, "peo", ti)] = A["a_peo"].inc()

        @E.block.vector
        def _(ve, ti=ti, tt=tt, sl2=sl2):
            ve.wait_ge(A["a_peo"].h, tok[(dkey, "peo", ti)])
            ve.wait_ge(s_ldx[tt].h, lntok[("ldx", dkey, ti)])
            if ti == 0:
                for sl in range(2):
                    ve.wait_ge(s_stk[sl].h, s_stk[sl].n)
                ve.wait_ge(A["a_pep"].h, A["a_pep"].n)
            for oh in range(2):
                sl = tt % 2
                bs = base_stx[(sl, oh)]
                wge(ve, s_stx[(sl, oh)], bs + (ti // 2) * 16)
                xa = xbuf[:, tt * D + oh * 1024:tt * D + (oh + 1) * 1024]
                oa = xout[:, sl * D + oh * 1024:sl * D + (oh + 1) * 1024]
                ve.tensor_tensor(oa, pD[ti % 2][:, oh * 1024:(oh + 1) * 1024],
                                 xa, OP.add).then_inc(A["a_dveo"].h, 1)
                tok[(dkey, "dveo", ti, oh)] = A["a_dveo"].inc()
            E.xslot_guard[tt] = [(A["a_dveo"], tok[(dkey, "dveo", ti, 1)])]

        @E.block.gpsimd
        def _(gp, ti=ti, tt=tt):
            for oh in range(2):
                sl = tt % 2
                gp.wait_ge(A["a_dveo"].h, tok[(dkey, "dveo", ti, oh)])
                gp.dma_start(
                    out=x_dst[ti * 128:(ti + 1) * 128, oh * 1024:(oh + 1) * 1024],
                    in_=xout[:, sl * D + oh * 1024:sl * D + (oh + 1) * 1024])\
                    .then_inc(s_stx[(sl, oh)].h, 16)
                tok[(dkey, "stx", ti, oh)] = s_stx[(sl, oh)].inc(16)

    E.x_ready = {ti: [(s_stx[(ti % 4 % 2, 0)], tok[(dkey, "stx", ti, 0)]),
                      (s_stx[(ti % 4 % 2, 1)], tok[(dkey, "stx", ti, 1)])]
                 for ti in range(16)}
    E.attn_exit_guard = [
        (A["a_peo"], A["a_peo"].n), (A["a_dveo"], A["a_dveo"].n),
        (A["a_acte"], A["a_acte"].n), (A["a_gpm"], A["a_gpm"].n),
        (A["a_peav"], A["a_peav"].n), (A["a_dvecs"], A["a_dvecs"].n),
        (s_stk[0], s_stk[0].n), (s_stk[1], s_stk[1].n),
        (s_stv[0], s_stv[0].n), (s_stv[1], s_stv[1].n),
        (s_sct[0], s_sct[0].n), (s_sct[1], s_sct[1].n),
        (s_lct[0], s_lct[0].n), (s_lct[1], s_lct[1].n)]


"""Fused megakernel for mk_macaron_d2: one raw-Bass program per core, one
run_bass_kernel_spmd launch for the whole forward (6 macaron sublayers).

Sharding: core = 2*b + p (batch b of 4, parity p); each core owns the
interleaved 256-token units of its sequence:
  local token lt -> global position 512*(lt//256) + 256*p + (lt%256).
Weights are sharded 1/8 per core and AllGather'ed on device; activations
stream through per-core DRAM between sublayers inside the single program.
"""
import os
import sys
from contextlib import ExitStack

import numpy as np
import ml_dtypes
import concourse.bass as bass
import concourse.mybir as mybir
from concourse.bass_utils import run_bass_kernel_spmd


T, D, I_FFN = 2048, 2048, 8192



def _l0_attn_wT(w_qkv, w_out):
    rows_q, rows_k, rows_v = [], [], []
    for h in range(16):
        m, j = h // 4, h % 4
        base = 1536 * m + 128 * j
        rows_q.extend(range(base, base + 128))
        rows_v.extend(range(base + 512, base + 640))
        rows_k.extend(range(base + 1024, base + 1152))
    wq = np.ascontiguousarray(w_qkv[rows_q].T)
    wk = np.ascontiguousarray(w_qkv[rows_k].T)
    wv = np.ascontiguousarray(w_qkv[rows_v].T)
    wo = np.ascontiguousarray(w_out.T)
    return np.concatenate([wq, wk, wv, wo], axis=0).astype(np.float32)


def _l1_attn_wT(wq, wk, wv, wo):
    return np.concatenate([np.ascontiguousarray(w.T) for w in (wq, wk, wv, wo)],
                          axis=0).astype(np.float32)


_PERM = (4 * np.arange(I_FFN // 4)[None, :] + np.arange(4)[:, None]).reshape(-1)


def _host_prep(inputs):
    h = {}
    h["wa"] = np.concatenate([
        _l0_attn_wT(np.asarray(inputs["L0_attn_w_qkv"]),
                    np.asarray(inputs["L0_attn_w_out"])),
        _l1_attn_wT(np.asarray(inputs["L1_attn_w_q"]),
                    np.asarray(inputs["L1_attn_w_k"]),
                    np.asarray(inputs["L1_attn_w_v"]),
                    np.asarray(inputs["L1_attn_w_o"]))], axis=0)
    g0 = np.asarray(inputs["L0_ffn_gate_w"]).astype(np.uint8)
    u0 = np.asarray(inputs["L0_ffn_up_w"]).astype(np.uint8)
    d0 = np.asarray(inputs["L0_ffn_down_w"]).astype(np.uint8)
    h["wq8"] = np.concatenate([np.ascontiguousarray(g0.T),
                               np.ascontiguousarray(u0.T)], axis=0)
    h["wq8d"] = np.ascontiguousarray(d0.T)
    wg1 = np.asarray(inputs["L1_ffn_w_gate"])[:, _PERM]
    wu1 = np.asarray(inputs["L1_ffn_w_up"])[:, _PERM]
    wd1 = np.asarray(inputs["L1_ffn_w_down"])[_PERM, :]
    h["wf1"] = np.concatenate([wg1, wu1], axis=0).astype(np.float32)
    h["wf1d"] = np.ascontiguousarray(wd1).astype(np.float32)
    h["gsc"] = np.ascontiguousarray(
        np.asarray(inputs["L0_ffn_gate_scale"]).reshape(64, 128).T).astype(np.float32)
    h["usc"] = np.ascontiguousarray(
        np.asarray(inputs["L0_ffn_up_scale"]).reshape(64, 128).T).astype(np.float32)
    h["dsc"] = np.ascontiguousarray(np.broadcast_to(
        np.asarray(inputs["L0_ffn_down_scale"]), (128, D))).astype(np.float32)
    iv0 = np.zeros(64, np.float32)
    iv0[:32] = 1.0 / 10000.0 ** (np.arange(0, 64, 2) / 64)
    iv1 = (1.0 / 10000.0 ** (np.arange(0, 128, 2) / 128)).astype(np.float32)
    h["invf0"] = np.broadcast_to(iv0, (128, 64)).copy()
    h["invf1"] = np.broadcast_to(iv1, (128, 64)).copy()
    return h


def _shard_x(x, b, p):
    out = np.empty((T, D), np.float32)
    for u in range(8):
        out[u * 256:(u + 1) * 256] = x[b, 512 * u + 256 * p:512 * u + 256 * p + 256]
    return np.ascontiguousarray(out)


def _unshard(res):
    y = np.empty((4, 4096, D), np.float32)
    for c in range(8):
        b, p = c // 2, c % 2
        o = res[c]
        for u in range(8):
            y[b, 512 * u + 256 * p:512 * u + 256 * p + 256] = \
                o[u * 256:(u + 1) * 256]
    return y



def build_program(ncores=8):
    nc = bass.Bass("TRN2", target_bir_lowering=False, debug=False)
    WORLD = [list(range(ncores))]
    RG = [[i, i + 1] for i in range(0, ncores, 2)]
    ap = {}
    ein = {
        "xin": ([T, D], F32),
        "wa_sh": ([16384 // ncores, 2048], F32),
        "wq8_sh": ([4096 // ncores, 8192], U8),
        "wq8d_sh": ([8192 // ncores, 2048], U8),
        "wf1_sh": ([4096 // ncores, 8192], F32),
        "wf1d_sh": ([8192 // ncores, 2048], F32),
        "gsc": ([128, 64], F32),
        "usc": ([128, 64], F32),
        "dsc": ([128, D], F32),
        "pofs_i": ([128, 1], F32),
        "qpos_i": ([128, 16], F32),
        "invf0_i": ([128, 64], F32),
        "invf1_i": ([128, 64], F32),
    }
    for n, (shape, dt) in ein.items():
        ap[n] = nc.dram_tensor(n, shape, dt, kind="ExternalInput").ap()
    ap["xout"] = nc.dram_tensor("xout", [T, D], F32, kind="ExternalOutput").ap()
    intr = {
        "wa_b": ([16384 // ncores, 2048], F32), "wa": ([16384, 2048], F32),
        "wq8_b": ([4096 // ncores, 8192], U8), "wq8": ([4096, 8192], U8),
        "wq8d_b": ([8192 // ncores, 2048], U8), "wq8d": ([8192, 2048], U8),
        "wf1_b": ([4096 // ncores, 8192], F32), "wf1": ([4096, 8192], F32),
        "wf1d_b": ([8192 // ncores, 2048], F32), "wf1d": ([8192, 2048], F32),
        "l0gu": ([4096, 8192], BF16), "l0d": ([8192, 2048], BF16),
        "l1gu": ([4096, 8192], BF16), "l1d": ([8192, 2048], BF16),
        "xs1": ([T, D], F32), "xs2": ([T, D], F32),
        "q_fm": ([2048, 2048], BF16), "kv_in": ([4096, 2048], BF16),
        "kv_out": ([8192, 2048], BF16), "ctx_buf": ([2048, 2048], BF16),
    }
    for n, (shape, dt) in intr.items():
        ap[n] = nc.dram_tensor(n, shape, dt).ap()

    with ExitStack() as ctx:
        with nc.Block() as block:
            E = Env(nc, block, ctx)
            emit_const_prologue(E)
            pofs = E.sbuf("p_pofs", [128, 1], F32)
            qpos_c = E.sbuf("p_qpos", [128, 16], F32)
            invf = [E.sbuf("p_invf0", [128, 64], F32),
                    E.sbuf("p_invf1", [128, 64], F32)]
            gs = E.sbuf("l0_gs", [128, 64], F32)
            us = E.sbuf("l0_us", [128, 64], F32)
            sbc = E.sbuf("l0_sbc", [128, D], F32)
            ldp = E.sem("c_ldp")
            sbcs = E.sem("c_sbc")

            @E.block.sync
            def _(sy):
                for (dst, src) in ((pofs, ap["pofs_i"]), (qpos_c, ap["qpos_i"]),
                                   (invf[0], ap["invf0_i"]),
                                   (invf[1], ap["invf1_i"]),
                                   (gs, ap["gsc"]), (us, ap["usc"]),
                                   (sbc, ap["dsc"])):
                    sy.dma_start(out=dst[:], in_=src).then_inc(ldp.h, 16)
                    ldp.inc(16)

            @E.block.vector
            def _(ve):
                ve.wait_ge(ldp.h, ldp.n)
                ve.tensor_scalar(sbc[:], sbc[:], 0.5, None, OP.mult)\
                    .then_inc(sbcs.h, 1)
                sbcs.inc()

            cvt = {n: E.sem(f"c_cv_{n}") for n in ["q8", "q8d", "f1", "f1d"]}
            gat = {n: E.sem(f"c_ag_{n}") for n in
                   ["wa", "wq8", "wq8d", "wf1", "wf1d"]}
            bnc = {n: E.sem(f"c_bnc_{n}") for n in
                   ["wq8", "wq8d", "wa", "wf1", "wf1d"]}

            @E.block.gpsimd
            def _(gp):
                for (sh, b, n) in (("wq8_sh", "wq8_b", "wq8"),
                                   ("wq8d_sh", "wq8d_b", "wq8d"),
                                   ("wa_sh", "wa_b", "wa"),
                                   ("wf1_sh", "wf1_b", "wf1"),
                                   ("wf1d_sh", "wf1d_b", "wf1d")):
                    gp.dma_start(out=ap[b], in_=ap[sh]).then_inc(bnc[n].h, 16)
                    bnc[n].inc(16)
                gp.wait_ge(bnc["wq8"].h, 16)
                gp.wait_ge(bnc["wq8d"].h, 16)
                gp.collective_compute(
                    "AllGather", OP.bypass, replica_groups=WORLD,
                    ins=[ap["wq8_b"].opt()], outs=[ap["wq8"].opt()])\
                    .then_inc(gat["wq8"].h, 1)
                gat["wq8"].inc()
                gp.collective_compute(
                    "AllGather", OP.bypass, replica_groups=WORLD,
                    ins=[ap["wq8d_b"].opt()], outs=[ap["wq8d"].opt()])\
                    .then_inc(gat["wq8d"].h, 1)
                gat["wq8d"].inc()
                gp.wait_ge(bnc["wa"].h, 16)
                gp.wait_ge(bnc["wf1"].h, 16)
                gp.wait_ge(bnc["wf1d"].h, 16)
                gp.collective_compute(
                    "AllGather", OP.bypass, replica_groups=WORLD,
                    ins=[ap["wa_b"].opt()], outs=[ap["wa"].opt()])\
                    .then_inc(gat["wa"].h, 1)
                gat["wa"].inc()
                gp.collective_compute(
                    "AllGather", OP.bypass, replica_groups=WORLD,
                    ins=[ap["wf1_b"].opt()], outs=[ap["wf1"].opt()])\
                    .then_inc(gat["wf1"].h, 1)
                gat["wf1"].inc()
                gp.collective_compute(
                    "AllGather", OP.bypass, replica_groups=WORLD,
                    ins=[ap["wf1d_b"].opt()], outs=[ap["wf1d"].opt()])\
                    .then_inc(gat["wf1d"].h, 1)
                gat["wf1d"].inc()
                gp.wait_ge(gat["wq8"].h, 1)
                for i in range(8):
                    gp.dma_start(out=ap["l0gu"][i * 512:(i + 1) * 512, :],
                                 in_=ap["wq8"][i * 512:(i + 1) * 512, :],
                                 max_dma_last_dim=2048).then_inc(cvt["q8"].h, 16)
                    cvt["q8"].inc(16)
                gp.wait_ge(gat["wq8d"].h, 1)
                for i in range(8):
                    gp.dma_start(out=ap["l0d"][i * 1024:(i + 1) * 1024, :],
                                 in_=ap["wq8d"][i * 1024:(i + 1) * 1024, :],
                                 max_dma_last_dim=2048).then_inc(cvt["q8d"].h, 16)
                    cvt["q8d"].inc(16)
                gp.wait_ge(gat["wf1"].h, 1)
                for i in range(8):
                    gp.dma_start(out=ap["l1gu"][i * 512:(i + 1) * 512, :],
                                 in_=ap["wf1"][i * 512:(i + 1) * 512, :],
                                 max_dma_last_dim=8192).then_inc(cvt["f1"].h, 16)
                    cvt["f1"].inc(16)
                gp.wait_ge(gat["wf1d"].h, 1)
                for i in range(8):
                    gp.dma_start(out=ap["l1d"][i * 1024:(i + 1) * 1024, :],
                                 in_=ap["wf1d"][i * 1024:(i + 1) * 1024, :],
                                 max_dma_last_dim=8192).then_inc(cvt["f1d"].h, 16)
                    cvt["f1d"].inc(16)

            def ffn_exit():
                names = ["f_pednc", "f_pegu", "f_petr", "f_pesh", "f_actd",
                         "f_actc", "f_acta", "f_actb", "f_dvea", "f_dveb",
                         "f_dvec", "f_dved", "f_dvesh", "f_gpa",
                         "f_stx00", "f_stx01", "f_stx10", "f_stx11"]
                return [(E.sems[n], E.sems[n].n) for n in names if n in E.sems]

            l0ffn = dict(wg=ap["l0gu"][0:2048, :], wu=ap["l0gu"][2048:4096, :],
                         wd=ap["l0d"], gs=gs, us=us, sbc=sbc)
            l1ffn = dict(wg=ap["l1gu"][0:2048, :], wu=ap["l1gu"][2048:4096, :],
                         wd=ap["l1d"])
            guard0 = ([(cvt["q8"], cvt["q8"].n), (cvt["q8d"], cvt["q8d"].n),
                       (sbcs, sbcs.n)])
            emit_ffn(E, "l0", ap["xin"], ap["xs1"], entry_guard=guard0,
                        **l0ffn)
            emit_attn(E, 0, ap["xs1"], ap["xs2"], ap["wa"][0:8192, :],
                          ap["q_fm"], ap["kv_in"], ap["kv_out"], ap["ctx_buf"],
                          pofs, qpos_c, invf[0],
                          ffn_exit() + [(gat["wa"], gat["wa"].n)], RG)
            emit_ffn(E, "l0", ap["xs2"], ap["xs1"],
                        entry_guard=E.attn_exit_guard, **l0ffn)
            guard1 = ffn_exit() + [(cvt["f1"], cvt["f1"].n),
                                   (cvt["f1d"], cvt["f1d"].n)]
            emit_ffn(E, "l1", ap["xs1"], ap["xs2"], entry_guard=guard1,
                        **l1ffn)
            emit_attn(E, 1, ap["xs2"], ap["xs1"], ap["wa"][8192:16384, :],
                          ap["q_fm"], ap["kv_in"], ap["kv_out"], ap["ctx_buf"],
                          pofs, qpos_c, invf[1], ffn_exit(), RG)
            emit_ffn(E, "l1", ap["xs1"], ap["xout"],
                        entry_guard=E.attn_exit_guard, **l1ffn)
            emit_final_wait(E)
    return nc


_NC_CACHE = [None]


def make_in_maps(inputs, ncores=8):
    h = _host_prep(inputs)
    x = np.asarray(inputs["L0_attn_hidden_states"], dtype=np.float32)
    in_maps = []
    for c in range(ncores):
        b, p = c // 2, c % 2
        pos = np.array([512 * (lt // 256) + 256 * p + (lt % 256)
                        for lt in range(T)], np.float32)
        r1, r2, r3 = 16384 // ncores, 4096 // ncores, 8192 // ncores
        in_maps.append({
            "xin": _shard_x(x, b, p),
            "wa_sh": np.ascontiguousarray(h["wa"][c * r1:(c + 1) * r1]),
            "wq8_sh": np.ascontiguousarray(h["wq8"][c * r2:(c + 1) * r2]),
            "wq8d_sh": np.ascontiguousarray(h["wq8d"][c * r3:(c + 1) * r3]),
            "wf1_sh": np.ascontiguousarray(h["wf1"][c * r2:(c + 1) * r2]),
            "wf1d_sh": np.ascontiguousarray(h["wf1d"][c * r3:(c + 1) * r3]),
            "gsc": h["gsc"], "usc": h["usc"], "dsc": h["dsc"],
            "pofs_i": np.full((128, 1), float(p), np.float32),
            "qpos_i": np.ascontiguousarray(pos.reshape(16, 128).T),
            "invf0_i": h["invf0"], "invf1_i": h["invf1"],
        })
    return in_maps


def kernel(**inputs):
    if _NC_CACHE[0] is None:
        _NC_CACHE[0] = build_program()
    nc = _NC_CACHE[0]
    in_maps = make_in_maps(inputs, 8)
    res = run_bass_kernel_spmd(nc, in_maps, core_ids=list(range(8)))
    return _unshard([res.results[c]["xout"] for c in range(8)])
